# version 14
# baseline (speedup 1.0000x reference)
import math
import contextlib
import numpy as np

import concourse.bass as bass
import concourse.mybir as mybir
from concourse.bass_utils import run_bass_kernel_spmd

F32 = mybir.dt.float32
BF16 = mybir.dt.bfloat16
AF = mybir.ActivationFunctionType
ALU = mybir.AluOpType
AX = mybir.AxisListType

B, T, E, H = 32, 2048, 1024, 4
Dk, Dv = 256, 512
INNER = 4096
NCORES = 8
SPC = B // NCORES
CH = 256
NCH = T // CH
TT = 512
NTT = T // TT

C1 = float(1.5 * 2**23)
C2 = float(1.5 * 2**52)
C3 = float(1.5 * 2**32)
GAMMAS = [1.0 - 2.0 ** (-5.0 - h) for h in range(H)]


def dram_ap(td, shape, idx, dims):
    strides = [1] * len(shape)
    for i in range(len(shape) - 2, -1, -1):
        strides[i] = strides[i + 1] * shape[i + 1]
    off = sum(i * s for i, s in zip(idx, strides))
    ap = [[strides[d], c] for d, c in dims]
    return bass.AP(td, off, ap)


class Prog:

    ENGINES = ("tensor", "vector", "scalar", "gpsimd", "sync")

    def __init__(self, nc, stack):
        self.nc = nc
        self.stack = stack
        self.q = {e: [] for e in self.ENGINES}
        self.sem = {}
        self.cnt = {}
        for e in self.ENGINES:
            self.mksem(stack, "ch_" + e)
        self.lastw = {}
        self.readers = {}
        self.wm = {e: {} for e in self.ENGINES}
        self.ninstr = 0

    def mksem(self, stack, name):
        self.sem[name] = stack.enter_context(self.nc.semaphore(name))
        self.cnt[name] = 0

    def dsem(self, buf):
        name = "D_" + buf
        if name not in self.sem:
            self.mksem(self.stack, name)
        return name

    def _wait(self, eng, sname, val):
        if val <= 0 or self.wm[eng].get(sname, 0) >= val:
            return
        h = self.sem[sname]
        self.q[eng].append(lambda e, h=h, v=val: e.wait_ge(h, v))
        self.wm[eng][sname] = val
        self.ninstr += 1

    def _dep_waits(self, eng, reads, writes):
        need = {}
        for r in reads:
            w = self.lastw.get(r)
            if w:
                need[w[0]] = max(need.get(w[0], 0), w[1])
        for w in writes:
            lw = self.lastw.get(w)
            if lw:
                need[lw[0]] = max(need.get(lw[0], 0), lw[1])
            for s2, v2 in self.readers.get(w, {}).items():
                need[s2] = max(need.get(s2, 0), v2)
        for s2, v2 in need.items():
            if s2.startswith("D_"):
                v2 = self.cnt[s2]
            self._wait(eng, s2, v2)

    def _register(self, sname, val, reads, writes):
        for w in writes:
            self.lastw[w] = (sname, val)
            self.readers[w] = {}
        for r in reads:
            d = self.readers.setdefault(r, {})
            d[sname] = max(d.get(sname, 0), val)

    def emit(self, eng, fn, reads=(), writes=(), inc=1, sem=None):
        sname = sem or ("ch_" + eng)
        self._dep_waits(eng, reads, writes)
        h = self.sem[sname]
        self.cnt[sname] += inc
        val = self.cnt[sname]

        def run(e, fn=fn, h=h, inc=inc):
            fn(e).then_inc(h, inc)
        self.q[eng].append(run)
        self.ninstr += 1
        self._register(sname, val, reads, writes)
        return sname, val

    def group(self, eng, fns_reads, writes, inc=1, sem=None):
        sname = sem or ("ch_" + eng)
        all_reads = []
        n = len(fns_reads)
        for i, (gfn, greads) in enumerate(fns_reads):
            self._dep_waits(eng, greads, writes if i == 0 else ())
            all_reads += list(greads)
            if i < n - 1:
                self.q[eng].append(lambda e, f=gfn: f(e))
                self.ninstr += 1
        h = self.sem[sname]
        self.cnt[sname] += inc
        val = self.cnt[sname]
        gfn = fns_reads[-1][0]

        def run(e, fn=gfn, h=h, inc=inc):
            fn(e).then_inc(h, inc)
        self.q[eng].append(run)
        self.ninstr += 1
        self._register(sname, val, all_reads, writes)
        return sname, val

    def dma(self, eng, fn, reads=(), writes=(), sem=None, buf=None):
        if buf is None:
            buf = (list(writes) + list(reads) + ["misc"])[0]
        return self.emit(eng, fn, reads, writes, inc=16, sem=self.dsem(buf))

    def barrier(self):
        totals = {s: v for s, v in self.cnt.items() if v > 0}
        for e in self.ENGINES:
            for s, v in totals.items():
                self._wait(e, s, v)
        self.lastw = {}
        self.readers = {}

    def finalize(self, block):
        for ename in self.ENGINES:
            qq = self.q[ename]
            sec = getattr(block, ename)

            def make(qq):
                def _(eng):
                    for f in qq:
                        f(eng)
                return _
            sec(make(qq))


def emit_round(p, out_ap, tmp_ap, in_ap, okey, tkey, ikey, eng="vector"):
    M = p.mask01
    p.emit(eng, lambda e: e.tensor_scalar(out_ap, in_ap, C1, C1, ALU.add, ALU.subtract),
           reads=(ikey,), writes=(okey,))
    p.emit(eng, lambda e: e.tensor_scalar(tmp_ap, in_ap, C2, C2, ALU.add, ALU.subtract),
           reads=(ikey,), writes=(tkey,))
    p.emit(eng, lambda e: e.tensor_sub(tmp_ap, out_ap, tmp_ap),
           reads=(okey, tkey), writes=(tkey,))
    p.emit(eng, lambda e: e.tensor_scalar_mul(tmp_ap, tmp_ap, M[:, :1]),
           reads=(tkey, "mask01"), writes=(tkey,))
    p.emit(eng, lambda e: e.tensor_sub(out_ap, out_ap, tmp_ap),
           reads=(okey, tkey), writes=(okey,))


def build_program(phases=("A1", "A2", "B", "C1", "C2"), debug=False):
    nc = bass.Bass("TRN2", target_bir_lowering=False)
    stack = contextlib.ExitStack()

    x_d = nc.dram_tensor("x", [SPC, T, E], F32, kind="ExternalInput")
    wq_d = nc.dram_tensor("wq", [E, E], F32, kind="ExternalInput")
    wk_d = nc.dram_tensor("wk", [E, E], F32, kind="ExternalInput")
    wv_d = nc.dram_tensor("wv", [E, 2 * E], F32, kind="ExternalInput")
    wg_d = nc.dram_tensor("wg", [E, 2 * E], F32, kind="ExternalInput")
    wo_d = nc.dram_tensor("wo", [2 * E, E], F32, kind="ExternalInput")
    gn_d = nc.dram_tensor("gn", [Dv], F32, kind="ExternalInput")
    wgate_d = nc.dram_tensor("wgate", [E, INNER], F32, kind="ExternalInput")
    wup_d = nc.dram_tensor("wup", [E, INNER], F32, kind="ExternalInput")
    wdown_d = nc.dram_tensor("wdown", [INNER, E], F32, kind="ExternalInput")
    out_d = nc.dram_tensor("out", [SPC, T, E], F32, kind="ExternalOutput")

    skind = "ExternalOutput" if debug else "Internal"
    SH_XN = [SPC, E // 128, 128, T]
    xnfm_d = nc.dram_tensor("xnfm", SH_XN, F32, kind=skind)
    qhat_d = nc.dram_tensor("qhat", SH_XN, F32, kind=skind)
    khat_d = nc.dram_tensor("khat", SH_XN, F32, kind=skind)
    on_d = nc.dram_tensor("on", [SPC, T, 2 * E], F32, kind=skind)
    xp_d = nc.dram_tensor("xp", [SPC, T, E], F32, kind=skind)
    SH_HQ = [SPC, INNER // 128, 128, T]
    hq_d = nc.dram_tensor("hq", SH_HQ, BF16, kind=skind)
    hsc_d = nc.dram_tensor("hsc", [SPC, 128, T // 128], F32, kind=skind)

    p = Prog(nc, stack)

    ident = stack.enter_context(nc.sbuf_tensor("ident", [128, 128], F32))
    identb = stack.enter_context(nc.sbuf_tensor("identb", [128, 128], BF16))
    epst = stack.enter_context(nc.sbuf_tensor("epst", [128, 1], F32))
    zeros1 = stack.enter_context(nc.sbuf_tensor("zeros1", [128, 1], F32))
    mask01 = stack.enter_context(nc.sbuf_tensor("mask01", [128, 1], F32))
    p_mask_ref = mask01

    pb = [stack.enter_context(nc.psum_tensor(f"pb{i}", [128, 512], F32))
          for i in range(7)]
    PK = ["p%d" % i for i in range(7)]
    pbb = stack.enter_context(nc.psum_tensor("pbb", [128, 1024], BF16))

    with nc.sbuf_tensor("onesb", [128, 128], BF16) as onesb, \
         nc.sbuf_tensor("tD0", [128, 128], F32) as tD:

        g = "gpsimd"
        p.emit(g, lambda e: e.memset(zeros1[:, :], 0.0), writes=("zeros1",))
        p.emit(g, lambda e: e.memset(ident[:, :], 0.0), writes=("ident",))
        p.emit(g, lambda e: e.memset(identb[:, :], 0.0), writes=("identb",))
        p.emit(g, lambda e: e.memset(tD[:, :128], 1.0), writes=("tD",))
        p.emit(g, lambda e: e.memset(onesb[:, :], 1.0), writes=("onesb",))
        p.emit(g, lambda e: e.affine_select(
            ident[:, :], tD[:, :128], [[-1, 128]], ALU.is_equal, 0.0,
            base=0, channel_multiplier=1),
            reads=("tD",), writes=("ident",))
        p.emit(g, lambda e: e.affine_select(
            identb[:, :], onesb[:, :], [[-1, 128]], ALU.is_equal, 0.0,
            base=0, channel_multiplier=1),
            reads=("onesb",), writes=("identb",))
        p.emit("vector", lambda e: e.tensor_scalar_add(epst[:, :], zeros1[:, :], 1e-5),
               reads=("zeros1",), writes=("epst",))
        p.emit("vector", lambda e: e.tensor_scalar_add(mask01[:, :], zeros1[:, :],
                                                       8.25),
               reads=("zeros1",), writes=("mask01",))
        p.emit("vector", lambda e: e.tensor_scalar(mask01[:, :], mask01[:, :],
                                                   C2, C2, ALU.add, ALU.subtract),
               reads=("mask01",), writes=("mask01",))
        p.emit("vector", lambda e: e.tensor_scalar_mul(mask01[:, :], mask01[:, :],
                                                       0.125),
               reads=("mask01",), writes=("mask01",))
        p.mask01 = mask01

        p.barrier()

    if "A1" in phases:
        a1s = contextlib.ExitStack()
        c16 = a1s.enter_context(nc.sbuf_tensor("c16", [128, T], F32, side="right"))
        s16 = a1s.enter_context(nc.sbuf_tensor("s16", [128, T], F32, side="right"))
        with nc.sbuf_tensor("tA", [128, T], F32) as tA, \
               nc.sbuf_tensor("tB", [128, T], F32) as tB, \
               nc.sbuf_tensor("tC", [128, T], F32) as tC, \
               nc.sbuf_tensor("tD", [128, T], F32) as tD, \
               nc.sbuf_tensor("pidx", [128, 1], F32) as pidx, \
               nc.sbuf_tensor("invf", [128, 1], F32) as invf:
            g = "gpsimd"
            p.emit(g, lambda e: e.iota(pidx[:, :], [[0, 1]], base=0, channel_multiplier=1,
                                       allow_small_or_imprecise_dtypes=True),
                   writes=("pidx",))
            p.emit("vector", lambda e: e.tensor_scalar_add(invf[:, :], zeros1[:, :], 1.0),
                   reads=("zeros1",), writes=("invf",))
            p.emit("vector", lambda e: e.tensor_copy(tB[:, :1], pidx[:, :]),
                   reads=("pidx",), writes=("tB",))
            for j in range(6, -1, -1):
                rj = 10000.0 ** (-(2.0 ** j) / 128.0)
                bit = float(2 ** j)
                p.emit("vector", lambda e, bit=bit: e.tensor_single_scalar(
                    tC[:, :1], tB[:, :1], bit, ALU.is_ge), reads=("tB",), writes=("tC",))
                p.emit("vector", lambda e, bit=bit: e.tensor_scalar(
                    tD[:, :1], tC[:, :1], bit, None, ALU.mult), reads=("tC",), writes=("tD",))
                p.emit("vector", lambda e: e.tensor_sub(tB[:, :1], tB[:, :1], tD[:, :1]),
                       reads=("tB", "tD"), writes=("tB",))
                p.emit("vector", lambda e, rj=rj: e.tensor_scalar(
                    tC[:, :1], tC[:, :1], rj - 1.0, 1.0, ALU.mult, ALU.add),
                    reads=("tC",), writes=("tC",))
                p.emit("vector", lambda e: e.tensor_mul(invf[:, :], invf[:, :], tC[:, :1]),
                       reads=("invf", "tC"), writes=("invf",))
            p.emit(g, lambda e: e.iota(tA[:, :], [[1, T]], base=0, channel_multiplier=0,
                                       allow_small_or_imprecise_dtypes=True),
                   writes=("tA",))
            p.emit("vector", lambda e: e.tensor_scalar_mul(tA[:, :], tA[:, :], invf[:, :1]),
                   reads=("tA", "invf"), writes=("tA",))
            p.emit("vector", lambda e: e.tensor_scalar(
                tC[:, :], tA[:, :], 1.0 / (2 * math.pi), 0.5, ALU.mult, ALU.add),
                reads=("tA",), writes=("tC",))
            p.emit("vector", lambda e: e.memset(tB[:, :], 0.0), writes=("tB",))
            for j in range(8, -1, -1):
                v = float(2 ** j)
                p.emit("vector", lambda e, v=v: e.tensor_single_scalar(
                    tD[:, :], tC[:, :], v, ALU.is_ge), reads=("tC",), writes=("tD",))
                p.emit("vector", lambda e, v=v: e.tensor_scalar_mul(tD[:, :], tD[:, :], v),
                       reads=("tD",), writes=("tD",))
                p.emit("vector", lambda e: e.tensor_sub(tC[:, :], tC[:, :], tD[:, :]),
                       reads=("tC", "tD"), writes=("tC",))
                p.emit("vector", lambda e: e.tensor_add(tB[:, :], tB[:, :], tD[:, :]),
                       reads=("tB", "tD"), writes=("tB",))
            p.emit("vector", lambda e: e.tensor_scalar_mul(tB[:, :], tB[:, :],
                                                           2 * math.pi),
                   reads=("tB",), writes=("tB",))
            p.emit("vector", lambda e: e.tensor_sub(tA[:, :], tA[:, :], tB[:, :]),
                   reads=("tA", "tB"), writes=("tA",))
            p.emit("scalar", lambda e: e.activation(tC[:, :], tA[:, :], AF.Abs),
                   reads=("tA",), writes=("tC",))
            p.emit("vector", lambda e: e.tensor_scalar_add(tD[:, :1], zeros1[:, :],
                                                           math.pi / 2),
                   reads=("zeros1",), writes=("tD",))
            p.emit("scalar", lambda e: e.activation(c16[:, :], tC[:, :], AF.Sin,
                                                    scale=-1.0, bias=tD[:, :1]),
                   reads=("tC", "tD"), writes=("c16",))
            p.emit("vector", lambda e: e.tensor_scalar_mul(c16[:, :], c16[:, :],
                                                           1.0 / 16),
                   reads=("c16",), writes=("c16",))
            p.emit("vector", lambda e: e.tensor_scalar(
                tB[:, :], tC[:, :], -1.0, math.pi, ALU.mult, ALU.add),
                reads=("tC",), writes=("tB",))
            p.emit("vector", lambda e: e.tensor_tensor(tC[:, :], tC[:, :], tB[:, :],
                                                       ALU.min),
                   reads=("tC", "tB"), writes=("tC",))
            p.emit("scalar", lambda e: e.activation(s16[:, :], tC[:, :], AF.Sin),
                   reads=("tC",), writes=("s16",))
            p.emit("scalar", lambda e: e.activation(tB[:, :], tA[:, :], AF.Sign),
                   reads=("tA",), writes=("tB",))
            p.emit("vector", lambda e: e.tensor_mul(s16[:, :], s16[:, :], tB[:, :]),
                   reads=("s16", "tB"), writes=("s16",))
            p.emit("vector", lambda e: e.tensor_scalar_mul(s16[:, :], s16[:, :],
                                                           1.0 / 16),
                   reads=("s16",), writes=("s16",))

            p.barrier()
        with nc.sbuf_tensor("wq_sb", [128, 8 * E], F32) as wq_sb, \
             nc.sbuf_tensor("wk_sb", [128, 8 * E], F32) as wk_sb, \
             nc.sbuf_tensor("xb0", [128, 4 * E], F32) as xb0, \
             nc.sbuf_tensor("xb1", [128, 4 * E], F32) as xb1, \
             nc.sbuf_tensor("xnt_0", [128, 4 * E], F32) as xnt_0, \
             nc.sbuf_tensor("xnfm_0", [128, 4 * E], F32) as xnfm_0, \
             nc.sbuf_tensor("xnfm_1", [128, 4 * E], F32) as xnfm_1, \
             nc.sbuf_tensor("qh_sb", [128, 4 * E], F32) as qh_sb, \
             nc.sbuf_tensor("kh_sb", [128, 4 * E], F32) as kh_sb, \
             nc.sbuf_tensor("sc1", [128, E], F32) as sc1, \
             nc.sbuf_tensor("st_a1", [128, 8], F32) as st_a1, \
             nc.sbuf_tensor("tr1", [128, TT], F32) as tr1, \
             nc.sbuf_tensor("tr2", [128, TT], F32) as tr2:

            for kc in range(8):
                p.dma("sync", lambda e, kc=kc: e.dma_start(
                    out=wq_sb[:, kc * E:(kc + 1) * E],
                    in_=wq_d[kc * 128:(kc + 1) * 128, :]),
                    writes=("wq_sb",))
                p.dma("sync", lambda e, kc=kc: e.dma_start(
                    out=wk_sb[:, kc * E:(kc + 1) * E],
                    in_=wk_d[kc * 128:(kc + 1) * 128, :]),
                    writes=("wk_sb",))

            for s in range(SPC):
                for it in range(NTT):
                    t0 = it * TT
                    slot = (s * NTT + it) % 2
                    xb = xb0 if slot == 0 else xb1
                    xkey = "xb%d" % slot
                    xnt = xnt_0
                    xntk = "xnt0"
                    xnfm_sb = xnfm_0 if slot == 0 else xnfm_1
                    xnfk = "xnfm%d" % slot
                    for j in range(4):
                        p.dma("sync", lambda e, s=s, t0=t0, j=j, xb=xb: e.dma_start(
                            out=xb[:, j * E:(j + 1) * E],
                            in_=x_d[s, t0 + j * 128: t0 + (j + 1) * 128, :]),
                            writes=(xkey,))
                    for j in range(4):
                        xs = xb[:, j * E:(j + 1) * E]
                        xn = xnt[:, j * E:(j + 1) * E]
                        mcol = st_a1[:, j:j + 1]
                        qcol = st_a1[:, 4 + j:5 + j]
                        p.emit("vector", lambda e, xs=xs, mcol=mcol: e.reduce_sum(
                            mcol, xs, axis=AX.X), reads=(xkey,), writes=("st_a1",))
                        p.emit("scalar", lambda e, mcol=mcol: e.mul(
                            mcol, mcol, 1.0 / E), reads=("st_a1",), writes=("st_a1",))
                        p.emit("vector", lambda e, xs=xs, xn=xn, mcol=mcol:
                               e.tensor_scalar_sub(xn, xs, mcol),
                               reads=(xkey, "st_a1"), writes=(xntk,))
                        p.emit("scalar", lambda e, xn=xn, qcol=qcol: e.activation(
                            sc1[:, :], xn, AF.Square, accum_out=qcol),
                            reads=(xntk,), writes=("sc1", "st_a1"))
                        p.emit("scalar", lambda e, qcol=qcol: e.activation(
                            qcol, qcol, AF.Sqrt, scale=1.0 / E, bias=epst[:, :1]),
                            reads=("st_a1", "epst"), writes=("st_a1",))
                        p.emit("vector", lambda e, qcol=qcol: e.reciprocal(
                            qcol, qcol), reads=("st_a1",), writes=("st_a1",))
                        p.emit("vector", lambda e, xn=xn, qcol=qcol:
                               e.tensor_scalar_mul(xn, xn, qcol),
                               reads=(xntk, "st_a1"), writes=(xntk,))
                    for kc in range(8):
                        bk = 0 if kc % 2 == 0 else 5
                        fml = xnfm_sb[:, kc * TT:(kc + 1) * TT]
                        fns = []
                        for j in range(4):
                            src = xnt[:, j * E + kc * 128: j * E + (kc + 1) * 128]
                            dst = pb[bk][:, j * 128:(j + 1) * 128]
                            fns.append((lambda e, src=src, dst=dst: e.transpose(
                                dst, src, ident[:, :]), (xntk, "ident")))
                        p.group("tensor", fns, writes=(PK[bk],))
                        p.emit("scalar", lambda e, fml=fml, bk=bk: e.copy(
                            fml, pb[bk][:, :512]), reads=(PK[bk],), writes=(xnfk,))
                    p.dma("sync", lambda e, s=s, t0=t0, xnfm_sb=xnfm_sb:
                          e.dma_start(
                        out=dram_ap(xnfm_d, SH_XN, [s, 0, 0, t0],
                                    [(2, 128), (1, 8), (3, TT)]),
                        in_=bass.AP(xnfm_sb, 0, [[4 * E, 128], [TT, 8], [1, TT]])),
                        reads=(xnfk,))
                    qk_i = 0
                    for h in range(H):
                        for (wsb, wkey, osb, okey, isk) in (
                                (wq_sb, "wq_sb", qh_sb, "qh_sb", False),
                                (wk_sb, "wk_sb", kh_sb, "kh_sb", True)):
                            bko = 1 if qk_i % 2 == 0 else 3
                            qk_i += 1
                            for half in range(2):
                                dq = 2 * h + half
                                fns = []
                                for kc in range(8):
                                    lhs = wsb[:, kc * E + dq * 128:
                                              kc * E + (dq + 1) * 128]
                                    rhs = xnfm_sb[:, kc * TT:(kc + 1) * TT]
                                    dst = pb[bko + half][:, :TT]
                                    fns.append((lambda e, lhs=lhs, rhs=rhs, dst=dst,
                                                st=(kc == 0), sp=(kc == 7):
                                                e.matmul(dst, lhs, rhs,
                                                         start=st, stop=sp),
                                                (wkey, xnfk)))
                                p.group("tensor", fns, writes=(PK[bko + half],))
                            cs = c16[:, t0:t0 + TT]
                            sn = s16[:, t0:t0 + TT]
                            o1 = osb[:, (2 * h) * TT:(2 * h) * TT + TT]
                            o2 = osb[:, (2 * h + 1) * TT:(2 * h + 1) * TT + TT]
                            pA, pB = pb[bko], pb[bko + 1]
                            kA, kB = PK[bko], PK[bko + 1]
                            p.emit("vector", lambda e, cs=cs, pA=pA: e.tensor_tensor(
                                tr1[:, :], pA[:, :TT], cs, ALU.mult),
                                reads=(kA, "c16"), writes=("tr1",))
                            p.emit("vector", lambda e, sn=sn, pB=pB: e.tensor_tensor(
                                tr2[:, :], pB[:, :TT], sn, ALU.mult),
                                reads=(kB, "s16"), writes=("tr2",))
                            p.emit("vector", lambda e, o1=o1: e.tensor_sub(
                                o1, tr1[:, :], tr2[:, :]),
                                reads=("tr1", "tr2"), writes=(okey,))
                            p.emit("vector", lambda e, cs=cs, pB=pB: e.tensor_tensor(
                                tr1[:, :], pB[:, :TT], cs, ALU.mult),
                                reads=(kB, "c16"), writes=("tr1",))
                            p.emit("vector", lambda e, sn=sn, pA=pA: e.tensor_tensor(
                                tr2[:, :], pA[:, :TT], sn, ALU.mult),
                                reads=(kA, "s16"), writes=("tr2",))
                            p.emit("vector", lambda e, o2=o2: e.tensor_add(
                                o2, tr1[:, :], tr2[:, :]),
                                reads=("tr1", "tr2"), writes=(okey,))
                            if isk:
                                p.emit("vector", lambda e, o1=o1: e.tensor_scalar_mul(
                                    o1, o1, 16.0), reads=(okey,), writes=(okey,))
                                p.emit("vector", lambda e, o2=o2: e.tensor_scalar_mul(
                                    o2, o2, 16.0), reads=(okey,), writes=(okey,))
                    p.dma("sync", lambda e, s=s, t0=t0: e.dma_start(
                        out=dram_ap(qhat_d, SH_XN, [s, 0, 0, t0],
                                    [(2, 128), (1, 8), (3, TT)]),
                        in_=bass.AP(qh_sb, 0, [[4 * E, 128], [TT, 8], [1, TT]])),
                        reads=("qh_sb",))
                    p.dma("sync", lambda e, s=s, t0=t0: e.dma_start(
                        out=dram_ap(khat_d, SH_XN, [s, 0, 0, t0],
                                    [(2, 128), (1, 8), (3, TT)]),
                        in_=bass.AP(kh_sb, 0, [[4 * E, 128], [TT, 8], [1, TT]])),
                        reads=("kh_sb",))
            p.barrier()
        a1s.close()

    if "A2" in phases:
        a2s = contextlib.ExitStack()
        gnb = a2s.enter_context(nc.sbuf_tensor("gnb", [128, 2 * E], F32, side="right"))
        dmask = a2s.enter_context(nc.sbuf_tensor("dmask", [128, H * 2 * CH], F32, side="right"))
        gam_q = a2s.enter_context(nc.sbuf_tensor("gam_q", [128, H * CH], F32, side="right"))
        gam_k = a2s.enter_context(nc.sbuf_tensor("gam_k", [128, H * 2], F32, side="right"))
        with nc.sbuf_tensor("tD2", [128, CH], F32) as tDm:
            g = "gpsimd"
            p.dma("sync", lambda e: e.dma_start(
                out=bass.AP(gnb, 0, [[2 * E, 128], [Dv, H], [1, Dv]]),
                in_=bass.AP(gn_d, 0, [[0, 128], [0, H], [1, Dv]])),
                writes=("gnb",))
            for h in range(H):
                lg = math.log(GAMMAS[h])
                for sub in range(2):
                    sl = dmask[:, (h * 2 + sub) * CH:(h * 2 + sub + 1) * CH]
                    p.emit(g, lambda e, sl=sl, sub=sub: e.iota(
                        sl, [[1, CH]], base=-128 * sub, channel_multiplier=-1,
                        allow_small_or_imprecise_dtypes=True), writes=("dmask",))
                    p.emit("vector", lambda e, sl=sl: e.tensor_single_scalar(
                        tDm[:, :CH], sl, 0.0, ALU.is_ge), reads=("dmask",), writes=("tD",))
                    p.emit("scalar", lambda e, sl=sl, lg=lg: e.activation(
                        sl, sl, AF.Exp, scale=lg), reads=("dmask",), writes=("dmask",))
                    p.emit("vector", lambda e, sl=sl: e.tensor_mul(sl, sl, tDm[:, :CH]),
                           reads=("dmask", "tD"), writes=("dmask",))
                slq = gam_q[:, h * CH:(h + 1) * CH]
                p.emit(g, lambda e, slq=slq: e.iota(
                    slq, [[1, CH]], base=1, channel_multiplier=0,
                    allow_small_or_imprecise_dtypes=True), writes=("gam_q",))
                p.emit("scalar", lambda e, slq=slq, lg=lg: e.activation(
                    slq, slq, AF.Exp, scale=lg), reads=("gam_q",), writes=("gam_q",))
                slk = gam_k[:, h * 2:(h + 1) * 2]
                p.emit(g, lambda e, slk=slk: e.iota(
                    slk, [[-128, 2]], base=CH - 1, channel_multiplier=-1,
                    allow_small_or_imprecise_dtypes=True), writes=("gam_k",))
                p.emit("scalar", lambda e, slk=slk, lg=lg: e.activation(
                    slk, slk, AF.Exp, scale=lg), reads=("gam_k",), writes=("gam_k",))

            p.barrier()
        with nc.sbuf_tensor("wv_sb", [128, 8 * 2 * E], F32) as wv_sb, \
             nc.sbuf_tensor("s_sb", [128, H * 2 * 512], F32) as s_sb, \
             nc.sbuf_tensor("xnc0", [128, 8 * CH], F32) as xnc0, \
             nc.sbuf_tensor("xnc1", [128, 8 * CH], F32) as xnc1, \
             nc.sbuf_tensor("qhc0", [128, 8 * CH], F32) as qhc0, \
             nc.sbuf_tensor("qhc1", [128, 8 * CH], F32) as qhc1, \
             nc.sbuf_tensor("khc0", [128, 8 * CH], F32) as khc0, \
             nc.sbuf_tensor("khc1", [128, 8 * CH], F32) as khc1, \
             nc.sbuf_tensor("v_sb", [128, 2 * 2 * E], F32) as v_sb, \
             nc.sbuf_tensor("on_sb", [128, 2 * 2 * E], F32) as on_sb, \
             nc.sbuf_tensor("scm", [128, 512], F32) as scm, \
             nc.sbuf_tensor("kdec", [128, 512], F32) as kdec, \
             nc.sbuf_tensor("qd_sb", [128, 512], F32) as qd_sb, \
             nc.sbuf_tensor("rst", [128, 4], F32) as rst, \
             nc.sbuf_tensor("sc2", [128, 512], F32) as sc2:

            for kc in range(8):
                p.dma("sync", lambda e, kc=kc: e.dma_start(
                    out=wv_sb[:, kc * 2 * E:(kc + 1) * 2 * E],
                    in_=wv_d[kc * 128:(kc + 1) * 128, :]),
                    writes=("wv_sb",))

            for s in range(SPC):
                p.emit("vector", lambda e: e.memset(s_sb[:, :], 0.0),
                       writes=("s_sb",))
                for c in range(NCH):
                    t0 = c * CH
                    slot = c % 2
                    xnc = xnc0 if slot == 0 else xnc1
                    qhc = qhc0 if slot == 0 else qhc1
                    khc = khc0 if slot == 0 else khc1
                    xk, qk_, kk = "xnc%d" % slot, "qhc%d" % slot, "khc%d" % slot
                    for (dst, dkey, srcd) in ((xnc, xk, xnfm_d), (qhc, qk_, qhat_d),
                                              (khc, kk, khat_d)):
                        p.dma("sync", lambda e, dst=dst, srcd=srcd, s=s, t0=t0:
                              e.dma_start(
                                  out=bass.AP(dst, 0, [[8 * CH, 128], [CH, 8], [1, CH]]),
                                  in_=dram_ap(srcd, SH_XN, [s, 0, 0, t0],
                                              [(2, 128), (1, 8), (3, CH)])),
                              writes=(dkey,))
                    for tau in range(2):
                        for dvt in range(4):
                            pbk = pb[tau]
                            pkey = PK[tau]
                            fns = []
                            for kc in range(8):
                                lhs = xnc[:, kc * CH + tau * 128: kc * CH + tau * 128 + 128]
                                rhs = wv_sb[:, kc * 2 * E + dvt * 512: kc * 2 * E + (dvt + 1) * 512]
                                fns.append((lambda e, lhs=lhs, rhs=rhs, pbk=pbk,
                                            st=(kc == 0), sp=(kc == 7): e.matmul(
                                    pbk[:, :], lhs, rhs, start=st, stop=sp),
                                    (xk, "wv_sb")))
                            p.group("tensor", fns, writes=(pkey,))
                            dst = v_sb[:, tau * 2 * E + dvt * 512: tau * 2 * E + (dvt + 1) * 512]
                            p.emit("scalar", lambda e, dst=dst, pbk=pbk: e.copy(
                                dst, pbk[:, :]), reads=(pkey,), writes=("v_sb",))
                    for h in range(H):
                        sbk = 2 if h % 2 == 0 else 5
                        kbk = 5 if h % 2 == 0 else 2
                        for dcl in range(2):
                            qsl = qhc[:, (2 * h + dcl) * CH:(2 * h + dcl + 1) * CH]
                            gsl = gam_q[:, h * CH:(h + 1) * CH]
                            dst = qd_sb[:, dcl * CH:(dcl + 1) * CH]
                            p.emit("vector", lambda e, dst=dst, qsl=qsl, gsl=gsl:
                                   e.tensor_tensor(dst, qsl, gsl, ALU.mult),
                                   reads=(qk_, "gam_q"), writes=("qd_sb",))
                        fns = []
                        for dcl in range(2):
                            lhs = khc[:, (2 * h + dcl) * CH: (2 * h + dcl) * CH + 128]
                            rhs = qhc[:, (2 * h + dcl) * CH:(2 * h + dcl + 1) * CH]
                            fns.append((lambda e, lhs=lhs, rhs=rhs, sbk=sbk,
                                        st=(dcl == 0),
                                        sp=(dcl == 1): e.matmul(
                                pb[sbk][:, :CH], lhs, rhs, start=st, stop=sp),
                                (kk, qk_)))
                        p.group("tensor", fns, writes=(PK[sbk],))
                        p.emit("vector", lambda e, h=h, sbk=sbk: e.tensor_tensor(
                            scm[:, :CH], pb[sbk][:, :CH],
                            dmask[:, (2 * h) * CH:(2 * h + 1) * CH], ALU.mult),
                            reads=(PK[sbk], "dmask"), writes=("scm",))
                        fns = []
                        for dcl in range(2):
                            lhs = khc[:, (2 * h + dcl) * CH + 128: (2 * h + dcl) * CH + 256]
                            rhs = qhc[:, (2 * h + dcl) * CH + 128:(2 * h + dcl + 1) * CH]
                            fns.append((lambda e, lhs=lhs, rhs=rhs, sbk=sbk,
                                        st=(dcl == 0),
                                        sp=(dcl == 1): e.matmul(
                                pb[sbk][:, CH:CH + 128], lhs, rhs, start=st, stop=sp),
                                (kk, qk_)))
                        p.group("tensor", fns, writes=(PK[sbk],))
                        p.emit("vector", lambda e, h=h, sbk=sbk: e.tensor_tensor(
                            scm[:, CH:CH + 128], pb[sbk][:, CH:CH + 128],
                            dmask[:, (2 * h + 1) * CH + 128:(2 * h + 2) * CH], ALU.mult),
                            reads=(PK[sbk], "dmask"), writes=("scm",))
                        for tau in range(2):
                            pbk = pb[3 + tau]
                            pkey = PK[3 + tau]
                            fns = []
                            for dcl in range(2):
                                lhs = qd_sb[:, dcl * CH + tau * 128: dcl * CH + tau * 128 + 128]
                                rhs = s_sb[:, (h * 2 + dcl) * 512:(h * 2 + dcl + 1) * 512]
                                fns.append((lambda e, lhs=lhs, rhs=rhs, pbk=pbk,
                                            st=(dcl == 0): e.matmul(
                                    pbk[:, :], lhs, rhs, start=st, stop=False),
                                    ("qd_sb", "s_sb")))
                            if tau == 0:
                                lhs = scm[:, 0:128]
                                rhs = v_sb[:, 0 * 2 * E + h * 512: 0 * 2 * E + (h + 1) * 512]
                                fns.append((lambda e, lhs=lhs, rhs=rhs, pbk=pbk:
                                            e.matmul(pbk[:, :], lhs, rhs,
                                                     start=False, stop=True),
                                            ("scm", "v_sb")))
                            else:
                                lhs = scm[:, 128:256]
                                rhs = v_sb[:, 0 * 2 * E + h * 512: 0 * 2 * E + (h + 1) * 512]
                                fns.append((lambda e, lhs=lhs, rhs=rhs, pbk=pbk:
                                            e.matmul(pbk[:, :], lhs, rhs,
                                                     start=False, stop=False),
                                            ("scm", "v_sb")))
                                lhs = scm[:, CH:CH + 128]
                                rhs = v_sb[:, 1 * 2 * E + h * 512: 1 * 2 * E + (h + 1) * 512]
                                fns.append((lambda e, lhs=lhs, rhs=rhs, pbk=pbk:
                                            e.matmul(pbk[:, :], lhs, rhs,
                                                     start=False, stop=True),
                                            ("scm", "v_sb")))
                            p.group("tensor", fns, writes=(pkey,))
                            rcol = rst[:, tau:tau + 1]
                            p.emit("scalar", lambda e, pbk=pbk, rcol=rcol: e.activation(
                                sc2[:, :], pbk[:, :], AF.Square, accum_out=rcol),
                                reads=(pkey,), writes=("sc2", "rst"))
                            p.emit("scalar", lambda e, rcol=rcol: e.activation(
                                rcol, rcol, AF.Sqrt, scale=1.0 / Dv, bias=epst[:, :1]),
                                reads=("rst", "epst"), writes=("rst",))
                            p.emit("vector", lambda e, rcol=rcol: e.reciprocal(
                                rcol, rcol), reads=("rst",), writes=("rst",))
                            dst = on_sb[:, tau * 2 * E + h * 512: tau * 2 * E + (h + 1) * 512]
                            p.emit("vector", lambda e, dst=dst, pbk=pbk, rcol=rcol, h=h:
                                   e.scalar_tensor_tensor(
                                       dst, pbk[:, :], rcol,
                                       gnb[:, h * 512:(h + 1) * 512],
                                       ALU.mult, ALU.mult),
                                   reads=(pkey, "rst", "gnb"), writes=("on_sb",))
                        for tau in range(2):
                            fns = []
                            for dcl in range(2):
                                src = khc[:, (2 * h + dcl) * CH + tau * 128:
                                          (2 * h + dcl) * CH + tau * 128 + 128]
                                dst = pb[kbk][:, dcl * 128:(dcl + 1) * 128]
                                fns.append((lambda e, src=src, dst=dst: e.transpose(
                                    dst, src, ident[:, :]), (kk, "ident")))
                            p.group("tensor", fns, writes=(PK[kbk],))
                            dst = kdec[:, tau * CH:(tau + 1) * CH]
                            gsl = gam_k[:, h * 2 + tau: h * 2 + tau + 1]
                            p.emit("scalar", lambda e, dst=dst, gsl=gsl, kbk=kbk:
                                   e.activation(
                                dst, pb[kbk][:, :CH], AF.Copy, scale=gsl),
                                reads=(PK[kbk], "gam_k"), writes=("kdec",))
                        gc = float(GAMMAS[h] ** CH)
                        for dcl in range(2):
                            fns = []
                            for tau in range(2):
                                lhs = kdec[:, tau * CH + dcl * 128: tau * CH + dcl * 128 + 128]
                                rhs = v_sb[:, tau * 2 * E + h * 512: tau * 2 * E + (h + 1) * 512]
                                fns.append((lambda e, lhs=lhs, rhs=rhs, st=(tau == 0),
                                            sp=(tau == 1): e.matmul(
                                    pb[6][:, :], lhs, rhs, start=st, stop=sp),
                                    ("kdec", "v_sb")))
                            p.group("tensor", fns, writes=(PK[6],))
                            ssl = s_sb[:, (h * 2 + dcl) * 512:(h * 2 + dcl + 1) * 512]
                            p.emit("vector", lambda e, ssl=ssl, gc=gc:
                                   e.scalar_tensor_tensor(ssl, ssl, gc, pb[6][:, :],
                                                          ALU.mult, ALU.add),
                                   reads=("s_sb", PK[6]), writes=("s_sb",))
                    for tau in range(2):
                        p.dma("sync", lambda e, s=s, t0=t0, tau=tau: e.dma_start(
                            out=on_d[s, t0 + tau * 128: t0 + (tau + 1) * 128, :],
                            in_=on_sb[:, tau * 2 * E:(tau + 1) * 2 * E]),
                            reads=("on_sb",))
            p.barrier()
        a2s.close()

    if "B" in phases:
        with nc.sbuf_tensor("wgp_sb", [128, 8 * 2 * E], F32) as wgp_sb, \
             nc.sbuf_tensor("wo_sb", [128, 16 * E], F32) as wo_sb, \
             nc.sbuf_tensor("xnfb", [128, 8 * TT], F32) as xnfb, \
             nc.sbuf_tensor("onb0", [128, 2 * E], F32) as onb0, \
             nc.sbuf_tensor("onb1", [128, 2 * E], F32) as onb1, \
             nc.sbuf_tensor("xrb0", [128, E], F32) as xrb0, \
             nc.sbuf_tensor("xrb1", [128, E], F32) as xrb1, \
             nc.sbuf_tensor("g_sb", [128, 2 * E], F32) as g_sb, \
             nc.sbuf_tensor("u_sb", [128, 2 * E], F32) as u_sb, \
             nc.sbuf_tensor("ufm_sb", [128, 2 * E], F32) as ufm_sb, \
             nc.sbuf_tensor("xp_sb0", [128, E], F32) as xp_sb0, \
             nc.sbuf_tensor("xp_sb1", [128, E], F32) as xp_sb1:

            for kc in range(8):
                p.dma("sync", lambda e, kc=kc: e.dma_start(
                    out=wgp_sb[:, kc * 2 * E:(kc + 1) * 2 * E],
                    in_=wg_d[kc * 128:(kc + 1) * 128, :]),
                    writes=("wgp_sb",))
            for fc in range(16):
                p.dma("sync", lambda e, fc=fc: e.dma_start(
                    out=wo_sb[:, fc * E:(fc + 1) * E],
                    in_=wo_d[fc * 128:(fc + 1) * 128, :]),
                    writes=("wo_sb",))

            for s in range(SPC):
                for it in range(NTT):
                    t0 = it * TT
                    p.dma("sync", lambda e, s=s, t0=t0: e.dma_start(
                        out=bass.AP(xnfb, 0, [[8 * TT, 128], [TT, 8], [1, TT]]),
                        in_=dram_ap(xnfm_d, SH_XN, [s, 0, 0, t0],
                                    [(2, 128), (1, 8), (3, TT)])),
                        writes=("xnfb",))
                    for tau in range(4):
                        tr = t0 + tau * 128
                        slot = tau % 2
                        onb = onb0 if slot == 0 else onb1
                        xrb = xrb0 if slot == 0 else xrb1
                        xpb = xp_sb0 if slot == 0 else xp_sb1
                        onk, xrk, xpk = ("onb%d" % slot, "xrb%d" % slot,
                                         "xp_sb%d" % slot)
                        p.dma("sync", lambda e, s=s, tr=tr, onb=onb: e.dma_start(
                            out=onb[:, :], in_=on_d[s, tr:tr + 128, :]),
                            writes=(onk,))
                        p.dma("sync", lambda e, s=s, tr=tr, xrb=xrb: e.dma_start(
                            out=xrb[:, :], in_=x_d[s, tr:tr + 128, :]),
                            writes=(xrk,))
                        for gvt in range(4):
                            pbk = pb[gvt % 2]
                            pkey = PK[gvt % 2]
                            fns = []
                            for kc in range(8):
                                lhs = xnfb[:, kc * TT + tau * 128:
                                           kc * TT + (tau + 1) * 128]
                                rhs = wgp_sb[:, kc * 2 * E + gvt * 512:
                                             kc * 2 * E + (gvt + 1) * 512]
                                fns.append((lambda e, lhs=lhs, rhs=rhs, pbk=pbk,
                                            st=(kc == 0), sp=(kc == 7): e.matmul(
                                    pbk[:, :], lhs, rhs, start=st, stop=sp),
                                    ("xnfb", "wgp_sb")))
                            p.group("tensor", fns, writes=(pkey,))
                            dst = g_sb[:, gvt * 512:(gvt + 1) * 512]
                            p.emit("scalar", lambda e, dst=dst, pbk=pbk:
                                   e.copy(dst, pbk[:, :]),
                                   reads=(pkey,), writes=("g_sb",))
                        p.emit("vector", lambda e, onb=onb: e.tensor_tensor(
                            u_sb[:, :], onb[:, :], g_sb[:, :], ALU.mult),
                            reads=(onk, "g_sb"), writes=("u_sb",))
                        for a in range(4):
                            ubk = 2 if a % 2 == 0 else 5
                            fns = []
                            for j in range(4):
                                fcq = a * 4 + j
                                srcu = u_sb[:, fcq * 128:(fcq + 1) * 128]
                                dstp = pb[ubk][:, j * 128:(j + 1) * 128]
                                fns.append((lambda e, srcu=srcu, dstp=dstp:
                                            e.transpose(dstp, srcu, ident[:, :]),
                                            ("u_sb", "ident")))
                            p.group("tensor", fns, writes=(PK[ubk],))
                            dst = ufm_sb[:, a * 512:(a + 1) * 512]
                            p.emit("scalar", lambda e, dst=dst, ubk=ubk: e.copy(
                                dst, pb[ubk][:, :]), reads=(PK[ubk],), writes=("ufm_sb",))
                        bpair = (3, 4) if tau % 2 == 0 else (6, 2)
                        for et in range(2):
                            pbk = pb[bpair[et]]
                            pkey = PK[bpair[et]]
                            fns = []
                            for fc in range(16):
                                lhs = ufm_sb[:, fc * 128:(fc + 1) * 128]
                                rhs = wo_sb[:, fc * E + et * 512:
                                            fc * E + (et + 1) * 512]
                                fns.append((lambda e, lhs=lhs, rhs=rhs, pbk=pbk,
                                            st=(fc == 0), sp=(fc == 15): e.matmul(
                                    pbk[:, :], lhs, rhs, start=st, stop=sp),
                                    ("ufm_sb", "wo_sb")))
                            p.group("tensor", fns, writes=(pkey,))
                            dst = xpb[:, et * 512:(et + 1) * 512]
                            p.emit("vector", lambda e, dst=dst, pbk=pbk, xrb=xrb,
                                   et=et: e.tensor_add(
                                dst, pbk[:, :], xrb[:, et * 512:(et + 1) * 512]),
                                reads=(pkey, xrk), writes=(xpk,))
                        p.dma("sync", lambda e, s=s, tr=tr, xpb=xpb: e.dma_start(
                            out=xp_d[s, tr:tr + 128, :], in_=xpb[:, :]),
                            reads=(xpk,))
            p.barrier()

    if "C1" in phases or "C2" in phases:
        ones1 = stack.enter_context(nc.sbuf_tensor("ones1", [128, 1], F32))
        p.emit("vector", lambda e: e.tensor_scalar_add(ones1[:, :], zeros1[:, :], 1.0),
               reads=("zeros1",), writes=("ones1",))
        smem_d = nc.dram_tensor("smem", [8], F32, kind="Internal")

    def quant_weight(wsrc_d, nkc, wid, wtern_sb, wtkey, mcb, mckey, qs):
        wfpA = qs["wfp"]
        wfpB = qs["wfpB"]
        rt1 = qs["rt1"]
        rt2 = qs["rt2"]
        acc = qs["acc"]
        pk6 = PK[6]
        p.emit("vector", lambda e: e.memset(acc[:, :1], 0.0), writes=("qacc",))
        for kc in range(nkc):
            wfp = wfpA if kc % 2 == 0 else wfpB
            wkq = "qwfp" if kc % 2 == 0 else "qwfpB"
            a_i = 1 + (kc % 2)
            p.dma("sync", lambda e, kc=kc, wfp=wfp: e.dma_start(
                out=wfp[:, :wid], in_=wsrc_d[kc * 128:(kc + 1) * 128, :]),
                writes=(wkq,))
            p.emit("vector", lambda e, wfp=wfp, a_i=a_i: e.reduce_sum(
                acc[:, a_i:a_i + 1], wfp[:, :wid], axis=AX.X,
                apply_absolute_value=True),
                reads=(wkq,), writes=("qacc",))
            p.emit("vector", lambda e, a_i=a_i: e.tensor_add(
                acc[:, :1], acc[:, :1], acc[:, a_i:a_i + 1]),
                reads=("qacc",), writes=("qacc",))
        p.group("tensor", [(lambda e: e.matmul(pb[6][:1, :1], acc[:, :1],
                                               ones1[:, :1], start=True, stop=True),
                            ("qacc", "ones1"))], writes=(pk6,))
        p.emit("vector", lambda e: e.tensor_scalar(
            acc[:1, 3:4], pb[6][:1, :1], 1.0 / (nkc * 128 * wid), 1e-5,
            ALU.mult, ALU.max), reads=(pk6,), writes=("qacc",))
        p.dma("sync", lambda e: e.dma_start(out=smem_d[0:1], in_=acc[:1, 3:4]),
              reads=("qacc",), writes=("smem",), buf="smem")
        p.dma("sync", lambda e, mcb=mcb: e.dma_start(
            out=mcb[:, :], in_=bass.AP(smem_d, 0, [[0, 128], [1, 1]])),
            reads=("smem",), writes=(mckey,), buf=mckey)
        p.emit("vector", lambda e, mcb=mcb: e.reciprocal(acc[:, 0:1], mcb[:, :]),
               reads=(mckey,), writes=("qacc",))
        for kc in range(nkc):
            wfp = wfpA if kc % 2 == 0 else wfpB
            wkq = "qwfp" if kc % 2 == 0 else "qwfpB"
            p.dma("sync", lambda e, kc=kc, wfp=wfp: e.dma_start(
                out=wfp[:, :wid], in_=wsrc_d[kc * 128:(kc + 1) * 128, :]),
                writes=(wkq,))
            vap = wfp[:, :wid]
            p.emit("vector", lambda e, vap=vap: e.tensor_scalar_mul(
                vap, vap, acc[:, 0:1]), reads=(wkq, "qacc"), writes=(wkq,))
            p.emit("vector", lambda e, vap=vap: e.tensor_single_scalar(
                rt1[:, :wid], vap, 0.5, ALU.is_gt), reads=(wkq,), writes=("qrt1",))
            p.emit("vector", lambda e, vap=vap: e.tensor_single_scalar(
                rt2[:, :wid], vap, -0.5, ALU.is_lt), reads=(wkq,), writes=("qrt2",))
            dst = wtern_sb[:, kc * wid:(kc + 1) * wid]
            p.emit("vector", lambda e, dst=dst: e.tensor_sub(
                dst, rt1[:, :wid], rt2[:, :wid]),
                reads=("qrt1", "qrt2"), writes=(wtkey,))

    if "C1" in phases:
        c1s = contextlib.ExitStack()
        wgt_sb = c1s.enter_context(nc.sbuf_tensor("wgt_sb", [128, 8 * INNER], BF16, side="right"))
        wut_sb = c1s.enter_context(nc.sbuf_tensor("wut_sb", [128, 8 * INNER], BF16, side="right"))
        mcg = c1s.enter_context(nc.sbuf_tensor("mcg", [128, 1], F32, side="right"))
        mcu = c1s.enter_context(nc.sbuf_tensor("mcu", [128, 1], F32, side="right"))
        with nc.sbuf_tensor("qwfp", [128, INNER], F32) as qwfp, \
             nc.sbuf_tensor("qwfpB", [128, INNER], F32) as qwfpB, \
             nc.sbuf_tensor("qrt1", [128, INNER], F32) as qrt1, \
             nc.sbuf_tensor("qrt2", [128, INNER], F32) as qrt2, \
             nc.sbuf_tensor("qacc", [128, 4], F32) as qacc:
            qs = {"wfp": qwfp, "wfpB": qwfpB, "rt1": qrt1, "rt2": qrt2, "acc": qacc}
            quant_weight(wgate_d, 8, INNER, wgt_sb, "wgt_sb", mcg, "mcg", qs)
            quant_weight(wup_d, 8, INNER, wut_sb, "wut_sb", mcu, "mcu", qs)
            p.barrier()

        c1l = contextlib.ExitStack()
        xpc0 = c1l.enter_context(nc.sbuf_tensor("xpc0", [128, E], F32))
        xpc1 = c1l.enter_context(nc.sbuf_tensor("xpc1", [128, E], F32))
        xn2 = c1l.enter_context(nc.sbuf_tensor("xn2", [128, E], F32))
        cst = c1l.enter_context(nc.sbuf_tensor("cst", [128, 8], F32))
        crt1 = c1l.enter_context(nc.sbuf_tensor("crt1", [128, E], F32))
        xqf = c1l.enter_context(nc.sbuf_tensor("xqf", [128, E], BF16))
        qbf = c1l.enter_context(nc.sbuf_tensor("qbf", [128, E], BF16))
        gate_sb = c1l.enter_context(nc.sbuf_tensor("gate_sb", [128, 2 * 512], F32))
        h_sb = c1l.enter_context(nc.sbuf_tensor("h_sb", [128, INNER], F32))
        hr1c = c1l.enter_context(nc.sbuf_tensor("hr1c", [128, E], F32))
        hr2c = c1l.enter_context(nc.sbuf_tensor("hr2c", [128, E], F32))
        gr1c = c1l.enter_context(nc.sbuf_tensor("gr1c", [128, E], F32))
        gr2c = c1l.enter_context(nc.sbuf_tensor("gr2c", [128, E], F32))
        gbf = c1l.enter_context(nc.sbuf_tensor("gbf", [128, E], BF16))
        gsc1 = c1l.enter_context(nc.sbuf_tensor("gsc1", [128, E], F32))
        csc1 = c1l.enter_context(nc.sbuf_tensor("csc1", [128, E], F32))
        hqt0 = c1l.enter_context(nc.sbuf_tensor("hqt0", [128, 1024], BF16))
        hqt1 = c1l.enter_context(nc.sbuf_tensor("hqt1", [128, 1024], BF16))
        if True:
            for s in range(SPC):
                for it in range(NTT):
                    for tau in range(4):
                        tr = it * TT + tau * 128
                        tcol = it * 4 + tau
                        slot = tau % 2
                        xpc = xpc0 if slot == 0 else xpc1
                        xpk = "xpc%d" % slot
                        p.dma("sync", lambda e, s=s, tr=tr, xpc=xpc: e.dma_start(
                            out=xpc[:, :], in_=xp_d[s, tr:tr + 128, :]),
                            writes=(xpk,), sem=None, buf=xpk)
                        mcol = cst[:, 0:1]
                        qcol = cst[:, 1:2]
                        p.emit("vector", lambda e, xpc=xpc, mcol=mcol: e.reduce_sum(
                            mcol, xpc[:, :], axis=AX.X), reads=(xpk,), writes=("cst",))
                        p.emit("scalar", lambda e, mcol=mcol: e.mul(
                            mcol, mcol, 1.0 / E), reads=("cst",), writes=("cst",))
                        p.emit("vector", lambda e, xpc=xpc, mcol=mcol:
                               e.tensor_scalar_sub(xn2[:, :], xpc[:, :], mcol),
                               reads=(xpk, "cst"), writes=("xn2",))
                        p.emit("scalar", lambda e, qcol=qcol: e.activation(
                            csc1[:, :], xn2[:, :], AF.Square, accum_out=qcol),
                            reads=("xn2",), writes=("csc1", "cst"))
                        p.emit("scalar", lambda e, qcol=qcol: e.activation(
                            qcol, qcol, AF.Sqrt, scale=1.0 / E, bias=epst[:, :1]),
                            reads=("cst", "epst"), writes=("cst",))
                        p.emit("vector", lambda e, qcol=qcol: e.reciprocal(
                            qcol, qcol), reads=("cst",), writes=("cst",))
                        p.emit("vector", lambda e, qcol=qcol: e.tensor_scalar_mul(
                            xn2[:, :], xn2[:, :], qcol),
                            reads=("xn2", "cst"), writes=("xn2",))
                        am = cst[:, 2:3]
                        sx = cst[:, 3:4]
                        isx = cst[:, 4:5]
                        p.emit("vector", lambda e, am=am: e.reduce_max(
                            am, xn2[:, :], axis=AX.X, apply_absolute_value=True),
                            reads=("xn2",), writes=("cst",))
                        p.emit("vector", lambda e, am=am: e.tensor_scalar_max(
                            am, am, 1e-5), reads=("cst",), writes=("cst",))
                        p.emit("vector", lambda e, am=am, sx=sx: e.reciprocal(
                            sx, am), reads=("cst",), writes=("cst",))
                        p.emit("vector", lambda e, sx=sx: e.tensor_scalar_mul(
                            sx, sx, 127.0), reads=("cst",), writes=("cst",))
                        p.emit("vector", lambda e, am=am, isx=isx: e.tensor_scalar_mul(
                            isx, am, 1.0 / 127.0), reads=("cst",), writes=("cst",))
                        p.emit("scalar", lambda e, sx=sx: e.activation(
                            xn2[:, :], xn2[:, :], AF.Copy, scale=sx),
                            reads=("xn2", "cst"), writes=("xn2",))
                        emit_round(p, crt1[:, :], csc1[:, :], xn2[:, :],
                                   "crt1", "csc1", "xn2")
                        p.emit("vector", lambda e: e.tensor_scalar(
                            qbf[:, :], crt1[:, :], -128.0, 127.0, ALU.max, ALU.min),
                            reads=("crt1",), writes=("qbf",))
                        fns = []
                        for fc in range(8):
                            srcq = qbf[:, fc * 128:(fc + 1) * 128]
                            dstq = pbb[:, fc * 128:(fc + 1) * 128]
                            fns.append((lambda e, srcq=srcq, dstq=dstq: e.transpose(
                                dstq, srcq, identb[:, :]), ("qbf", "identb")))
                        p.group("tensor", fns, writes=("pbb",))
                        p.emit("scalar", lambda e: e.copy(
                            xqf[:, :], pbb[:, :E]), reads=("pbb",), writes=("xqf",))
                        scg = cst[:, 5:6]
                        scu = cst[:, 6:7]
                        p.emit("vector", lambda e, isx=isx, scg=scg: e.tensor_tensor(
                            scg, isx, mcg[:, :], ALU.mult),
                            reads=("cst", "mcg"), writes=("cst",))
                        p.emit("vector", lambda e, isx=isx, scu=scu: e.tensor_tensor(
                            scu, isx, mcu[:, :], ALU.mult),
                            reads=("cst", "mcu"), writes=("cst",))
                        for ivt in range(8):
                            gb, ub = (0, 1) if ivt % 2 == 0 else (3, 4)
                            for (wsb2, wk2, pbk, pkey) in (
                                    (wgt_sb, "wgt_sb", pb[gb], PK[gb]),
                                    (wut_sb, "wut_sb", pb[ub], PK[ub])):
                                fns = []
                                for kc in range(8):
                                    lhs = xqf[:, kc * 128:(kc + 1) * 128]
                                    rhs = wsb2[:, kc * INNER + ivt * 512:
                                               kc * INNER + (ivt + 1) * 512]
                                    fns.append((lambda e, lhs=lhs, rhs=rhs, pbk=pbk,
                                                st=(kc == 0), sp=(kc == 7): e.matmul(
                                        pbk[:, :], lhs, rhs, start=st, stop=sp),
                                        ("xqf", wk2)))
                                p.group("tensor", fns, writes=(pkey,))
                            dstg = gate_sb[:, (ivt % 2) * 512:(ivt % 2) * 512 + 512]
                            p.emit("scalar", lambda e, dstg=dstg, gb=gb, scg=scg:
                                   e.activation(
                                dstg, pb[gb][:, :], AF.Silu, scale=scg),
                                reads=(PK[gb], "cst"),
                                writes=("gate_sb%d" % (ivt % 2),))
                            dsth = h_sb[:, ivt * 512:(ivt + 1) * 512]
                            p.emit("scalar", lambda e, dsth=dsth, ub=ub, scu=scu:
                                   e.activation(
                                dsth, pb[ub][:, :], AF.Copy, scale=scu),
                                reads=(PK[ub], "cst"), writes=("h_sb",))
                            if ivt % 2 == 1:
                                dhp = h_sb[:, (ivt - 1) * 512:(ivt + 1) * 512]
                                p.emit("vector", lambda e, dhp=dhp:
                                       e.tensor_mul(dhp, gate_sb[:, :], dhp),
                                       reads=("gate_sb0", "gate_sb1", "h_sb"),
                                       writes=("h_sb",))
                        am2 = cst[:, 2:3]
                        sx2 = cst[:, 3:4]
                        isx2 = cst[:, 7:8]
                        p.emit("vector", lambda e, am2=am2: e.reduce_max(
                            am2, h_sb[:, :], axis=AX.X, apply_absolute_value=True),
                            reads=("h_sb",), writes=("cst",))
                        p.emit("vector", lambda e, am2=am2: e.tensor_scalar_max(
                            am2, am2, 1e-5), reads=("cst",), writes=("cst",))
                        p.emit("vector", lambda e, am2=am2, sx2=sx2: e.reciprocal(
                            sx2, am2), reads=("cst",), writes=("cst",))
                        p.emit("vector", lambda e, sx2=sx2: e.tensor_scalar_mul(
                            sx2, sx2, 127.0), reads=("cst",), writes=("cst",))
                        p.emit("vector", lambda e, am2=am2, isx2=isx2:
                               e.tensor_scalar_mul(isx2, am2, 1.0 / 127.0),
                               reads=("cst",), writes=("cst",))
                        p.dma("sync", lambda e, s=s, tcol=tcol, isx2=isx2:
                              e.dma_start(
                                  out=dram_ap(hsc_d, [SPC, 128, T // 128],
                                              [s, 0, tcol], [(1, 128), (2, 1)]),
                                  in_=isx2, allow_slow_non_contiguous=True),
                              reads=("cst",), buf="hsc_st")
                        for a in range(4):
                            hsl = h_sb[:, a * E:(a + 1) * E]
                            lane = "vector"
                            r1, r2, rsc = ((hr1c, hr2c, csc1) if a % 2 == 0
                                           else (gr1c, gr2c, gsc1))
                            k1, k2, ks = (("hr1c", "hr2c", "csc1") if a % 2 == 0
                                          else ("gr1c", "gr2c", "gsc1"))
                            qb, qk2 = (qbf, "qbf") if a % 2 == 0 else (gbf, "gbf")
                            p.emit(lane, lambda e, sx2=sx2, hsl=hsl, r2=r2:
                                   e.tensor_scalar_mul(r2[:, :], hsl, sx2),
                                   reads=("h_sb", "cst"), writes=(k2,))
                            emit_round(p, r1[:, :], rsc[:, :], r2[:, :],
                                       k1, ks, k2, eng=lane)
                            p.emit(lane, lambda e, r1=r1, qb=qb: e.tensor_scalar(
                                qb[:, :], r1[:, :], -128.0, 127.0,
                                ALU.max, ALU.min), reads=(k1,), writes=(qk2,))
                            fns = []
                            for j in range(8):
                                srch = qb[:, j * 128:(j + 1) * 128]
                                dsth2 = pbb[:, j * 128:(j + 1) * 128]
                                fns.append((lambda e, srch=srch, dsth2=dsth2:
                                            e.transpose(dsth2, srch, identb[:, :]),
                                            (qk2, "identb")))
                            p.group("tensor", fns, writes=("pbb",))
                            hqt = hqt0 if a % 2 == 0 else hqt1
                            hqk = "hqt%d" % (a % 2)
                            p.emit("scalar", lambda e, hqt=hqt: e.copy(
                                hqt[:, :], pbb[:, :]), reads=("pbb",), writes=(hqk,))
                            p.dma("sync", lambda e, s=s, tr=tr, a=a, hqt=hqt:
                                  e.dma_start(
                                      out=dram_ap(hq_d, SH_HQ, [s, 8 * a, 0, tr],
                                                  [(2, 128), (1, 8), (3, 128)]),
                                      in_=bass.AP(hqt, 0, [[1024, 128], [128, 8],
                                                           [1, 128]])),
                                  reads=(hqk,), buf=hqk)
                        p.barrier()
        c1l.close()
        c1s.close()

    if "C2" in phases:
        c2s = contextlib.ExitStack()
        wdt_sb = c2s.enter_context(nc.sbuf_tensor("wdt_sb", [128, 32 * E], BF16, side="right"))
        mcd = c2s.enter_context(nc.sbuf_tensor("mcd", [128, 1], F32, side="right"))
        with nc.sbuf_tensor("qwfp2", [128, E], F32) as qwfp2, \
             nc.sbuf_tensor("qwfp2B", [128, E], F32) as qwfp2B, \
             nc.sbuf_tensor("qrt1b", [128, E], F32) as qrt1b, \
             nc.sbuf_tensor("qrt2b", [128, E], F32) as qrt2b, \
             nc.sbuf_tensor("qaccb", [128, 4], F32) as qaccb:
            qs2 = {"wfp": qwfp2, "wfpB": qwfp2B, "rt1": qrt1b, "rt2": qrt2b,
                   "acc": qaccb}
            quant_weight(wdown_d, 32, E, wdt_sb, "wdt_sb", mcd, "mcd", qs2)
            p.barrier()

        with nc.sbuf_tensor("hqc0", [128, INNER], BF16) as hqc0, \
             nc.sbuf_tensor("hqc1", [128, INNER], BF16) as hqc1, \
             nc.sbuf_tensor("xpd0", [128, E], F32) as xpd0, \
             nc.sbuf_tensor("xpd1", [128, E], F32) as xpd1, \
             nc.sbuf_tensor("oub0", [128, E], F32) as oub0, \
             nc.sbuf_tensor("oub1", [128, E], F32) as oub1, \
             nc.sbuf_tensor("dst2", [128, 2], F32) as dst2:

            for s in range(SPC):
                for it in range(NTT):
                    for tau in range(4):
                        tr = it * TT + tau * 128
                        tcol = it * 4 + tau
                        slot = tau % 2
                        hqc = hqc0 if slot == 0 else hqc1
                        xpd = xpd0 if slot == 0 else xpd1
                        oub = oub0 if slot == 0 else oub1
                        hk, xk2, ok2 = ("hqc%d" % slot, "xpd%d" % slot,
                                        "oub%d" % slot)
                        p.dma("sync", lambda e, s=s, tr=tr, hqc=hqc: e.dma_start(
                            out=bass.AP(hqc, 0, [[INNER, 128], [128, 32], [1, 128]]),
                            in_=dram_ap(hq_d, SH_HQ, [s, 0, 0, tr],
                                        [(2, 128), (1, 32), (3, 128)])),
                            writes=(hk,))
                        p.dma("sync", lambda e, s=s, tr=tr, xpd=xpd: e.dma_start(
                            out=xpd[:, :], in_=xp_d[s, tr:tr + 128, :]),
                            writes=(xk2,))
                        p.dma("sync", lambda e, s=s, tcol=tcol: e.dma_start(
                            out=dst2[:, 0:1],
                            in_=dram_ap(hsc_d, [SPC, 128, T // 128],
                                        [s, 0, tcol], [(1, 128), (2, 1)]),
                            allow_slow_non_contiguous=True),
                            writes=("dst2",))
                        p.emit("vector", lambda e: e.tensor_tensor(
                            dst2[:, 1:2], dst2[:, 0:1], mcd[:, :], ALU.mult),
                            reads=("dst2", "mcd"), writes=("dst2",))
                        cpair = (0, 1) if tau % 2 == 0 else (3, 4)
                        for et in range(2):
                            pbk = pb[cpair[et]]
                            pkey = PK[cpair[et]]
                            fns = []
                            for ic in range(32):
                                lhs = hqc[:, ic * 128:(ic + 1) * 128]
                                rhs = wdt_sb[:, ic * E + et * 512:
                                             ic * E + (et + 1) * 512]
                                fns.append((lambda e, lhs=lhs, rhs=rhs, pbk=pbk,
                                            st=(ic == 0), sp=(ic == 31): e.matmul(
                                    pbk[:, :], lhs, rhs, start=st, stop=sp),
                                    (hk, "wdt_sb")))
                            p.group("tensor", fns, writes=(pkey,))
                            dsto = oub[:, et * 512:(et + 1) * 512]
                            xsl = xpd[:, et * 512:(et + 1) * 512]
                            p.emit("vector", lambda e, dsto=dsto, pbk=pbk, xsl=xsl:
                                   e.scalar_tensor_tensor(
                                       dsto, pbk[:, :], dst2[:, 1:2], xsl,
                                       ALU.mult, ALU.add),
                                   reads=(pkey, "dst2", xk2), writes=(ok2,))
                        p.dma("sync", lambda e, s=s, tr=tr, oub=oub: e.dma_start(
                            out=out_d[s, tr:tr + 128, :], in_=oub[:, :]),
                            reads=(ok2,))
            p.barrier()
        c2s.close()

    for sname, v in list(p.cnt.items()):
        if sname.startswith("D_") and v > 0:
            p._wait("sync", sname, v)
            p._wait("gpsimd", sname, v)

    with nc.Block() as block:
        p.finalize(block)
    return nc, stack


def kernel(L0_attn_x, L0_attn_Wq, L0_attn_Wk, L0_attn_Wv, L0_attn_Wg, L0_attn_Wo,
           L0_attn_gn_weight, L0_ffn_w_gate, L0_ffn_w_up, L0_ffn_w_down):
    x = np.ascontiguousarray(np.asarray(L0_attn_x, dtype=np.float32))
    base = {
        "wq": np.ascontiguousarray(np.asarray(L0_attn_Wq, np.float32)),
        "wk": np.ascontiguousarray(np.asarray(L0_attn_Wk, np.float32)),
        "wv": np.ascontiguousarray(np.asarray(L0_attn_Wv, np.float32)),
        "wg": np.ascontiguousarray(np.asarray(L0_attn_Wg, np.float32)),
        "wo": np.ascontiguousarray(np.asarray(L0_attn_Wo, np.float32)),
        "gn": np.ascontiguousarray(np.asarray(L0_attn_gn_weight, np.float32)),
        "wgate": np.ascontiguousarray(np.asarray(L0_ffn_w_gate, np.float32).T),
        "wup": np.ascontiguousarray(np.asarray(L0_ffn_w_up, np.float32).T),
        "wdown": np.ascontiguousarray(np.asarray(L0_ffn_w_down, np.float32).T),
    }
    nc, stack = build_program()
    in_maps = [dict(base, x=np.ascontiguousarray(x[i * SPC:(i + 1) * SPC]))
               for i in range(NCORES)]
    res = run_bass_kernel_spmd(nc, in_maps, core_ids=list(range(NCORES)))
    stack.close()
    out = np.concatenate([r["out"] for r in res.results], axis=0)
    return out.reshape(B, T, E)


# version 15
# speedup vs baseline: 1.0126x; 1.0126x over previous
import math
import contextlib
import numpy as np

import concourse.bass as bass
import concourse.mybir as mybir
from concourse.bass_utils import run_bass_kernel_spmd

F32 = mybir.dt.float32
BF16 = mybir.dt.bfloat16
AF = mybir.ActivationFunctionType
ALU = mybir.AluOpType
AX = mybir.AxisListType

B, T, E, H = 32, 2048, 1024, 4
Dk, Dv = 256, 512
INNER = 4096
NCORES = 8
SPC = B // NCORES
CH = 256
NCH = T // CH
TT = 512
NTT = T // TT

C1 = float(1.5 * 2**23)
C2 = float(1.5 * 2**52)
C3 = float(1.5 * 2**32)
GAMMAS = [1.0 - 2.0 ** (-5.0 - h) for h in range(H)]


def dram_ap(td, shape, idx, dims):
    strides = [1] * len(shape)
    for i in range(len(shape) - 2, -1, -1):
        strides[i] = strides[i + 1] * shape[i + 1]
    off = sum(i * s for i, s in zip(idx, strides))
    ap = [[strides[d], c] for d, c in dims]
    return bass.AP(td, off, ap)


class Prog:

    ENGINES = ("tensor", "vector", "scalar", "gpsimd", "sync")

    def __init__(self, nc, stack):
        self.nc = nc
        self.stack = stack
        self.q = {e: [] for e in self.ENGINES}
        self.sem = {}
        self.cnt = {}
        for e in self.ENGINES:
            self.mksem(stack, "ch_" + e)
        self.lastw = {}
        self.readers = {}
        self.wm = {e: {} for e in self.ENGINES}
        self.ninstr = 0

    def mksem(self, stack, name):
        self.sem[name] = stack.enter_context(self.nc.semaphore(name))
        self.cnt[name] = 0

    def dsem(self, buf):
        name = "D_" + buf
        if name not in self.sem:
            self.mksem(self.stack, name)
        return name

    def _wait(self, eng, sname, val):
        if val <= 0 or self.wm[eng].get(sname, 0) >= val:
            return
        h = self.sem[sname]
        self.q[eng].append(lambda e, h=h, v=val: e.wait_ge(h, v))
        self.wm[eng][sname] = val
        self.ninstr += 1

    def _dep_waits(self, eng, reads, writes):
        need = {}
        for r in reads:
            w = self.lastw.get(r)
            if w:
                need[w[0]] = max(need.get(w[0], 0), w[1])
        for w in writes:
            lw = self.lastw.get(w)
            if lw:
                need[lw[0]] = max(need.get(lw[0], 0), lw[1])
            for s2, v2 in self.readers.get(w, {}).items():
                need[s2] = max(need.get(s2, 0), v2)
        for s2, v2 in need.items():
            if s2.startswith("D_"):
                v2 = self.cnt[s2]
            self._wait(eng, s2, v2)

    def _register(self, sname, val, reads, writes):
        for w in writes:
            self.lastw[w] = (sname, val)
            self.readers[w] = {}
        for r in reads:
            d = self.readers.setdefault(r, {})
            d[sname] = max(d.get(sname, 0), val)

    def emit(self, eng, fn, reads=(), writes=(), inc=1, sem=None):
        sname = sem or ("ch_" + eng)
        self._dep_waits(eng, reads, writes)
        h = self.sem[sname]
        self.cnt[sname] += inc
        val = self.cnt[sname]

        def run(e, fn=fn, h=h, inc=inc):
            fn(e).then_inc(h, inc)
        self.q[eng].append(run)
        self.ninstr += 1
        self._register(sname, val, reads, writes)
        return sname, val

    def group(self, eng, fns_reads, writes, inc=1, sem=None):
        sname = sem or ("ch_" + eng)
        all_reads = []
        n = len(fns_reads)
        for i, (gfn, greads) in enumerate(fns_reads):
            self._dep_waits(eng, greads, writes if i == 0 else ())
            all_reads += list(greads)
            if i < n - 1:
                self.q[eng].append(lambda e, f=gfn: f(e))
                self.ninstr += 1
        h = self.sem[sname]
        self.cnt[sname] += inc
        val = self.cnt[sname]
        gfn = fns_reads[-1][0]

        def run(e, fn=gfn, h=h, inc=inc):
            fn(e).then_inc(h, inc)
        self.q[eng].append(run)
        self.ninstr += 1
        self._register(sname, val, all_reads, writes)
        return sname, val

    def dma(self, eng, fn, reads=(), writes=(), sem=None, buf=None):
        if buf is None:
            buf = (list(writes) + list(reads) + ["misc"])[0]
        return self.emit(eng, fn, reads, writes, inc=16, sem=self.dsem(buf))

    def barrier(self):
        totals = {s: v for s, v in self.cnt.items() if v > 0}
        for e in self.ENGINES:
            for s, v in totals.items():
                self._wait(e, s, v)
        self.lastw = {}
        self.readers = {}

    def finalize(self, block):
        for ename in self.ENGINES:
            qq = self.q[ename]
            sec = getattr(block, ename)

            def make(qq):
                def _(eng):
                    for f in qq:
                        f(eng)
                return _
            sec(make(qq))


def emit_round(p, out_ap, tmp_ap, in_ap, okey, tkey, ikey, eng="vector"):
    M = p.mask01
    p.emit(eng, lambda e: e.tensor_scalar(out_ap, in_ap, C1, C1, ALU.add, ALU.subtract),
           reads=(ikey,), writes=(okey,))
    p.emit(eng, lambda e: e.tensor_scalar(tmp_ap, in_ap, C2, C2, ALU.add, ALU.subtract),
           reads=(ikey,), writes=(tkey,))
    p.emit(eng, lambda e: e.tensor_sub(tmp_ap, out_ap, tmp_ap),
           reads=(okey, tkey), writes=(tkey,))
    p.emit(eng, lambda e: e.tensor_scalar_mul(tmp_ap, tmp_ap, M[:, :1]),
           reads=(tkey, "mask01"), writes=(tkey,))
    p.emit(eng, lambda e: e.tensor_sub(out_ap, out_ap, tmp_ap),
           reads=(okey, tkey), writes=(okey,))


def build_program(phases=("A1", "A2", "B", "C1", "C2"), debug=False):
    nc = bass.Bass("TRN2", target_bir_lowering=False)
    stack = contextlib.ExitStack()

    x_d = nc.dram_tensor("x", [SPC, T, E], F32, kind="ExternalInput")
    wq_d = nc.dram_tensor("wq", [E, E], F32, kind="ExternalInput")
    wk_d = nc.dram_tensor("wk", [E, E], F32, kind="ExternalInput")
    wv_d = nc.dram_tensor("wv", [E, 2 * E], F32, kind="ExternalInput")
    wg_d = nc.dram_tensor("wg", [E, 2 * E], F32, kind="ExternalInput")
    wo_d = nc.dram_tensor("wo", [2 * E, E], F32, kind="ExternalInput")
    gn_d = nc.dram_tensor("gn", [Dv], F32, kind="ExternalInput")
    wgate_d = nc.dram_tensor("wgate", [E, INNER], F32, kind="ExternalInput")
    wup_d = nc.dram_tensor("wup", [E, INNER], F32, kind="ExternalInput")
    wdown_d = nc.dram_tensor("wdown", [INNER, E], F32, kind="ExternalInput")
    out_d = nc.dram_tensor("out", [SPC, T, E], F32, kind="ExternalOutput")

    skind = "ExternalOutput" if debug else "Internal"
    SH_XN = [SPC, E // 128, 128, T]
    xnfm_d = nc.dram_tensor("xnfm", SH_XN, F32, kind=skind)
    qhat_d = nc.dram_tensor("qhat", SH_XN, F32, kind=skind)
    khat_d = nc.dram_tensor("khat", SH_XN, F32, kind=skind)
    on_d = nc.dram_tensor("on", [SPC, T, 2 * E], F32, kind=skind)
    xp_d = nc.dram_tensor("xp", [SPC, T, E], F32, kind=skind)
    SH_HQ = [SPC, INNER // 128, 128, T]
    hq_d = nc.dram_tensor("hq", SH_HQ, BF16, kind=skind)
    hsc_d = nc.dram_tensor("hsc", [SPC, 128, T // 128], F32, kind=skind)

    p = Prog(nc, stack)

    ident = stack.enter_context(nc.sbuf_tensor("ident", [128, 128], F32))
    identb = stack.enter_context(nc.sbuf_tensor("identb", [128, 128], BF16))
    epst = stack.enter_context(nc.sbuf_tensor("epst", [128, 1], F32))
    zeros1 = stack.enter_context(nc.sbuf_tensor("zeros1", [128, 1], F32))
    mask01 = stack.enter_context(nc.sbuf_tensor("mask01", [128, 1], F32))
    p_mask_ref = mask01

    pb = [stack.enter_context(nc.psum_tensor(f"pb{i}", [128, 512], F32))
          for i in range(7)]
    PK = ["p%d" % i for i in range(7)]
    pbb = stack.enter_context(nc.psum_tensor("pbb", [128, 1024], BF16))

    with nc.sbuf_tensor("onesb", [128, 128], BF16) as onesb, \
         nc.sbuf_tensor("tD0", [128, 128], F32) as tD:

        g = "gpsimd"
        p.emit(g, lambda e: e.memset(zeros1[:, :], 0.0), writes=("zeros1",))
        p.emit(g, lambda e: e.memset(ident[:, :], 0.0), writes=("ident",))
        p.emit(g, lambda e: e.memset(identb[:, :], 0.0), writes=("identb",))
        p.emit(g, lambda e: e.memset(tD[:, :128], 1.0), writes=("tD",))
        p.emit(g, lambda e: e.memset(onesb[:, :], 1.0), writes=("onesb",))
        p.emit(g, lambda e: e.affine_select(
            ident[:, :], tD[:, :128], [[-1, 128]], ALU.is_equal, 0.0,
            base=0, channel_multiplier=1),
            reads=("tD",), writes=("ident",))
        p.emit(g, lambda e: e.affine_select(
            identb[:, :], onesb[:, :], [[-1, 128]], ALU.is_equal, 0.0,
            base=0, channel_multiplier=1),
            reads=("onesb",), writes=("identb",))
        p.emit("vector", lambda e: e.tensor_scalar_add(epst[:, :], zeros1[:, :], 1e-5),
               reads=("zeros1",), writes=("epst",))
        p.emit("vector", lambda e: e.tensor_scalar_add(mask01[:, :], zeros1[:, :],
                                                       8.25),
               reads=("zeros1",), writes=("mask01",))
        p.emit("vector", lambda e: e.tensor_scalar(mask01[:, :], mask01[:, :],
                                                   C2, C2, ALU.add, ALU.subtract),
               reads=("mask01",), writes=("mask01",))
        p.emit("vector", lambda e: e.tensor_scalar_mul(mask01[:, :], mask01[:, :],
                                                       0.125),
               reads=("mask01",), writes=("mask01",))
        p.mask01 = mask01

        p.barrier()

    if "A1" in phases:
        a1s = contextlib.ExitStack()
        c16 = a1s.enter_context(nc.sbuf_tensor("c16", [128, T], F32, side="right"))
        s16 = a1s.enter_context(nc.sbuf_tensor("s16", [128, T], F32, side="right"))
        with nc.sbuf_tensor("tA", [128, T], F32) as tA, \
               nc.sbuf_tensor("tB", [128, T], F32) as tB, \
               nc.sbuf_tensor("tC", [128, T], F32) as tC, \
               nc.sbuf_tensor("tD", [128, T], F32) as tD, \
               nc.sbuf_tensor("pidx", [128, 1], F32) as pidx, \
               nc.sbuf_tensor("invf", [128, 1], F32) as invf:
            g = "gpsimd"
            p.emit(g, lambda e: e.iota(pidx[:, :], [[0, 1]], base=0, channel_multiplier=1,
                                       allow_small_or_imprecise_dtypes=True),
                   writes=("pidx",))
            p.emit("vector", lambda e: e.tensor_scalar_add(invf[:, :], zeros1[:, :], 1.0),
                   reads=("zeros1",), writes=("invf",))
            p.emit("vector", lambda e: e.tensor_copy(tB[:, :1], pidx[:, :]),
                   reads=("pidx",), writes=("tB",))
            for j in range(6, -1, -1):
                rj = 10000.0 ** (-(2.0 ** j) / 128.0)
                bit = float(2 ** j)
                p.emit("vector", lambda e, bit=bit: e.tensor_single_scalar(
                    tC[:, :1], tB[:, :1], bit, ALU.is_ge), reads=("tB",), writes=("tC",))
                p.emit("vector", lambda e, bit=bit: e.tensor_scalar(
                    tD[:, :1], tC[:, :1], bit, None, ALU.mult), reads=("tC",), writes=("tD",))
                p.emit("vector", lambda e: e.tensor_sub(tB[:, :1], tB[:, :1], tD[:, :1]),
                       reads=("tB", "tD"), writes=("tB",))
                p.emit("vector", lambda e, rj=rj: e.tensor_scalar(
                    tC[:, :1], tC[:, :1], rj - 1.0, 1.0, ALU.mult, ALU.add),
                    reads=("tC",), writes=("tC",))
                p.emit("vector", lambda e: e.tensor_mul(invf[:, :], invf[:, :], tC[:, :1]),
                       reads=("invf", "tC"), writes=("invf",))
            p.emit(g, lambda e: e.iota(tA[:, :], [[1, T]], base=0, channel_multiplier=0,
                                       allow_small_or_imprecise_dtypes=True),
                   writes=("tA",))
            p.emit("vector", lambda e: e.tensor_scalar_mul(tA[:, :], tA[:, :], invf[:, :1]),
                   reads=("tA", "invf"), writes=("tA",))
            p.emit("vector", lambda e: e.tensor_scalar(
                tC[:, :], tA[:, :], 1.0 / (2 * math.pi), 0.5, ALU.mult, ALU.add),
                reads=("tA",), writes=("tC",))
            p.emit("vector", lambda e: e.memset(tB[:, :], 0.0), writes=("tB",))
            for j in range(8, -1, -1):
                v = float(2 ** j)
                p.emit("vector", lambda e, v=v: e.tensor_single_scalar(
                    tD[:, :], tC[:, :], v, ALU.is_ge), reads=("tC",), writes=("tD",))
                p.emit("vector", lambda e, v=v: e.tensor_scalar_mul(tD[:, :], tD[:, :], v),
                       reads=("tD",), writes=("tD",))
                p.emit("vector", lambda e: e.tensor_sub(tC[:, :], tC[:, :], tD[:, :]),
                       reads=("tC", "tD"), writes=("tC",))
                p.emit("vector", lambda e: e.tensor_add(tB[:, :], tB[:, :], tD[:, :]),
                       reads=("tB", "tD"), writes=("tB",))
            p.emit("vector", lambda e: e.tensor_scalar_mul(tB[:, :], tB[:, :],
                                                           2 * math.pi),
                   reads=("tB",), writes=("tB",))
            p.emit("vector", lambda e: e.tensor_sub(tA[:, :], tA[:, :], tB[:, :]),
                   reads=("tA", "tB"), writes=("tA",))
            p.emit("scalar", lambda e: e.activation(tC[:, :], tA[:, :], AF.Abs),
                   reads=("tA",), writes=("tC",))
            p.emit("vector", lambda e: e.tensor_scalar_add(tD[:, :1], zeros1[:, :],
                                                           math.pi / 2),
                   reads=("zeros1",), writes=("tD",))
            p.emit("scalar", lambda e: e.activation(c16[:, :], tC[:, :], AF.Sin,
                                                    scale=-1.0, bias=tD[:, :1]),
                   reads=("tC", "tD"), writes=("c16",))
            p.emit("vector", lambda e: e.tensor_scalar_mul(c16[:, :], c16[:, :],
                                                           1.0 / 16),
                   reads=("c16",), writes=("c16",))
            p.emit("vector", lambda e: e.tensor_scalar(
                tB[:, :], tC[:, :], -1.0, math.pi, ALU.mult, ALU.add),
                reads=("tC",), writes=("tB",))
            p.emit("vector", lambda e: e.tensor_tensor(tC[:, :], tC[:, :], tB[:, :],
                                                       ALU.min),
                   reads=("tC", "tB"), writes=("tC",))
            p.emit("scalar", lambda e: e.activation(s16[:, :], tC[:, :], AF.Sin),
                   reads=("tC",), writes=("s16",))
            p.emit("scalar", lambda e: e.activation(tB[:, :], tA[:, :], AF.Sign),
                   reads=("tA",), writes=("tB",))
            p.emit("vector", lambda e: e.tensor_mul(s16[:, :], s16[:, :], tB[:, :]),
                   reads=("s16", "tB"), writes=("s16",))
            p.emit("vector", lambda e: e.tensor_scalar_mul(s16[:, :], s16[:, :],
                                                           1.0 / 16),
                   reads=("s16",), writes=("s16",))

            p.barrier()
        with nc.sbuf_tensor("wq_sb", [128, 8 * E], F32) as wq_sb, \
             nc.sbuf_tensor("wk_sb", [128, 8 * E], F32) as wk_sb, \
             nc.sbuf_tensor("xb0", [128, 4 * E], F32) as xb0, \
             nc.sbuf_tensor("xb1", [128, 4 * E], F32) as xb1, \
             nc.sbuf_tensor("xnt", [128, 4 * E], F32) as xnt, \
             nc.sbuf_tensor("xnfm_sb", [128, 4 * E], F32) as xnfm_sb, \
             nc.sbuf_tensor("qh_sb", [128, 4 * E], F32) as qh_sb, \
             nc.sbuf_tensor("kh_sb", [128, 4 * E], F32) as kh_sb, \
             nc.sbuf_tensor("sc1", [128, E], F32) as sc1, \
             nc.sbuf_tensor("st_a1", [128, 8], F32) as st_a1, \
             nc.sbuf_tensor("tr1", [128, TT], F32) as tr1, \
             nc.sbuf_tensor("tr2", [128, TT], F32) as tr2:

            for kc in range(8):
                p.dma("sync", lambda e, kc=kc: e.dma_start(
                    out=wq_sb[:, kc * E:(kc + 1) * E],
                    in_=wq_d[kc * 128:(kc + 1) * 128, :]),
                    writes=("wq_sb#%d" % kc,), buf="wq_sb")
                p.dma("sync", lambda e, kc=kc: e.dma_start(
                    out=wk_sb[:, kc * E:(kc + 1) * E],
                    in_=wk_d[kc * 128:(kc + 1) * 128, :]),
                    writes=("wk_sb#%d" % kc,), buf="wk_sb")

            for s in range(SPC):
                for it in range(NTT):
                    t0 = it * TT
                    slot = (s * NTT + it) % 2
                    xb = xb0 if slot == 0 else xb1
                    xkey = "xb%d" % slot
                    for j in range(4):
                        p.dma("sync", lambda e, s=s, t0=t0, j=j, xb=xb: e.dma_start(
                            out=xb[:, j * E:(j + 1) * E],
                            in_=x_d[s, t0 + j * 128: t0 + (j + 1) * 128, :]),
                            writes=(xkey,))
                    for j in range(4):
                        xs = xb[:, j * E:(j + 1) * E]
                        xn = xnt[:, j * E:(j + 1) * E]
                        mcol = st_a1[:, j:j + 1]
                        qcol = st_a1[:, 4 + j:5 + j]
                        p.emit("vector", lambda e, xs=xs, mcol=mcol: e.reduce_sum(
                            mcol, xs, axis=AX.X), reads=(xkey,), writes=("st_a1",))
                        p.emit("scalar", lambda e, mcol=mcol: e.mul(
                            mcol, mcol, 1.0 / E), reads=("st_a1",), writes=("st_a1",))
                        p.emit("vector", lambda e, xs=xs, xn=xn, mcol=mcol:
                               e.tensor_scalar_sub(xn, xs, mcol),
                               reads=(xkey, "st_a1"), writes=("xnt",))
                        p.emit("scalar", lambda e, xn=xn, qcol=qcol: e.activation(
                            sc1[:, :], xn, AF.Square, accum_out=qcol),
                            reads=("xnt",), writes=("sc1", "st_a1"))
                        p.emit("scalar", lambda e, qcol=qcol: e.activation(
                            qcol, qcol, AF.Sqrt, scale=1.0 / E, bias=epst[:, :1]),
                            reads=("st_a1", "epst"), writes=("st_a1",))
                        p.emit("vector", lambda e, qcol=qcol: e.reciprocal(
                            qcol, qcol), reads=("st_a1",), writes=("st_a1",))
                        p.emit("vector", lambda e, xn=xn, qcol=qcol:
                               e.tensor_scalar_mul(xn, xn, qcol),
                               reads=("xnt", "st_a1"), writes=("xnt",))
                    for kc in range(8):
                        bk = 0 if kc % 2 == 0 else 5
                        fml = xnfm_sb[:, kc * TT:(kc + 1) * TT]
                        fns = []
                        for j in range(4):
                            src = xnt[:, j * E + kc * 128: j * E + (kc + 1) * 128]
                            dst = pb[bk][:, j * 128:(j + 1) * 128]
                            fns.append((lambda e, src=src, dst=dst: e.transpose(
                                dst, src, ident[:, :]), ("xnt", "ident")))
                        p.group("tensor", fns, writes=(PK[bk],))
                        p.emit("scalar", lambda e, fml=fml, bk=bk: e.copy(
                            fml, pb[bk][:, :512]), reads=(PK[bk],), writes=("xnfm_sb",))
                    p.dma("sync", lambda e, s=s, t0=t0: e.dma_start(
                        out=dram_ap(xnfm_d, SH_XN, [s, 0, 0, t0],
                                    [(2, 128), (1, 8), (3, TT)]),
                        in_=bass.AP(xnfm_sb, 0, [[4 * E, 128], [TT, 8], [1, TT]])),
                        reads=("xnfm_sb",))
                    qk_i = 0
                    for h in range(H):
                        for (wsb, wkey, osb, okey, isk) in (
                                (wq_sb, "wq_sb", qh_sb, "qh_sb", False),
                                (wk_sb, "wk_sb", kh_sb, "kh_sb", True)):
                            bko = 1 if qk_i % 2 == 0 else 3
                            qk_i += 1
                            for half in range(2):
                                dq = 2 * h + half
                                fns = []
                                for kc in range(8):
                                    lhs = wsb[:, kc * E + dq * 128:
                                              kc * E + (dq + 1) * 128]
                                    rhs = xnfm_sb[:, kc * TT:(kc + 1) * TT]
                                    dst = pb[bko + half][:, :TT]
                                    fns.append((lambda e, lhs=lhs, rhs=rhs, dst=dst,
                                                st=(kc == 0), sp=(kc == 7):
                                                e.matmul(dst, lhs, rhs,
                                                         start=st, stop=sp),
                                                (wkey + "#%d" % kc, "xnfm_sb")))
                                p.group("tensor", fns, writes=(PK[bko + half],))
                            cs = c16[:, t0:t0 + TT]
                            sn = s16[:, t0:t0 + TT]
                            o1 = osb[:, (2 * h) * TT:(2 * h) * TT + TT]
                            o2 = osb[:, (2 * h + 1) * TT:(2 * h + 1) * TT + TT]
                            pA, pB = pb[bko], pb[bko + 1]
                            kA, kB = PK[bko], PK[bko + 1]
                            p.emit("vector", lambda e, cs=cs, pA=pA: e.tensor_tensor(
                                tr1[:, :], pA[:, :TT], cs, ALU.mult),
                                reads=(kA, "c16"), writes=("tr1",))
                            p.emit("vector", lambda e, sn=sn, pB=pB: e.tensor_tensor(
                                tr2[:, :], pB[:, :TT], sn, ALU.mult),
                                reads=(kB, "s16"), writes=("tr2",))
                            p.emit("vector", lambda e, o1=o1: e.tensor_sub(
                                o1, tr1[:, :], tr2[:, :]),
                                reads=("tr1", "tr2"), writes=(okey,))
                            p.emit("vector", lambda e, cs=cs, pB=pB: e.tensor_tensor(
                                tr1[:, :], pB[:, :TT], cs, ALU.mult),
                                reads=(kB, "c16"), writes=("tr1",))
                            p.emit("vector", lambda e, sn=sn, pA=pA: e.tensor_tensor(
                                tr2[:, :], pA[:, :TT], sn, ALU.mult),
                                reads=(kA, "s16"), writes=("tr2",))
                            p.emit("vector", lambda e, o2=o2: e.tensor_add(
                                o2, tr1[:, :], tr2[:, :]),
                                reads=("tr1", "tr2"), writes=(okey,))
                            if isk:
                                p.emit("vector", lambda e, o1=o1: e.tensor_scalar_mul(
                                    o1, o1, 16.0), reads=(okey,), writes=(okey,))
                                p.emit("vector", lambda e, o2=o2: e.tensor_scalar_mul(
                                    o2, o2, 16.0), reads=(okey,), writes=(okey,))
                    p.dma("sync", lambda e, s=s, t0=t0: e.dma_start(
                        out=dram_ap(qhat_d, SH_XN, [s, 0, 0, t0],
                                    [(2, 128), (1, 8), (3, TT)]),
                        in_=bass.AP(qh_sb, 0, [[4 * E, 128], [TT, 8], [1, TT]])),
                        reads=("qh_sb",))
                    p.dma("sync", lambda e, s=s, t0=t0: e.dma_start(
                        out=dram_ap(khat_d, SH_XN, [s, 0, 0, t0],
                                    [(2, 128), (1, 8), (3, TT)]),
                        in_=bass.AP(kh_sb, 0, [[4 * E, 128], [TT, 8], [1, TT]])),
                        reads=("kh_sb",))
            p.barrier()
        a1s.close()

    if "A2" in phases:
        a2s = contextlib.ExitStack()
        gnb = a2s.enter_context(nc.sbuf_tensor("gnb", [128, 2 * E], F32, side="right"))
        dmask = a2s.enter_context(nc.sbuf_tensor("dmask", [128, H * 2 * CH], F32, side="right"))
        gam_q = a2s.enter_context(nc.sbuf_tensor("gam_q", [128, H * CH], F32, side="right"))
        gam_k = a2s.enter_context(nc.sbuf_tensor("gam_k", [128, H * 2], F32, side="right"))
        with nc.sbuf_tensor("tD2", [128, CH], F32) as tDm:
            g = "gpsimd"
            p.dma("sync", lambda e: e.dma_start(
                out=bass.AP(gnb, 0, [[2 * E, 128], [Dv, H], [1, Dv]]),
                in_=bass.AP(gn_d, 0, [[0, 128], [0, H], [1, Dv]])),
                writes=("gnb",))
            for h in range(H):
                lg = math.log(GAMMAS[h])
                for sub in range(2):
                    sl = dmask[:, (h * 2 + sub) * CH:(h * 2 + sub + 1) * CH]
                    p.emit(g, lambda e, sl=sl, sub=sub: e.iota(
                        sl, [[1, CH]], base=-128 * sub, channel_multiplier=-1,
                        allow_small_or_imprecise_dtypes=True), writes=("dmask",))
                    p.emit("vector", lambda e, sl=sl: e.tensor_single_scalar(
                        tDm[:, :CH], sl, 0.0, ALU.is_ge), reads=("dmask",), writes=("tD",))
                    p.emit("scalar", lambda e, sl=sl, lg=lg: e.activation(
                        sl, sl, AF.Exp, scale=lg), reads=("dmask",), writes=("dmask",))
                    p.emit("vector", lambda e, sl=sl: e.tensor_mul(sl, sl, tDm[:, :CH]),
                           reads=("dmask", "tD"), writes=("dmask",))
                slq = gam_q[:, h * CH:(h + 1) * CH]
                p.emit(g, lambda e, slq=slq: e.iota(
                    slq, [[1, CH]], base=1, channel_multiplier=0,
                    allow_small_or_imprecise_dtypes=True), writes=("gam_q",))
                p.emit("scalar", lambda e, slq=slq, lg=lg: e.activation(
                    slq, slq, AF.Exp, scale=lg), reads=("gam_q",), writes=("gam_q",))
                slk = gam_k[:, h * 2:(h + 1) * 2]
                p.emit(g, lambda e, slk=slk: e.iota(
                    slk, [[-128, 2]], base=CH - 1, channel_multiplier=-1,
                    allow_small_or_imprecise_dtypes=True), writes=("gam_k",))
                p.emit("scalar", lambda e, slk=slk, lg=lg: e.activation(
                    slk, slk, AF.Exp, scale=lg), reads=("gam_k",), writes=("gam_k",))

            p.barrier()
        with nc.sbuf_tensor("wv_sb", [128, 8 * 2 * E], F32) as wv_sb, \
             nc.sbuf_tensor("s_sb", [128, H * 2 * 512], F32) as s_sb, \
             nc.sbuf_tensor("xnc0", [128, 8 * CH], F32) as xnc0, \
             nc.sbuf_tensor("xnc1", [128, 8 * CH], F32) as xnc1, \
             nc.sbuf_tensor("qhc0", [128, 8 * CH], F32) as qhc0, \
             nc.sbuf_tensor("qhc1", [128, 8 * CH], F32) as qhc1, \
             nc.sbuf_tensor("khc0", [128, 8 * CH], F32) as khc0, \
             nc.sbuf_tensor("khc1", [128, 8 * CH], F32) as khc1, \
             nc.sbuf_tensor("v_sb", [128, 2 * 2 * E], F32) as v_sb, \
             nc.sbuf_tensor("on_sb", [128, 2 * 2 * E], F32) as on_sb, \
             nc.sbuf_tensor("scm", [128, 512], F32) as scm, \
             nc.sbuf_tensor("kdec", [128, 512], F32) as kdec, \
             nc.sbuf_tensor("qd_sb", [128, 512], F32) as qd_sb, \
             nc.sbuf_tensor("rst", [128, 4], F32) as rst, \
             nc.sbuf_tensor("sc2", [128, 512], F32) as sc2:

            for kc in range(8):
                p.dma("sync", lambda e, kc=kc: e.dma_start(
                    out=wv_sb[:, kc * 2 * E:(kc + 1) * 2 * E],
                    in_=wv_d[kc * 128:(kc + 1) * 128, :]),
                    writes=("wv_sb#%d" % kc,), buf="wv_sb")

            for s in range(SPC):
                p.emit("vector", lambda e: e.memset(s_sb[:, :], 0.0),
                       writes=("s_sb",))
                for c in range(NCH):
                    t0 = c * CH
                    slot = c % 2
                    xnc = xnc0 if slot == 0 else xnc1
                    qhc = qhc0 if slot == 0 else qhc1
                    khc = khc0 if slot == 0 else khc1
                    xk, qk_, kk = "xnc%d" % slot, "qhc%d" % slot, "khc%d" % slot
                    for (dst, dkey, srcd) in ((xnc, xk, xnfm_d), (qhc, qk_, qhat_d),
                                              (khc, kk, khat_d)):
                        p.dma("sync", lambda e, dst=dst, srcd=srcd, s=s, t0=t0:
                              e.dma_start(
                                  out=bass.AP(dst, 0, [[8 * CH, 128], [CH, 8], [1, CH]]),
                                  in_=dram_ap(srcd, SH_XN, [s, 0, 0, t0],
                                              [(2, 128), (1, 8), (3, CH)])),
                              writes=(dkey,))
                    for tau in range(2):
                        for dvt in range(4):
                            pbk = pb[tau]
                            pkey = PK[tau]
                            fns = []
                            for kc in range(8):
                                lhs = xnc[:, kc * CH + tau * 128: kc * CH + tau * 128 + 128]
                                rhs = wv_sb[:, kc * 2 * E + dvt * 512: kc * 2 * E + (dvt + 1) * 512]
                                fns.append((lambda e, lhs=lhs, rhs=rhs, pbk=pbk,
                                            st=(kc == 0), sp=(kc == 7): e.matmul(
                                    pbk[:, :], lhs, rhs, start=st, stop=sp),
                                    (xk, "wv_sb#%d" % kc)))
                            p.group("tensor", fns, writes=(pkey,))
                            dst = v_sb[:, tau * 2 * E + dvt * 512: tau * 2 * E + (dvt + 1) * 512]
                            p.emit("scalar", lambda e, dst=dst, pbk=pbk: e.copy(
                                dst, pbk[:, :]), reads=(pkey,), writes=("v_sb",))
                    for h in range(H):
                        sbk = 2 if h % 2 == 0 else 5
                        kbk = 5 if h % 2 == 0 else 2
                        for dcl in range(2):
                            qsl = qhc[:, (2 * h + dcl) * CH:(2 * h + dcl + 1) * CH]
                            gsl = gam_q[:, h * CH:(h + 1) * CH]
                            dst = qd_sb[:, dcl * CH:(dcl + 1) * CH]
                            p.emit("vector", lambda e, dst=dst, qsl=qsl, gsl=gsl:
                                   e.tensor_tensor(dst, qsl, gsl, ALU.mult),
                                   reads=(qk_, "gam_q"), writes=("qd_sb",))
                        fns = []
                        for dcl in range(2):
                            lhs = khc[:, (2 * h + dcl) * CH: (2 * h + dcl) * CH + 128]
                            rhs = qhc[:, (2 * h + dcl) * CH:(2 * h + dcl + 1) * CH]
                            fns.append((lambda e, lhs=lhs, rhs=rhs, sbk=sbk,
                                        st=(dcl == 0),
                                        sp=(dcl == 1): e.matmul(
                                pb[sbk][:, :CH], lhs, rhs, start=st, stop=sp),
                                (kk, qk_)))
                        p.group("tensor", fns, writes=(PK[sbk],))
                        p.emit("vector", lambda e, h=h, sbk=sbk: e.tensor_tensor(
                            scm[:, :CH], pb[sbk][:, :CH],
                            dmask[:, (2 * h) * CH:(2 * h + 1) * CH], ALU.mult),
                            reads=(PK[sbk], "dmask"), writes=("scm",))
                        fns = []
                        for dcl in range(2):
                            lhs = khc[:, (2 * h + dcl) * CH + 128: (2 * h + dcl) * CH + 256]
                            rhs = qhc[:, (2 * h + dcl) * CH + 128:(2 * h + dcl + 1) * CH]
                            fns.append((lambda e, lhs=lhs, rhs=rhs, sbk=sbk,
                                        st=(dcl == 0),
                                        sp=(dcl == 1): e.matmul(
                                pb[sbk][:, CH:CH + 128], lhs, rhs, start=st, stop=sp),
                                (kk, qk_)))
                        p.group("tensor", fns, writes=(PK[sbk],))
                        p.emit("vector", lambda e, h=h, sbk=sbk: e.tensor_tensor(
                            scm[:, CH:CH + 128], pb[sbk][:, CH:CH + 128],
                            dmask[:, (2 * h + 1) * CH + 128:(2 * h + 2) * CH], ALU.mult),
                            reads=(PK[sbk], "dmask"), writes=("scm",))
                        for tau in range(2):
                            pbk = pb[3 + tau]
                            pkey = PK[3 + tau]
                            fns = []
                            for dcl in range(2):
                                lhs = qd_sb[:, dcl * CH + tau * 128: dcl * CH + tau * 128 + 128]
                                rhs = s_sb[:, (h * 2 + dcl) * 512:(h * 2 + dcl + 1) * 512]
                                fns.append((lambda e, lhs=lhs, rhs=rhs, pbk=pbk,
                                            st=(dcl == 0): e.matmul(
                                    pbk[:, :], lhs, rhs, start=st, stop=False),
                                    ("qd_sb", "s_sb")))
                            if tau == 0:
                                lhs = scm[:, 0:128]
                                rhs = v_sb[:, 0 * 2 * E + h * 512: 0 * 2 * E + (h + 1) * 512]
                                fns.append((lambda e, lhs=lhs, rhs=rhs, pbk=pbk:
                                            e.matmul(pbk[:, :], lhs, rhs,
                                                     start=False, stop=True),
                                            ("scm", "v_sb")))
                            else:
                                lhs = scm[:, 128:256]
                                rhs = v_sb[:, 0 * 2 * E + h * 512: 0 * 2 * E + (h + 1) * 512]
                                fns.append((lambda e, lhs=lhs, rhs=rhs, pbk=pbk:
                                            e.matmul(pbk[:, :], lhs, rhs,
                                                     start=False, stop=False),
                                            ("scm", "v_sb")))
                                lhs = scm[:, CH:CH + 128]
                                rhs = v_sb[:, 1 * 2 * E + h * 512: 1 * 2 * E + (h + 1) * 512]
                                fns.append((lambda e, lhs=lhs, rhs=rhs, pbk=pbk:
                                            e.matmul(pbk[:, :], lhs, rhs,
                                                     start=False, stop=True),
                                            ("scm", "v_sb")))
                            p.group("tensor", fns, writes=(pkey,))
                            rcol = rst[:, tau:tau + 1]
                            p.emit("scalar", lambda e, pbk=pbk, rcol=rcol: e.activation(
                                sc2[:, :], pbk[:, :], AF.Square, accum_out=rcol),
                                reads=(pkey,), writes=("sc2", "rst"))
                            p.emit("scalar", lambda e, rcol=rcol: e.activation(
                                rcol, rcol, AF.Sqrt, scale=1.0 / Dv, bias=epst[:, :1]),
                                reads=("rst", "epst"), writes=("rst",))
                            p.emit("vector", lambda e, rcol=rcol: e.reciprocal(
                                rcol, rcol), reads=("rst",), writes=("rst",))
                            dst = on_sb[:, tau * 2 * E + h * 512: tau * 2 * E + (h + 1) * 512]
                            p.emit("vector", lambda e, dst=dst, pbk=pbk, rcol=rcol, h=h:
                                   e.scalar_tensor_tensor(
                                       dst, pbk[:, :], rcol,
                                       gnb[:, h * 512:(h + 1) * 512],
                                       ALU.mult, ALU.mult),
                                   reads=(pkey, "rst", "gnb"), writes=("on_sb",))
                        for tau in range(2):
                            fns = []
                            for dcl in range(2):
                                src = khc[:, (2 * h + dcl) * CH + tau * 128:
                                          (2 * h + dcl) * CH + tau * 128 + 128]
                                dst = pb[kbk][:, dcl * 128:(dcl + 1) * 128]
                                fns.append((lambda e, src=src, dst=dst: e.transpose(
                                    dst, src, ident[:, :]), (kk, "ident")))
                            p.group("tensor", fns, writes=(PK[kbk],))
                            dst = kdec[:, tau * CH:(tau + 1) * CH]
                            gsl = gam_k[:, h * 2 + tau: h * 2 + tau + 1]
                            p.emit("scalar", lambda e, dst=dst, gsl=gsl, kbk=kbk:
                                   e.activation(
                                dst, pb[kbk][:, :CH], AF.Copy, scale=gsl),
                                reads=(PK[kbk], "gam_k"), writes=("kdec",))
                        gc = float(GAMMAS[h] ** CH)
                        for dcl in range(2):
                            fns = []
                            for tau in range(2):
                                lhs = kdec[:, tau * CH + dcl * 128: tau * CH + dcl * 128 + 128]
                                rhs = v_sb[:, tau * 2 * E + h * 512: tau * 2 * E + (h + 1) * 512]
                                fns.append((lambda e, lhs=lhs, rhs=rhs, st=(tau == 0),
                                            sp=(tau == 1): e.matmul(
                                    pb[6][:, :], lhs, rhs, start=st, stop=sp),
                                    ("kdec", "v_sb")))
                            p.group("tensor", fns, writes=(PK[6],))
                            ssl = s_sb[:, (h * 2 + dcl) * 512:(h * 2 + dcl + 1) * 512]
                            p.emit("vector", lambda e, ssl=ssl, gc=gc:
                                   e.scalar_tensor_tensor(ssl, ssl, gc, pb[6][:, :],
                                                          ALU.mult, ALU.add),
                                   reads=("s_sb", PK[6]), writes=("s_sb",))
                    for tau in range(2):
                        p.dma("sync", lambda e, s=s, t0=t0, tau=tau: e.dma_start(
                            out=on_d[s, t0 + tau * 128: t0 + (tau + 1) * 128, :],
                            in_=on_sb[:, tau * 2 * E:(tau + 1) * 2 * E]),
                            reads=("on_sb",))
            p.barrier()
        a2s.close()

    if "B" in phases:
        with nc.sbuf_tensor("wgp_sb", [128, 8 * 2 * E], F32) as wgp_sb, \
             nc.sbuf_tensor("wo_sb", [128, 16 * E], F32) as wo_sb, \
             nc.sbuf_tensor("xnfb", [128, 8 * TT], F32) as xnfb, \
             nc.sbuf_tensor("onb0", [128, 2 * E], F32) as onb0, \
             nc.sbuf_tensor("onb1", [128, 2 * E], F32) as onb1, \
             nc.sbuf_tensor("xrb0", [128, E], F32) as xrb0, \
             nc.sbuf_tensor("xrb1", [128, E], F32) as xrb1, \
             nc.sbuf_tensor("g_sb", [128, 2 * E], F32) as g_sb, \
             nc.sbuf_tensor("u_sb", [128, 2 * E], F32) as u_sb, \
             nc.sbuf_tensor("ufm_sb", [128, 2 * E], F32) as ufm_sb, \
             nc.sbuf_tensor("xp_sb0", [128, E], F32) as xp_sb0, \
             nc.sbuf_tensor("xp_sb1", [128, E], F32) as xp_sb1:

            for kc in range(8):
                p.dma("sync", lambda e, kc=kc: e.dma_start(
                    out=wgp_sb[:, kc * 2 * E:(kc + 1) * 2 * E],
                    in_=wg_d[kc * 128:(kc + 1) * 128, :]),
                    writes=("wgp_sb#%d" % kc,), buf="wgp_sb")
            for fc in range(16):
                p.dma("sync", lambda e, fc=fc: e.dma_start(
                    out=wo_sb[:, fc * E:(fc + 1) * E],
                    in_=wo_d[fc * 128:(fc + 1) * 128, :]),
                    writes=("wo_sb#%d" % fc,), buf="wo_sb")

            for s in range(SPC):
                for it in range(NTT):
                    t0 = it * TT
                    p.dma("sync", lambda e, s=s, t0=t0: e.dma_start(
                        out=bass.AP(xnfb, 0, [[8 * TT, 128], [TT, 8], [1, TT]]),
                        in_=dram_ap(xnfm_d, SH_XN, [s, 0, 0, t0],
                                    [(2, 128), (1, 8), (3, TT)])),
                        writes=("xnfb",))
                    for tau in range(4):
                        tr = t0 + tau * 128
                        slot = tau % 2
                        onb = onb0 if slot == 0 else onb1
                        xrb = xrb0 if slot == 0 else xrb1
                        xpb = xp_sb0 if slot == 0 else xp_sb1
                        onk, xrk, xpk = ("onb%d" % slot, "xrb%d" % slot,
                                         "xp_sb%d" % slot)
                        p.dma("sync", lambda e, s=s, tr=tr, onb=onb: e.dma_start(
                            out=onb[:, :], in_=on_d[s, tr:tr + 128, :]),
                            writes=(onk,))
                        p.dma("sync", lambda e, s=s, tr=tr, xrb=xrb: e.dma_start(
                            out=xrb[:, :], in_=x_d[s, tr:tr + 128, :]),
                            writes=(xrk,))
                        for gvt in range(4):
                            pbk = pb[gvt % 2]
                            pkey = PK[gvt % 2]
                            fns = []
                            for kc in range(8):
                                lhs = xnfb[:, kc * TT + tau * 128:
                                           kc * TT + (tau + 1) * 128]
                                rhs = wgp_sb[:, kc * 2 * E + gvt * 512:
                                             kc * 2 * E + (gvt + 1) * 512]
                                fns.append((lambda e, lhs=lhs, rhs=rhs, pbk=pbk,
                                            st=(kc == 0), sp=(kc == 7): e.matmul(
                                    pbk[:, :], lhs, rhs, start=st, stop=sp),
                                    ("xnfb", "wgp_sb#%d" % kc)))
                            p.group("tensor", fns, writes=(pkey,))
                            dst = g_sb[:, gvt * 512:(gvt + 1) * 512]
                            p.emit("scalar", lambda e, dst=dst, pbk=pbk:
                                   e.copy(dst, pbk[:, :]),
                                   reads=(pkey,), writes=("g_sb",))
                        p.emit("vector", lambda e, onb=onb: e.tensor_tensor(
                            u_sb[:, :], onb[:, :], g_sb[:, :], ALU.mult),
                            reads=(onk, "g_sb"), writes=("u_sb",))
                        for a in range(4):
                            ubk = 2 if a % 2 == 0 else 5
                            fns = []
                            for j in range(4):
                                fcq = a * 4 + j
                                srcu = u_sb[:, fcq * 128:(fcq + 1) * 128]
                                dstp = pb[ubk][:, j * 128:(j + 1) * 128]
                                fns.append((lambda e, srcu=srcu, dstp=dstp:
                                            e.transpose(dstp, srcu, ident[:, :]),
                                            ("u_sb", "ident")))
                            p.group("tensor", fns, writes=(PK[ubk],))
                            dst = ufm_sb[:, a * 512:(a + 1) * 512]
                            p.emit("scalar", lambda e, dst=dst, ubk=ubk: e.copy(
                                dst, pb[ubk][:, :]), reads=(PK[ubk],), writes=("ufm_sb",))
                        bpair = (3, 4) if tau % 2 == 0 else (6, 2)
                        for et in range(2):
                            pbk = pb[bpair[et]]
                            pkey = PK[bpair[et]]
                            fns = []
                            for fc in range(16):
                                lhs = ufm_sb[:, fc * 128:(fc + 1) * 128]
                                rhs = wo_sb[:, fc * E + et * 512:
                                            fc * E + (et + 1) * 512]
                                fns.append((lambda e, lhs=lhs, rhs=rhs, pbk=pbk,
                                            st=(fc == 0), sp=(fc == 15): e.matmul(
                                    pbk[:, :], lhs, rhs, start=st, stop=sp),
                                    ("ufm_sb", "wo_sb#%d" % fc)))
                            p.group("tensor", fns, writes=(pkey,))
                            dst = xpb[:, et * 512:(et + 1) * 512]
                            p.emit("vector", lambda e, dst=dst, pbk=pbk, xrb=xrb,
                                   et=et: e.tensor_add(
                                dst, pbk[:, :], xrb[:, et * 512:(et + 1) * 512]),
                                reads=(pkey, xrk), writes=(xpk,))
                        p.dma("sync", lambda e, s=s, tr=tr, xpb=xpb: e.dma_start(
                            out=xp_d[s, tr:tr + 128, :], in_=xpb[:, :]),
                            reads=(xpk,))
            p.barrier()

    if "C1" in phases or "C2" in phases:
        ones1 = stack.enter_context(nc.sbuf_tensor("ones1", [128, 1], F32))
        p.emit("vector", lambda e: e.tensor_scalar_add(ones1[:, :], zeros1[:, :], 1.0),
               reads=("zeros1",), writes=("ones1",))
        smem_d = nc.dram_tensor("smem", [8], F32, kind="Internal")

    def quant_weight(wsrc_d, nkc, wid, wtern_sb, wtkey, mcb, mckey, qs):
        wfpA = qs["wfp"]
        wfpB = qs["wfpB"]
        rt1 = qs["rt1"]
        rt2 = qs["rt2"]
        acc = qs["acc"]
        pk6 = PK[6]
        p.emit("vector", lambda e: e.memset(acc[:, :1], 0.0), writes=("qacc",))
        for kc in range(nkc):
            wfp = wfpA if kc % 2 == 0 else wfpB
            wkq = "qwfp" if kc % 2 == 0 else "qwfpB"
            a_i = 1 + (kc % 2)
            p.dma("sync", lambda e, kc=kc, wfp=wfp: e.dma_start(
                out=wfp[:, :wid], in_=wsrc_d[kc * 128:(kc + 1) * 128, :]),
                writes=(wkq,))
            p.emit("vector", lambda e, wfp=wfp, a_i=a_i: e.reduce_sum(
                acc[:, a_i:a_i + 1], wfp[:, :wid], axis=AX.X,
                apply_absolute_value=True),
                reads=(wkq,), writes=("qacc",))
            p.emit("vector", lambda e, a_i=a_i: e.tensor_add(
                acc[:, :1], acc[:, :1], acc[:, a_i:a_i + 1]),
                reads=("qacc",), writes=("qacc",))
        p.group("tensor", [(lambda e: e.matmul(pb[6][:1, :1], acc[:, :1],
                                               ones1[:, :1], start=True, stop=True),
                            ("qacc", "ones1"))], writes=(pk6,))
        p.emit("vector", lambda e: e.tensor_scalar(
            acc[:1, 3:4], pb[6][:1, :1], 1.0 / (nkc * 128 * wid), 1e-5,
            ALU.mult, ALU.max), reads=(pk6,), writes=("qacc",))
        p.dma("sync", lambda e: e.dma_start(out=smem_d[0:1], in_=acc[:1, 3:4]),
              reads=("qacc",), writes=("smem",), buf="smem")
        p.dma("sync", lambda e, mcb=mcb: e.dma_start(
            out=mcb[:, :], in_=bass.AP(smem_d, 0, [[0, 128], [1, 1]])),
            reads=("smem",), writes=(mckey,), buf=mckey)
        p.emit("vector", lambda e, mcb=mcb: e.reciprocal(acc[:, 0:1], mcb[:, :]),
               reads=(mckey,), writes=("qacc",))
        for kc in range(nkc):
            wfp = wfpA if kc % 2 == 0 else wfpB
            wkq = "qwfp" if kc % 2 == 0 else "qwfpB"
            p.dma("sync", lambda e, kc=kc, wfp=wfp: e.dma_start(
                out=wfp[:, :wid], in_=wsrc_d[kc * 128:(kc + 1) * 128, :]),
                writes=(wkq,))
            vap = wfp[:, :wid]
            p.emit("vector", lambda e, vap=vap: e.tensor_scalar_mul(
                vap, vap, acc[:, 0:1]), reads=(wkq, "qacc"), writes=(wkq,))
            p.emit("vector", lambda e, vap=vap: e.tensor_single_scalar(
                rt1[:, :wid], vap, 0.5, ALU.is_gt), reads=(wkq,), writes=("qrt1",))
            p.emit("vector", lambda e, vap=vap: e.tensor_single_scalar(
                rt2[:, :wid], vap, -0.5, ALU.is_lt), reads=(wkq,), writes=("qrt2",))
            dst = wtern_sb[:, kc * wid:(kc + 1) * wid]
            p.emit("vector", lambda e, dst=dst: e.tensor_sub(
                dst, rt1[:, :wid], rt2[:, :wid]),
                reads=("qrt1", "qrt2"), writes=(wtkey,))

    if "C1" in phases:
        c1s = contextlib.ExitStack()
        wgt_sb = c1s.enter_context(nc.sbuf_tensor("wgt_sb", [128, 8 * INNER], BF16, side="right"))
        wut_sb = c1s.enter_context(nc.sbuf_tensor("wut_sb", [128, 8 * INNER], BF16, side="right"))
        mcg = c1s.enter_context(nc.sbuf_tensor("mcg", [128, 1], F32, side="right"))
        mcu = c1s.enter_context(nc.sbuf_tensor("mcu", [128, 1], F32, side="right"))
        with nc.sbuf_tensor("qwfp", [128, INNER], F32) as qwfp, \
             nc.sbuf_tensor("qwfpB", [128, INNER], F32) as qwfpB, \
             nc.sbuf_tensor("qrt1", [128, INNER], F32) as qrt1, \
             nc.sbuf_tensor("qrt2", [128, INNER], F32) as qrt2, \
             nc.sbuf_tensor("qacc", [128, 4], F32) as qacc:
            qs = {"wfp": qwfp, "wfpB": qwfpB, "rt1": qrt1, "rt2": qrt2, "acc": qacc}
            quant_weight(wgate_d, 8, INNER, wgt_sb, "wgt_sb", mcg, "mcg", qs)
            quant_weight(wup_d, 8, INNER, wut_sb, "wut_sb", mcu, "mcu", qs)
            p.barrier()

        c1l = contextlib.ExitStack()
        xpc0 = c1l.enter_context(nc.sbuf_tensor("xpc0", [128, E], F32))
        xpc1 = c1l.enter_context(nc.sbuf_tensor("xpc1", [128, E], F32))
        xn2 = c1l.enter_context(nc.sbuf_tensor("xn2", [128, E], F32))
        cst = c1l.enter_context(nc.sbuf_tensor("cst", [128, 8], F32))
        crt1 = c1l.enter_context(nc.sbuf_tensor("crt1", [128, E], F32))
        xqf = c1l.enter_context(nc.sbuf_tensor("xqf", [128, E], BF16))
        qbf = c1l.enter_context(nc.sbuf_tensor("qbf", [128, E], BF16))
        gate_sb = c1l.enter_context(nc.sbuf_tensor("gate_sb", [128, 2 * 512], F32))
        h_sb = c1l.enter_context(nc.sbuf_tensor("h_sb", [128, INNER], F32))
        hr1c = c1l.enter_context(nc.sbuf_tensor("hr1c", [128, E], F32))
        hr2c = c1l.enter_context(nc.sbuf_tensor("hr2c", [128, E], F32))
        gr1c = c1l.enter_context(nc.sbuf_tensor("gr1c", [128, E], F32))
        gr2c = c1l.enter_context(nc.sbuf_tensor("gr2c", [128, E], F32))
        gbf = c1l.enter_context(nc.sbuf_tensor("gbf", [128, E], BF16))
        gsc1 = c1l.enter_context(nc.sbuf_tensor("gsc1", [128, E], F32))
        csc1 = c1l.enter_context(nc.sbuf_tensor("csc1", [128, E], F32))
        hqt0 = c1l.enter_context(nc.sbuf_tensor("hqt0", [128, 1024], BF16))
        hqt1 = c1l.enter_context(nc.sbuf_tensor("hqt1", [128, 1024], BF16))
        if True:
            for s in range(SPC):
                for it in range(NTT):
                    for tau in range(4):
                        tr = it * TT + tau * 128
                        tcol = it * 4 + tau
                        slot = tau % 2
                        xpc = xpc0 if slot == 0 else xpc1
                        xpk = "xpc%d" % slot
                        p.dma("sync", lambda e, s=s, tr=tr, xpc=xpc: e.dma_start(
                            out=xpc[:, :], in_=xp_d[s, tr:tr + 128, :]),
                            writes=(xpk,), sem=None, buf=xpk)
                        mcol = cst[:, 0:1]
                        qcol = cst[:, 1:2]
                        p.emit("vector", lambda e, xpc=xpc, mcol=mcol: e.reduce_sum(
                            mcol, xpc[:, :], axis=AX.X), reads=(xpk,), writes=("cst",))
                        p.emit("scalar", lambda e, mcol=mcol: e.mul(
                            mcol, mcol, 1.0 / E), reads=("cst",), writes=("cst",))
                        p.emit("vector", lambda e, xpc=xpc, mcol=mcol:
                               e.tensor_scalar_sub(xn2[:, :], xpc[:, :], mcol),
                               reads=(xpk, "cst"), writes=("xn2",))
                        p.emit("scalar", lambda e, qcol=qcol: e.activation(
                            csc1[:, :], xn2[:, :], AF.Square, accum_out=qcol),
                            reads=("xn2",), writes=("csc1", "cst"))
                        p.emit("scalar", lambda e, qcol=qcol: e.activation(
                            qcol, qcol, AF.Sqrt, scale=1.0 / E, bias=epst[:, :1]),
                            reads=("cst", "epst"), writes=("cst",))
                        p.emit("vector", lambda e, qcol=qcol: e.reciprocal(
                            qcol, qcol), reads=("cst",), writes=("cst",))
                        p.emit("vector", lambda e, qcol=qcol: e.tensor_scalar_mul(
                            xn2[:, :], xn2[:, :], qcol),
                            reads=("xn2", "cst"), writes=("xn2",))
                        am = cst[:, 2:3]
                        sx = cst[:, 3:4]
                        isx = cst[:, 4:5]
                        p.emit("vector", lambda e, am=am: e.reduce_max(
                            am, xn2[:, :], axis=AX.X, apply_absolute_value=True),
                            reads=("xn2",), writes=("cst",))
                        p.emit("vector", lambda e, am=am: e.tensor_scalar_max(
                            am, am, 1e-5), reads=("cst",), writes=("cst",))
                        p.emit("vector", lambda e, am=am, sx=sx: e.reciprocal(
                            sx, am), reads=("cst",), writes=("cst",))
                        p.emit("vector", lambda e, sx=sx: e.tensor_scalar_mul(
                            sx, sx, 127.0), reads=("cst",), writes=("cst",))
                        p.emit("vector", lambda e, am=am, isx=isx: e.tensor_scalar_mul(
                            isx, am, 1.0 / 127.0), reads=("cst",), writes=("cst",))
                        p.emit("scalar", lambda e, sx=sx: e.activation(
                            xn2[:, :], xn2[:, :], AF.Copy, scale=sx),
                            reads=("xn2", "cst"), writes=("xn2",))
                        emit_round(p, crt1[:, :], csc1[:, :], xn2[:, :],
                                   "crt1", "csc1", "xn2")
                        p.emit("vector", lambda e: e.tensor_scalar(
                            qbf[:, :], crt1[:, :], -128.0, 127.0, ALU.max, ALU.min),
                            reads=("crt1",), writes=("qbf",))
                        fns = []
                        for fc in range(8):
                            srcq = qbf[:, fc * 128:(fc + 1) * 128]
                            dstq = pbb[:, fc * 128:(fc + 1) * 128]
                            fns.append((lambda e, srcq=srcq, dstq=dstq: e.transpose(
                                dstq, srcq, identb[:, :]), ("qbf", "identb")))
                        p.group("tensor", fns, writes=("pbb",))
                        p.emit("scalar", lambda e: e.copy(
                            xqf[:, :], pbb[:, :E]), reads=("pbb",), writes=("xqf",))
                        scg = cst[:, 5:6]
                        scu = cst[:, 6:7]
                        p.emit("vector", lambda e, isx=isx, scg=scg: e.tensor_tensor(
                            scg, isx, mcg[:, :], ALU.mult),
                            reads=("cst", "mcg"), writes=("cst",))
                        p.emit("vector", lambda e, isx=isx, scu=scu: e.tensor_tensor(
                            scu, isx, mcu[:, :], ALU.mult),
                            reads=("cst", "mcu"), writes=("cst",))
                        for ivt in range(8):
                            gb, ub = (0, 1) if ivt % 2 == 0 else (3, 4)
                            for (wsb2, wk2, pbk, pkey) in (
                                    (wgt_sb, "wgt_sb", pb[gb], PK[gb]),
                                    (wut_sb, "wut_sb", pb[ub], PK[ub])):
                                fns = []
                                for kc in range(8):
                                    lhs = xqf[:, kc * 128:(kc + 1) * 128]
                                    rhs = wsb2[:, kc * INNER + ivt * 512:
                                               kc * INNER + (ivt + 1) * 512]
                                    fns.append((lambda e, lhs=lhs, rhs=rhs, pbk=pbk,
                                                st=(kc == 0), sp=(kc == 7): e.matmul(
                                        pbk[:, :], lhs, rhs, start=st, stop=sp),
                                        ("xqf", wk2)))
                                p.group("tensor", fns, writes=(pkey,))
                            dstg = gate_sb[:, (ivt % 2) * 512:(ivt % 2) * 512 + 512]
                            p.emit("scalar", lambda e, dstg=dstg, gb=gb, scg=scg:
                                   e.activation(
                                dstg, pb[gb][:, :], AF.Silu, scale=scg),
                                reads=(PK[gb], "cst"),
                                writes=("gate_sb%d" % (ivt % 2),))
                            dsth = h_sb[:, ivt * 512:(ivt + 1) * 512]
                            p.emit("scalar", lambda e, dsth=dsth, ub=ub, scu=scu:
                                   e.activation(
                                dsth, pb[ub][:, :], AF.Copy, scale=scu),
                                reads=(PK[ub], "cst"), writes=("h_sb",))
                            if ivt % 2 == 1:
                                dhp = h_sb[:, (ivt - 1) * 512:(ivt + 1) * 512]
                                p.emit("vector", lambda e, dhp=dhp:
                                       e.tensor_mul(dhp, gate_sb[:, :], dhp),
                                       reads=("gate_sb0", "gate_sb1", "h_sb"),
                                       writes=("h_sb",))
                        am2 = cst[:, 2:3]
                        sx2 = cst[:, 3:4]
                        isx2 = cst[:, 7:8]
                        p.emit("vector", lambda e, am2=am2: e.reduce_max(
                            am2, h_sb[:, :], axis=AX.X, apply_absolute_value=True),
                            reads=("h_sb",), writes=("cst",))
                        p.emit("vector", lambda e, am2=am2: e.tensor_scalar_max(
                            am2, am2, 1e-5), reads=("cst",), writes=("cst",))
                        p.emit("vector", lambda e, am2=am2, sx2=sx2: e.reciprocal(
                            sx2, am2), reads=("cst",), writes=("cst",))
                        p.emit("vector", lambda e, sx2=sx2: e.tensor_scalar_mul(
                            sx2, sx2, 127.0), reads=("cst",), writes=("cst",))
                        p.emit("vector", lambda e, am2=am2, isx2=isx2:
                               e.tensor_scalar_mul(isx2, am2, 1.0 / 127.0),
                               reads=("cst",), writes=("cst",))
                        p.dma("sync", lambda e, s=s, tcol=tcol, isx2=isx2:
                              e.dma_start(
                                  out=dram_ap(hsc_d, [SPC, 128, T // 128],
                                              [s, 0, tcol], [(1, 128), (2, 1)]),
                                  in_=isx2, allow_slow_non_contiguous=True),
                              reads=("cst",), buf="hsc_st")
                        for a in range(4):
                            hsl = h_sb[:, a * E:(a + 1) * E]
                            lane = "vector"
                            r1, r2, rsc = ((hr1c, hr2c, csc1) if a % 2 == 0
                                           else (gr1c, gr2c, gsc1))
                            k1, k2, ks = (("hr1c", "hr2c", "csc1") if a % 2 == 0
                                          else ("gr1c", "gr2c", "gsc1"))
                            qb, qk2 = (qbf, "qbf") if a % 2 == 0 else (gbf, "gbf")
                            p.emit(lane, lambda e, sx2=sx2, hsl=hsl, r2=r2:
                                   e.tensor_scalar_mul(r2[:, :], hsl, sx2),
                                   reads=("h_sb", "cst"), writes=(k2,))
                            emit_round(p, r1[:, :], rsc[:, :], r2[:, :],
                                       k1, ks, k2, eng=lane)
                            p.emit(lane, lambda e, r1=r1, qb=qb: e.tensor_scalar(
                                qb[:, :], r1[:, :], -128.0, 127.0,
                                ALU.max, ALU.min), reads=(k1,), writes=(qk2,))
                            fns = []
                            for j in range(8):
                                srch = qb[:, j * 128:(j + 1) * 128]
                                dsth2 = pbb[:, j * 128:(j + 1) * 128]
                                fns.append((lambda e, srch=srch, dsth2=dsth2:
                                            e.transpose(dsth2, srch, identb[:, :]),
                                            (qk2, "identb")))
                            p.group("tensor", fns, writes=("pbb",))
                            hqt = hqt0 if a % 2 == 0 else hqt1
                            hqk = "hqt%d" % (a % 2)
                            p.emit("scalar", lambda e, hqt=hqt: e.copy(
                                hqt[:, :], pbb[:, :]), reads=("pbb",), writes=(hqk,))
                            p.dma("sync", lambda e, s=s, tr=tr, a=a, hqt=hqt:
                                  e.dma_start(
                                      out=dram_ap(hq_d, SH_HQ, [s, 8 * a, 0, tr],
                                                  [(2, 128), (1, 8), (3, 128)]),
                                      in_=bass.AP(hqt, 0, [[1024, 128], [128, 8],
                                                           [1, 128]])),
                                  reads=(hqk,), buf=hqk)
                        p.barrier()
        c1l.close()
        c1s.close()

    if "C2" in phases:
        c2s = contextlib.ExitStack()
        wdt_sb = c2s.enter_context(nc.sbuf_tensor("wdt_sb", [128, 32 * E], BF16, side="right"))
        mcd = c2s.enter_context(nc.sbuf_tensor("mcd", [128, 1], F32, side="right"))
        with nc.sbuf_tensor("qwfp2", [128, E], F32) as qwfp2, \
             nc.sbuf_tensor("qwfp2B", [128, E], F32) as qwfp2B, \
             nc.sbuf_tensor("qrt1b", [128, E], F32) as qrt1b, \
             nc.sbuf_tensor("qrt2b", [128, E], F32) as qrt2b, \
             nc.sbuf_tensor("qaccb", [128, 4], F32) as qaccb:
            qs2 = {"wfp": qwfp2, "wfpB": qwfp2B, "rt1": qrt1b, "rt2": qrt2b,
                   "acc": qaccb}
            quant_weight(wdown_d, 32, E, wdt_sb, "wdt_sb", mcd, "mcd", qs2)
            p.barrier()

        with nc.sbuf_tensor("hqc0", [128, INNER], BF16) as hqc0, \
             nc.sbuf_tensor("hqc1", [128, INNER], BF16) as hqc1, \
             nc.sbuf_tensor("xpd0", [128, E], F32) as xpd0, \
             nc.sbuf_tensor("xpd1", [128, E], F32) as xpd1, \
             nc.sbuf_tensor("oub0", [128, E], F32) as oub0, \
             nc.sbuf_tensor("oub1", [128, E], F32) as oub1, \
             nc.sbuf_tensor("dst2", [128, 2], F32) as dst2:

            for s in range(SPC):
                for it in range(NTT):
                    for tau in range(4):
                        tr = it * TT + tau * 128
                        tcol = it * 4 + tau
                        slot = tau % 2
                        hqc = hqc0 if slot == 0 else hqc1
                        xpd = xpd0 if slot == 0 else xpd1
                        oub = oub0 if slot == 0 else oub1
                        hk, xk2, ok2 = ("hqc%d" % slot, "xpd%d" % slot,
                                        "oub%d" % slot)
                        p.dma("sync", lambda e, s=s, tr=tr, hqc=hqc: e.dma_start(
                            out=bass.AP(hqc, 0, [[INNER, 128], [128, 32], [1, 128]]),
                            in_=dram_ap(hq_d, SH_HQ, [s, 0, 0, tr],
                                        [(2, 128), (1, 32), (3, 128)])),
                            writes=(hk,))
                        p.dma("sync", lambda e, s=s, tr=tr, xpd=xpd: e.dma_start(
                            out=xpd[:, :], in_=xp_d[s, tr:tr + 128, :]),
                            writes=(xk2,))
                        p.dma("sync", lambda e, s=s, tcol=tcol: e.dma_start(
                            out=dst2[:, 0:1],
                            in_=dram_ap(hsc_d, [SPC, 128, T // 128],
                                        [s, 0, tcol], [(1, 128), (2, 1)]),
                            allow_slow_non_contiguous=True),
                            writes=("dst2",))
                        p.emit("vector", lambda e: e.tensor_tensor(
                            dst2[:, 1:2], dst2[:, 0:1], mcd[:, :], ALU.mult),
                            reads=("dst2", "mcd"), writes=("dst2",))
                        cpair = (0, 1) if tau % 2 == 0 else (3, 4)
                        for et in range(2):
                            pbk = pb[cpair[et]]
                            pkey = PK[cpair[et]]
                            fns = []
                            for ic in range(32):
                                lhs = hqc[:, ic * 128:(ic + 1) * 128]
                                rhs = wdt_sb[:, ic * E + et * 512:
                                             ic * E + (et + 1) * 512]
                                fns.append((lambda e, lhs=lhs, rhs=rhs, pbk=pbk,
                                            st=(ic == 0), sp=(ic == 31): e.matmul(
                                    pbk[:, :], lhs, rhs, start=st, stop=sp),
                                    (hk, "wdt_sb")))
                            p.group("tensor", fns, writes=(pkey,))
                            dsto = oub[:, et * 512:(et + 1) * 512]
                            xsl = xpd[:, et * 512:(et + 1) * 512]
                            p.emit("vector", lambda e, dsto=dsto, pbk=pbk, xsl=xsl:
                                   e.scalar_tensor_tensor(
                                       dsto, pbk[:, :], dst2[:, 1:2], xsl,
                                       ALU.mult, ALU.add),
                                   reads=(pkey, "dst2", xk2), writes=(ok2,))
                        p.dma("sync", lambda e, s=s, tr=tr, oub=oub: e.dma_start(
                            out=out_d[s, tr:tr + 128, :], in_=oub[:, :]),
                            reads=(ok2,))
            p.barrier()
        c2s.close()

    for sname, v in list(p.cnt.items()):
        if sname.startswith("D_") and v > 0:
            p._wait("sync", sname, v)
            p._wait("gpsimd", sname, v)

    with nc.Block() as block:
        p.finalize(block)
    return nc, stack


def kernel(L0_attn_x, L0_attn_Wq, L0_attn_Wk, L0_attn_Wv, L0_attn_Wg, L0_attn_Wo,
           L0_attn_gn_weight, L0_ffn_w_gate, L0_ffn_w_up, L0_ffn_w_down):
    x = np.ascontiguousarray(np.asarray(L0_attn_x, dtype=np.float32))
    base = {
        "wq": np.ascontiguousarray(np.asarray(L0_attn_Wq, np.float32)),
        "wk": np.ascontiguousarray(np.asarray(L0_attn_Wk, np.float32)),
        "wv": np.ascontiguousarray(np.asarray(L0_attn_Wv, np.float32)),
        "wg": np.ascontiguousarray(np.asarray(L0_attn_Wg, np.float32)),
        "wo": np.ascontiguousarray(np.asarray(L0_attn_Wo, np.float32)),
        "gn": np.ascontiguousarray(np.asarray(L0_attn_gn_weight, np.float32)),
        "wgate": np.ascontiguousarray(np.asarray(L0_ffn_w_gate, np.float32).T),
        "wup": np.ascontiguousarray(np.asarray(L0_ffn_w_up, np.float32).T),
        "wdown": np.ascontiguousarray(np.asarray(L0_ffn_w_down, np.float32).T),
    }
    nc, stack = build_program()
    in_maps = [dict(base, x=np.ascontiguousarray(x[i * SPC:(i + 1) * SPC]))
               for i in range(NCORES)]
    res = run_bass_kernel_spmd(nc, in_maps, core_ids=list(range(NCORES)))
    stack.close()
    out = np.concatenate([r["out"] for r in res.results], axis=0)
    return out.reshape(B, T, E)


# version 16 (final)
# speedup vs baseline: 1.1321x; 1.1180x over previous
import math
import contextlib
import numpy as np

import concourse.bass as bass
import concourse.mybir as mybir
from concourse.bass_utils import run_bass_kernel_spmd

F32 = mybir.dt.float32
BF16 = mybir.dt.bfloat16
AF = mybir.ActivationFunctionType
ALU = mybir.AluOpType
AX = mybir.AxisListType

B, T, E, H = 32, 2048, 1024, 4
Dk, Dv = 256, 512
INNER = 4096
NCORES = 8
SPC = B // NCORES
CH = 256
NCH = T // CH
TT = 512
NTT = T // TT

C1 = float(1.5 * 2**23)
C2 = float(1.5 * 2**52)
C3 = float(1.5 * 2**32)
GAMMAS = [1.0 - 2.0 ** (-5.0 - h) for h in range(H)]


def dram_ap(td, shape, idx, dims):
    strides = [1] * len(shape)
    for i in range(len(shape) - 2, -1, -1):
        strides[i] = strides[i + 1] * shape[i + 1]
    off = sum(i * s for i, s in zip(idx, strides))
    ap = [[strides[d], c] for d, c in dims]
    return bass.AP(td, off, ap)


class Prog:

    ENGINES = ("tensor", "vector", "scalar", "gpsimd", "sync")

    def __init__(self, nc, stack):
        self.nc = nc
        self.stack = stack
        self.q = {e: [] for e in self.ENGINES}
        self.sem = {}
        self.cnt = {}
        for e in self.ENGINES:
            self.mksem(stack, "ch_" + e)
        self.lastw = {}
        self.readers = {}
        self.wm = {e: {} for e in self.ENGINES}
        self.ninstr = 0

    def mksem(self, stack, name):
        self.sem[name] = stack.enter_context(self.nc.semaphore(name))
        self.cnt[name] = 0

    def dsem(self, buf):
        name = "D_" + buf
        if name not in self.sem:
            self.mksem(self.stack, name)
        return name

    def _wait(self, eng, sname, val):
        if val <= 0 or self.wm[eng].get(sname, 0) >= val:
            return
        h = self.sem[sname]
        self.q[eng].append(lambda e, h=h, v=val: e.wait_ge(h, v))
        self.wm[eng][sname] = val
        self.ninstr += 1

    def _dep_waits(self, eng, reads, writes):
        need = {}
        for r in reads:
            w = self.lastw.get(r)
            if w:
                need[w[0]] = max(need.get(w[0], 0), w[1])
        for w in writes:
            lw = self.lastw.get(w)
            if lw:
                need[lw[0]] = max(need.get(lw[0], 0), lw[1])
            for s2, v2 in self.readers.get(w, {}).items():
                need[s2] = max(need.get(s2, 0), v2)
        for s2, v2 in need.items():
            if s2.startswith("D_"):
                v2 = self.cnt[s2]
            self._wait(eng, s2, v2)

    def _register(self, sname, val, reads, writes):
        for w in writes:
            self.lastw[w] = (sname, val)
            self.readers[w] = {}
        for r in reads:
            d = self.readers.setdefault(r, {})
            d[sname] = max(d.get(sname, 0), val)

    def emit(self, eng, fn, reads=(), writes=(), inc=1, sem=None):
        sname = sem or ("ch_" + eng)
        self._dep_waits(eng, reads, writes)
        h = self.sem[sname]
        self.cnt[sname] += inc
        val = self.cnt[sname]

        def run(e, fn=fn, h=h, inc=inc):
            fn(e).then_inc(h, inc)
        self.q[eng].append(run)
        self.ninstr += 1
        self._register(sname, val, reads, writes)
        return sname, val

    def group(self, eng, fns_reads, writes, inc=1, sem=None):
        sname = sem or ("ch_" + eng)
        all_reads = []
        n = len(fns_reads)
        for i, (gfn, greads) in enumerate(fns_reads):
            self._dep_waits(eng, greads, writes if i == 0 else ())
            all_reads += list(greads)
            if i < n - 1:
                self.q[eng].append(lambda e, f=gfn: f(e))
                self.ninstr += 1
        h = self.sem[sname]
        self.cnt[sname] += inc
        val = self.cnt[sname]
        gfn = fns_reads[-1][0]

        def run(e, fn=gfn, h=h, inc=inc):
            fn(e).then_inc(h, inc)
        self.q[eng].append(run)
        self.ninstr += 1
        self._register(sname, val, all_reads, writes)
        return sname, val

    def dma(self, eng, fn, reads=(), writes=(), sem=None, buf=None):
        if buf is None:
            buf = (list(writes) + list(reads) + ["misc"])[0]
        return self.emit(eng, fn, reads, writes, inc=16, sem=self.dsem(buf))

    def barrier(self):
        totals = {s: v for s, v in self.cnt.items() if v > 0}
        for e in self.ENGINES:
            for s, v in totals.items():
                self._wait(e, s, v)
        self.lastw = {}
        self.readers = {}

    def finalize(self, block):
        for ename in self.ENGINES:
            qq = self.q[ename]
            sec = getattr(block, ename)

            def make(qq):
                def _(eng):
                    for f in qq:
                        f(eng)
                return _
            sec(make(qq))


def emit_round(p, out_ap, tmp_ap, in_ap, okey, tkey, ikey, eng="vector"):
    M = p.mask01
    p.emit(eng, lambda e: e.tensor_scalar(out_ap, in_ap, C1, C1, ALU.add, ALU.subtract),
           reads=(ikey,), writes=(okey,))
    p.emit(eng, lambda e: e.tensor_scalar(tmp_ap, in_ap, C2, C2, ALU.add, ALU.subtract),
           reads=(ikey,), writes=(tkey,))
    p.emit(eng, lambda e: e.tensor_sub(tmp_ap, out_ap, tmp_ap),
           reads=(okey, tkey), writes=(tkey,))
    p.emit(eng, lambda e: e.tensor_scalar_mul(tmp_ap, tmp_ap, M[:, :1]),
           reads=(tkey, "mask01"), writes=(tkey,))
    p.emit(eng, lambda e: e.tensor_sub(out_ap, out_ap, tmp_ap),
           reads=(okey, tkey), writes=(okey,))


def build_program(phases=("A1", "A2", "B", "C1", "C2"), debug=False):
    nc = bass.Bass("TRN2", target_bir_lowering=False)
    stack = contextlib.ExitStack()

    x_d = nc.dram_tensor("x", [SPC, T, E], F32, kind="ExternalInput")
    wq_d = nc.dram_tensor("wq", [E, E], F32, kind="ExternalInput")
    wk_d = nc.dram_tensor("wk", [E, E], F32, kind="ExternalInput")
    wv_d = nc.dram_tensor("wv", [E, 2 * E], F32, kind="ExternalInput")
    wg_d = nc.dram_tensor("wg", [E, 2 * E], F32, kind="ExternalInput")
    wo_d = nc.dram_tensor("wo", [2 * E, E], F32, kind="ExternalInput")
    gn_d = nc.dram_tensor("gn", [Dv], F32, kind="ExternalInput")
    wgate_d = nc.dram_tensor("wgate", [E, INNER], F32, kind="ExternalInput")
    wup_d = nc.dram_tensor("wup", [E, INNER], F32, kind="ExternalInput")
    wdown_d = nc.dram_tensor("wdown", [INNER, E], F32, kind="ExternalInput")
    out_d = nc.dram_tensor("out", [SPC, T, E], F32, kind="ExternalOutput")

    skind = "ExternalOutput" if debug else "Internal"
    SH_XN = [SPC, E // 128, 128, T]
    xnfm_d = nc.dram_tensor("xnfm", SH_XN, F32, kind=skind)
    qhat_d = nc.dram_tensor("qhat", SH_XN, F32, kind=skind)
    khat_d = nc.dram_tensor("khat", SH_XN, F32, kind=skind)
    on_d = nc.dram_tensor("on", [SPC, T, 2 * E], F32, kind=skind)
    xp_d = nc.dram_tensor("xp", [SPC, T, E], F32, kind=skind)
    SH_HQ = [SPC, INNER // 128, 128, T]
    hq_d = nc.dram_tensor("hq", SH_HQ, BF16, kind=skind)
    hsc_d = nc.dram_tensor("hsc", [SPC, 128, T // 128], F32, kind=skind)

    p = Prog(nc, stack)

    ident = stack.enter_context(nc.sbuf_tensor("ident", [128, 128], F32))
    identb = stack.enter_context(nc.sbuf_tensor("identb", [128, 128], BF16))
    epst = stack.enter_context(nc.sbuf_tensor("epst", [128, 1], F32))
    zeros1 = stack.enter_context(nc.sbuf_tensor("zeros1", [128, 1], F32))
    mask01 = stack.enter_context(nc.sbuf_tensor("mask01", [128, 1], F32))
    p_mask_ref = mask01

    pb = [stack.enter_context(nc.psum_tensor(f"pb{i}", [128, 512], F32))
          for i in range(7)]
    PK = ["p%d" % i for i in range(7)]
    pbb = stack.enter_context(nc.psum_tensor("pbb", [128, 1024], BF16))

    with nc.sbuf_tensor("onesb", [128, 128], BF16) as onesb, \
         nc.sbuf_tensor("tD0", [128, 128], F32) as tD:

        g = "gpsimd"
        p.emit(g, lambda e: e.memset(zeros1[:, :], 0.0), writes=("zeros1",))
        p.emit(g, lambda e: e.memset(ident[:, :], 0.0), writes=("ident",))
        p.emit(g, lambda e: e.memset(identb[:, :], 0.0), writes=("identb",))
        p.emit(g, lambda e: e.memset(tD[:, :128], 1.0), writes=("tD",))
        p.emit(g, lambda e: e.memset(onesb[:, :], 1.0), writes=("onesb",))
        p.emit(g, lambda e: e.affine_select(
            ident[:, :], tD[:, :128], [[-1, 128]], ALU.is_equal, 0.0,
            base=0, channel_multiplier=1),
            reads=("tD",), writes=("ident",))
        p.emit(g, lambda e: e.affine_select(
            identb[:, :], onesb[:, :], [[-1, 128]], ALU.is_equal, 0.0,
            base=0, channel_multiplier=1),
            reads=("onesb",), writes=("identb",))
        p.emit("vector", lambda e: e.tensor_scalar_add(epst[:, :], zeros1[:, :], 1e-5),
               reads=("zeros1",), writes=("epst",))
        p.emit("vector", lambda e: e.tensor_scalar_add(mask01[:, :], zeros1[:, :],
                                                       8.25),
               reads=("zeros1",), writes=("mask01",))
        p.emit("vector", lambda e: e.tensor_scalar(mask01[:, :], mask01[:, :],
                                                   C2, C2, ALU.add, ALU.subtract),
               reads=("mask01",), writes=("mask01",))
        p.emit("vector", lambda e: e.tensor_scalar_mul(mask01[:, :], mask01[:, :],
                                                       0.125),
               reads=("mask01",), writes=("mask01",))
        p.mask01 = mask01

        p.barrier()

    if "A1" in phases:
        a1s = contextlib.ExitStack()
        c16 = a1s.enter_context(nc.sbuf_tensor("c16", [128, T], F32, side="right"))
        s16 = a1s.enter_context(nc.sbuf_tensor("s16", [128, T], F32, side="right"))
        with nc.sbuf_tensor("tA", [128, T], F32) as tA, \
               nc.sbuf_tensor("tB", [128, T], F32) as tB, \
               nc.sbuf_tensor("tC", [128, T], F32) as tC, \
               nc.sbuf_tensor("tD", [128, T], F32) as tD, \
               nc.sbuf_tensor("pidx", [128, 1], F32) as pidx, \
               nc.sbuf_tensor("invf", [128, 1], F32) as invf:
            g = "gpsimd"
            p.emit(g, lambda e: e.iota(pidx[:, :], [[0, 1]], base=0, channel_multiplier=1,
                                       allow_small_or_imprecise_dtypes=True),
                   writes=("pidx",))
            p.emit("vector", lambda e: e.tensor_scalar_add(invf[:, :], zeros1[:, :], 1.0),
                   reads=("zeros1",), writes=("invf",))
            p.emit("vector", lambda e: e.tensor_copy(tB[:, :1], pidx[:, :]),
                   reads=("pidx",), writes=("tB",))
            for j in range(6, -1, -1):
                rj = 10000.0 ** (-(2.0 ** j) / 128.0)
                bit = float(2 ** j)
                p.emit("vector", lambda e, bit=bit: e.tensor_single_scalar(
                    tC[:, :1], tB[:, :1], bit, ALU.is_ge), reads=("tB",), writes=("tC",))
                p.emit("vector", lambda e, bit=bit: e.tensor_scalar(
                    tD[:, :1], tC[:, :1], bit, None, ALU.mult), reads=("tC",), writes=("tD",))
                p.emit("vector", lambda e: e.tensor_sub(tB[:, :1], tB[:, :1], tD[:, :1]),
                       reads=("tB", "tD"), writes=("tB",))
                p.emit("vector", lambda e, rj=rj: e.tensor_scalar(
                    tC[:, :1], tC[:, :1], rj - 1.0, 1.0, ALU.mult, ALU.add),
                    reads=("tC",), writes=("tC",))
                p.emit("vector", lambda e: e.tensor_mul(invf[:, :], invf[:, :], tC[:, :1]),
                       reads=("invf", "tC"), writes=("invf",))
            p.emit(g, lambda e: e.iota(tA[:, :], [[1, T]], base=0, channel_multiplier=0,
                                       allow_small_or_imprecise_dtypes=True),
                   writes=("tA",))
            p.emit("vector", lambda e: e.tensor_scalar_mul(tA[:, :], tA[:, :], invf[:, :1]),
                   reads=("tA", "invf"), writes=("tA",))
            p.emit("vector", lambda e: e.tensor_scalar(
                tC[:, :], tA[:, :], 1.0 / (2 * math.pi), 0.5, ALU.mult, ALU.add),
                reads=("tA",), writes=("tC",))
            p.emit("vector", lambda e: e.memset(tB[:, :], 0.0), writes=("tB",))
            for j in range(8, -1, -1):
                v = float(2 ** j)
                p.emit("vector", lambda e, v=v: e.tensor_single_scalar(
                    tD[:, :], tC[:, :], v, ALU.is_ge), reads=("tC",), writes=("tD",))
                p.emit("vector", lambda e, v=v: e.tensor_scalar_mul(tD[:, :], tD[:, :], v),
                       reads=("tD",), writes=("tD",))
                p.emit("vector", lambda e: e.tensor_sub(tC[:, :], tC[:, :], tD[:, :]),
                       reads=("tC", "tD"), writes=("tC",))
                p.emit("vector", lambda e: e.tensor_add(tB[:, :], tB[:, :], tD[:, :]),
                       reads=("tB", "tD"), writes=("tB",))
            p.emit("vector", lambda e: e.tensor_scalar_mul(tB[:, :], tB[:, :],
                                                           2 * math.pi),
                   reads=("tB",), writes=("tB",))
            p.emit("vector", lambda e: e.tensor_sub(tA[:, :], tA[:, :], tB[:, :]),
                   reads=("tA", "tB"), writes=("tA",))
            p.emit("scalar", lambda e: e.activation(tC[:, :], tA[:, :], AF.Abs),
                   reads=("tA",), writes=("tC",))
            p.emit("vector", lambda e: e.tensor_scalar_add(tD[:, :1], zeros1[:, :],
                                                           math.pi / 2),
                   reads=("zeros1",), writes=("tD",))
            p.emit("scalar", lambda e: e.activation(c16[:, :], tC[:, :], AF.Sin,
                                                    scale=-1.0, bias=tD[:, :1]),
                   reads=("tC", "tD"), writes=("c16",))
            p.emit("vector", lambda e: e.tensor_scalar_mul(c16[:, :], c16[:, :],
                                                           1.0 / 16),
                   reads=("c16",), writes=("c16",))
            p.emit("vector", lambda e: e.tensor_scalar(
                tB[:, :], tC[:, :], -1.0, math.pi, ALU.mult, ALU.add),
                reads=("tC",), writes=("tB",))
            p.emit("vector", lambda e: e.tensor_tensor(tC[:, :], tC[:, :], tB[:, :],
                                                       ALU.min),
                   reads=("tC", "tB"), writes=("tC",))
            p.emit("scalar", lambda e: e.activation(s16[:, :], tC[:, :], AF.Sin),
                   reads=("tC",), writes=("s16",))
            p.emit("scalar", lambda e: e.activation(tB[:, :], tA[:, :], AF.Sign),
                   reads=("tA",), writes=("tB",))
            p.emit("vector", lambda e: e.tensor_mul(s16[:, :], s16[:, :], tB[:, :]),
                   reads=("s16", "tB"), writes=("s16",))
            p.emit("vector", lambda e: e.tensor_scalar_mul(s16[:, :], s16[:, :],
                                                           1.0 / 16),
                   reads=("s16",), writes=("s16",))

            p.barrier()
        with nc.sbuf_tensor("wq_sb", [128, 8 * E], F32) as wq_sb, \
             nc.sbuf_tensor("wk_sb", [128, 8 * E], F32) as wk_sb, \
             nc.sbuf_tensor("xb0", [128, 4 * E], F32) as xb0, \
             nc.sbuf_tensor("xb1", [128, 4 * E], F32) as xb1, \
             nc.sbuf_tensor("xnt", [128, 4 * E], F32) as xnt, \
             nc.sbuf_tensor("xnfm_sb", [128, 4 * E], F32) as xnfm_sb, \
             nc.sbuf_tensor("qh_sb", [128, 4 * E], F32) as qh_sb, \
             nc.sbuf_tensor("kh_sb", [128, 4 * E], F32) as kh_sb, \
             nc.sbuf_tensor("sc1", [128, E], F32) as sc1, \
             nc.sbuf_tensor("st_a1", [128, 8], F32) as st_a1, \
             nc.sbuf_tensor("tr1", [128, TT], F32) as tr1, \
             nc.sbuf_tensor("tr2", [128, TT], F32) as tr2:

            for kc in range(8):
                p.dma("sync", lambda e, kc=kc: e.dma_start(
                    out=wq_sb[:, kc * E:(kc + 1) * E],
                    in_=wq_d[kc * 128:(kc + 1) * 128, :]),
                    writes=("wq_sb",))
                p.dma("sync", lambda e, kc=kc: e.dma_start(
                    out=wk_sb[:, kc * E:(kc + 1) * E],
                    in_=wk_d[kc * 128:(kc + 1) * 128, :]),
                    writes=("wk_sb",))

            for s in range(SPC):
                for it in range(NTT):
                    t0 = it * TT
                    slot = (s * NTT + it) % 2
                    xb = xb0 if slot == 0 else xb1
                    xkey = "xb%d" % slot
                    for j in range(4):
                        p.dma("sync", lambda e, s=s, t0=t0, j=j, xb=xb: e.dma_start(
                            out=xb[:, j * E:(j + 1) * E],
                            in_=x_d[s, t0 + j * 128: t0 + (j + 1) * 128, :]),
                            writes=(xkey,))
                    for j in range(4):
                        xs = xb[:, j * E:(j + 1) * E]
                        xn = xnt[:, j * E:(j + 1) * E]
                        mcol = st_a1[:, j:j + 1]
                        qcol = st_a1[:, 4 + j:5 + j]
                        p.emit("vector", lambda e, xs=xs, mcol=mcol: e.reduce_sum(
                            mcol, xs, axis=AX.X), reads=(xkey,), writes=("st_a1",))
                        p.emit("scalar", lambda e, mcol=mcol: e.mul(
                            mcol, mcol, 1.0 / E), reads=("st_a1",), writes=("st_a1",))
                        p.emit("vector", lambda e, xs=xs, xn=xn, mcol=mcol:
                               e.tensor_scalar_sub(xn, xs, mcol),
                               reads=(xkey, "st_a1"), writes=("xnt",))
                        p.emit("scalar", lambda e, xn=xn, qcol=qcol: e.activation(
                            sc1[:, :], xn, AF.Square, accum_out=qcol),
                            reads=("xnt",), writes=("sc1", "st_a1"))
                        p.emit("scalar", lambda e, qcol=qcol: e.activation(
                            qcol, qcol, AF.Sqrt, scale=1.0 / E, bias=epst[:, :1]),
                            reads=("st_a1", "epst"), writes=("st_a1",))
                        p.emit("vector", lambda e, qcol=qcol: e.reciprocal(
                            qcol, qcol), reads=("st_a1",), writes=("st_a1",))
                        p.emit("vector", lambda e, xn=xn, qcol=qcol:
                               e.tensor_scalar_mul(xn, xn, qcol),
                               reads=("xnt", "st_a1"), writes=("xnt",))
                    for kc in range(8):
                        bk = 0 if kc % 2 == 0 else 5
                        fml = xnfm_sb[:, kc * TT:(kc + 1) * TT]
                        fns = []
                        for j in range(4):
                            src = xnt[:, j * E + kc * 128: j * E + (kc + 1) * 128]
                            dst = pb[bk][:, j * 128:(j + 1) * 128]
                            fns.append((lambda e, src=src, dst=dst: e.transpose(
                                dst, src, ident[:, :]), ("xnt", "ident")))
                        p.group("tensor", fns, writes=(PK[bk],))
                        p.emit("scalar", lambda e, fml=fml, bk=bk: e.copy(
                            fml, pb[bk][:, :512]), reads=(PK[bk],), writes=("xnfm_sb",))
                    p.dma("gpsimd", lambda e, s=s, t0=t0: e.dma_start(
                        out=dram_ap(xnfm_d, SH_XN, [s, 0, 0, t0],
                                    [(2, 128), (1, 8), (3, TT)]),
                        in_=bass.AP(xnfm_sb, 0, [[4 * E, 128], [TT, 8], [1, TT]])),
                        reads=("xnfm_sb",))
                    qk_i = 0
                    for h in range(H):
                        for (wsb, wkey, osb, okey, isk) in (
                                (wq_sb, "wq_sb", qh_sb, "qh_sb", False),
                                (wk_sb, "wk_sb", kh_sb, "kh_sb", True)):
                            bko = 1 if qk_i % 2 == 0 else 3
                            qk_i += 1
                            for half in range(2):
                                dq = 2 * h + half
                                fns = []
                                for kc in range(8):
                                    lhs = wsb[:, kc * E + dq * 128:
                                              kc * E + (dq + 1) * 128]
                                    rhs = xnfm_sb[:, kc * TT:(kc + 1) * TT]
                                    dst = pb[bko + half][:, :TT]
                                    fns.append((lambda e, lhs=lhs, rhs=rhs, dst=dst,
                                                st=(kc == 0), sp=(kc == 7):
                                                e.matmul(dst, lhs, rhs,
                                                         start=st, stop=sp),
                                                (wkey, "xnfm_sb")))
                                p.group("tensor", fns, writes=(PK[bko + half],))
                            cs = c16[:, t0:t0 + TT]
                            sn = s16[:, t0:t0 + TT]
                            o1 = osb[:, (2 * h) * TT:(2 * h) * TT + TT]
                            o2 = osb[:, (2 * h + 1) * TT:(2 * h + 1) * TT + TT]
                            pA, pB = pb[bko], pb[bko + 1]
                            kA, kB = PK[bko], PK[bko + 1]
                            p.emit("vector", lambda e, cs=cs, pA=pA: e.tensor_tensor(
                                tr1[:, :], pA[:, :TT], cs, ALU.mult),
                                reads=(kA, "c16"), writes=("tr1",))
                            p.emit("vector", lambda e, sn=sn, pB=pB: e.tensor_tensor(
                                tr2[:, :], pB[:, :TT], sn, ALU.mult),
                                reads=(kB, "s16"), writes=("tr2",))
                            p.emit("vector", lambda e, o1=o1: e.tensor_sub(
                                o1, tr1[:, :], tr2[:, :]),
                                reads=("tr1", "tr2"), writes=(okey,))
                            p.emit("vector", lambda e, cs=cs, pB=pB: e.tensor_tensor(
                                tr1[:, :], pB[:, :TT], cs, ALU.mult),
                                reads=(kB, "c16"), writes=("tr1",))
                            p.emit("vector", lambda e, sn=sn, pA=pA: e.tensor_tensor(
                                tr2[:, :], pA[:, :TT], sn, ALU.mult),
                                reads=(kA, "s16"), writes=("tr2",))
                            p.emit("vector", lambda e, o2=o2: e.tensor_add(
                                o2, tr1[:, :], tr2[:, :]),
                                reads=("tr1", "tr2"), writes=(okey,))
                            if isk:
                                p.emit("vector", lambda e, o1=o1: e.tensor_scalar_mul(
                                    o1, o1, 16.0), reads=(okey,), writes=(okey,))
                                p.emit("vector", lambda e, o2=o2: e.tensor_scalar_mul(
                                    o2, o2, 16.0), reads=(okey,), writes=(okey,))
                    p.dma("gpsimd", lambda e, s=s, t0=t0: e.dma_start(
                        out=dram_ap(qhat_d, SH_XN, [s, 0, 0, t0],
                                    [(2, 128), (1, 8), (3, TT)]),
                        in_=bass.AP(qh_sb, 0, [[4 * E, 128], [TT, 8], [1, TT]])),
                        reads=("qh_sb",))
                    p.dma("gpsimd", lambda e, s=s, t0=t0: e.dma_start(
                        out=dram_ap(khat_d, SH_XN, [s, 0, 0, t0],
                                    [(2, 128), (1, 8), (3, TT)]),
                        in_=bass.AP(kh_sb, 0, [[4 * E, 128], [TT, 8], [1, TT]])),
                        reads=("kh_sb",))
            p.barrier()
        a1s.close()

    if "A2" in phases:
        a2s = contextlib.ExitStack()
        gnb = a2s.enter_context(nc.sbuf_tensor("gnb", [128, 2 * E], F32, side="right"))
        dmask = a2s.enter_context(nc.sbuf_tensor("dmask", [128, H * 2 * CH], F32, side="right"))
        gam_q = a2s.enter_context(nc.sbuf_tensor("gam_q", [128, H * CH], F32, side="right"))
        gam_k = a2s.enter_context(nc.sbuf_tensor("gam_k", [128, H * 2], F32, side="right"))
        with nc.sbuf_tensor("tD2", [128, CH], F32) as tDm:
            g = "gpsimd"
            p.dma("sync", lambda e: e.dma_start(
                out=bass.AP(gnb, 0, [[2 * E, 128], [Dv, H], [1, Dv]]),
                in_=bass.AP(gn_d, 0, [[0, 128], [0, H], [1, Dv]])),
                writes=("gnb",))
            for h in range(H):
                lg = math.log(GAMMAS[h])
                for sub in range(2):
                    sl = dmask[:, (h * 2 + sub) * CH:(h * 2 + sub + 1) * CH]
                    p.emit(g, lambda e, sl=sl, sub=sub: e.iota(
                        sl, [[1, CH]], base=-128 * sub, channel_multiplier=-1,
                        allow_small_or_imprecise_dtypes=True), writes=("dmask",))
                    p.emit("vector", lambda e, sl=sl: e.tensor_single_scalar(
                        tDm[:, :CH], sl, 0.0, ALU.is_ge), reads=("dmask",), writes=("tD",))
                    p.emit("scalar", lambda e, sl=sl, lg=lg: e.activation(
                        sl, sl, AF.Exp, scale=lg), reads=("dmask",), writes=("dmask",))
                    p.emit("vector", lambda e, sl=sl: e.tensor_mul(sl, sl, tDm[:, :CH]),
                           reads=("dmask", "tD"), writes=("dmask",))
                slq = gam_q[:, h * CH:(h + 1) * CH]
                p.emit(g, lambda e, slq=slq: e.iota(
                    slq, [[1, CH]], base=1, channel_multiplier=0,
                    allow_small_or_imprecise_dtypes=True), writes=("gam_q",))
                p.emit("scalar", lambda e, slq=slq, lg=lg: e.activation(
                    slq, slq, AF.Exp, scale=lg), reads=("gam_q",), writes=("gam_q",))
                slk = gam_k[:, h * 2:(h + 1) * 2]
                p.emit(g, lambda e, slk=slk: e.iota(
                    slk, [[-128, 2]], base=CH - 1, channel_multiplier=-1,
                    allow_small_or_imprecise_dtypes=True), writes=("gam_k",))
                p.emit("scalar", lambda e, slk=slk, lg=lg: e.activation(
                    slk, slk, AF.Exp, scale=lg), reads=("gam_k",), writes=("gam_k",))

            p.barrier()
        with nc.sbuf_tensor("wv_sb", [128, 8 * 2 * E], F32) as wv_sb, \
             nc.sbuf_tensor("s_sb", [128, H * 2 * 512], F32) as s_sb, \
             nc.sbuf_tensor("xnc0", [128, 8 * CH], F32) as xnc0, \
             nc.sbuf_tensor("xnc1", [128, 8 * CH], F32) as xnc1, \
             nc.sbuf_tensor("qhc0", [128, 8 * CH], F32) as qhc0, \
             nc.sbuf_tensor("qhc1", [128, 8 * CH], F32) as qhc1, \
             nc.sbuf_tensor("khc0", [128, 8 * CH], F32) as khc0, \
             nc.sbuf_tensor("khc1", [128, 8 * CH], F32) as khc1, \
             nc.sbuf_tensor("v_sb", [128, 2 * 2 * E], F32) as v_sb, \
             nc.sbuf_tensor("on_sb", [128, 2 * 2 * E], F32) as on_sb, \
             nc.sbuf_tensor("scm", [128, 512], F32) as scm, \
             nc.sbuf_tensor("kdec", [128, 512], F32) as kdec, \
             nc.sbuf_tensor("qd_sb", [128, 512], F32) as qd_sb, \
             nc.sbuf_tensor("rst", [128, 4], F32) as rst, \
             nc.sbuf_tensor("sc2", [128, 512], F32) as sc2:

            for kc in range(8):
                p.dma("sync", lambda e, kc=kc: e.dma_start(
                    out=wv_sb[:, kc * 2 * E:(kc + 1) * 2 * E],
                    in_=wv_d[kc * 128:(kc + 1) * 128, :]),
                    writes=("wv_sb",))

            for s in range(SPC):
                p.emit("vector", lambda e: e.memset(s_sb[:, :], 0.0),
                       writes=("s_sb",))
                for c in range(NCH):
                    t0 = c * CH
                    slot = c % 2
                    xnc = xnc0 if slot == 0 else xnc1
                    qhc = qhc0 if slot == 0 else qhc1
                    khc = khc0 if slot == 0 else khc1
                    xk, qk_, kk = "xnc%d" % slot, "qhc%d" % slot, "khc%d" % slot
                    for (dst, dkey, srcd) in ((xnc, xk, xnfm_d), (qhc, qk_, qhat_d),
                                              (khc, kk, khat_d)):
                        p.dma("sync", lambda e, dst=dst, srcd=srcd, s=s, t0=t0:
                              e.dma_start(
                                  out=bass.AP(dst, 0, [[8 * CH, 128], [CH, 8], [1, CH]]),
                                  in_=dram_ap(srcd, SH_XN, [s, 0, 0, t0],
                                              [(2, 128), (1, 8), (3, CH)])),
                              writes=(dkey,))
                    for tau in range(2):
                        for dvt in range(4):
                            pbk = pb[tau]
                            pkey = PK[tau]
                            fns = []
                            for kc in range(8):
                                lhs = xnc[:, kc * CH + tau * 128: kc * CH + tau * 128 + 128]
                                rhs = wv_sb[:, kc * 2 * E + dvt * 512: kc * 2 * E + (dvt + 1) * 512]
                                fns.append((lambda e, lhs=lhs, rhs=rhs, pbk=pbk,
                                            st=(kc == 0), sp=(kc == 7): e.matmul(
                                    pbk[:, :], lhs, rhs, start=st, stop=sp),
                                    (xk, "wv_sb")))
                            p.group("tensor", fns, writes=(pkey,))
                            dst = v_sb[:, tau * 2 * E + dvt * 512: tau * 2 * E + (dvt + 1) * 512]
                            p.emit("scalar", lambda e, dst=dst, pbk=pbk: e.copy(
                                dst, pbk[:, :]), reads=(pkey,), writes=("v_sb",))
                    for h in range(H):
                        sbk = 2 if h % 2 == 0 else 5
                        kbk = 5 if h % 2 == 0 else 2
                        for dcl in range(2):
                            qsl = qhc[:, (2 * h + dcl) * CH:(2 * h + dcl + 1) * CH]
                            gsl = gam_q[:, h * CH:(h + 1) * CH]
                            dst = qd_sb[:, dcl * CH:(dcl + 1) * CH]
                            p.emit("vector", lambda e, dst=dst, qsl=qsl, gsl=gsl:
                                   e.tensor_tensor(dst, qsl, gsl, ALU.mult),
                                   reads=(qk_, "gam_q"), writes=("qd_sb",))
                        fns = []
                        for dcl in range(2):
                            lhs = khc[:, (2 * h + dcl) * CH: (2 * h + dcl) * CH + 128]
                            rhs = qhc[:, (2 * h + dcl) * CH:(2 * h + dcl + 1) * CH]
                            fns.append((lambda e, lhs=lhs, rhs=rhs, sbk=sbk,
                                        st=(dcl == 0),
                                        sp=(dcl == 1): e.matmul(
                                pb[sbk][:, :CH], lhs, rhs, start=st, stop=sp),
                                (kk, qk_)))
                        p.group("tensor", fns, writes=(PK[sbk],))
                        p.emit("vector", lambda e, h=h, sbk=sbk: e.tensor_tensor(
                            scm[:, :CH], pb[sbk][:, :CH],
                            dmask[:, (2 * h) * CH:(2 * h + 1) * CH], ALU.mult),
                            reads=(PK[sbk], "dmask"), writes=("scm",))
                        fns = []
                        for dcl in range(2):
                            lhs = khc[:, (2 * h + dcl) * CH + 128: (2 * h + dcl) * CH + 256]
                            rhs = qhc[:, (2 * h + dcl) * CH + 128:(2 * h + dcl + 1) * CH]
                            fns.append((lambda e, lhs=lhs, rhs=rhs, sbk=sbk,
                                        st=(dcl == 0),
                                        sp=(dcl == 1): e.matmul(
                                pb[sbk][:, CH:CH + 128], lhs, rhs, start=st, stop=sp),
                                (kk, qk_)))
                        p.group("tensor", fns, writes=(PK[sbk],))
                        p.emit("vector", lambda e, h=h, sbk=sbk: e.tensor_tensor(
                            scm[:, CH:CH + 128], pb[sbk][:, CH:CH + 128],
                            dmask[:, (2 * h + 1) * CH + 128:(2 * h + 2) * CH], ALU.mult),
                            reads=(PK[sbk], "dmask"), writes=("scm",))
                        for tau in range(2):
                            pbk = pb[3 + tau]
                            pkey = PK[3 + tau]
                            fns = []
                            for dcl in range(2):
                                lhs = qd_sb[:, dcl * CH + tau * 128: dcl * CH + tau * 128 + 128]
                                rhs = s_sb[:, (h * 2 + dcl) * 512:(h * 2 + dcl + 1) * 512]
                                fns.append((lambda e, lhs=lhs, rhs=rhs, pbk=pbk,
                                            st=(dcl == 0): e.matmul(
                                    pbk[:, :], lhs, rhs, start=st, stop=False),
                                    ("qd_sb", "s_sb")))
                            if tau == 0:
                                lhs = scm[:, 0:128]
                                rhs = v_sb[:, 0 * 2 * E + h * 512: 0 * 2 * E + (h + 1) * 512]
                                fns.append((lambda e, lhs=lhs, rhs=rhs, pbk=pbk:
                                            e.matmul(pbk[:, :], lhs, rhs,
                                                     start=False, stop=True),
                                            ("scm", "v_sb")))
                            else:
                                lhs = scm[:, 128:256]
                                rhs = v_sb[:, 0 * 2 * E + h * 512: 0 * 2 * E + (h + 1) * 512]
                                fns.append((lambda e, lhs=lhs, rhs=rhs, pbk=pbk:
                                            e.matmul(pbk[:, :], lhs, rhs,
                                                     start=False, stop=False),
                                            ("scm", "v_sb")))
                                lhs = scm[:, CH:CH + 128]
                                rhs = v_sb[:, 1 * 2 * E + h * 512: 1 * 2 * E + (h + 1) * 512]
                                fns.append((lambda e, lhs=lhs, rhs=rhs, pbk=pbk:
                                            e.matmul(pbk[:, :], lhs, rhs,
                                                     start=False, stop=True),
                                            ("scm", "v_sb")))
                            p.group("tensor", fns, writes=(pkey,))
                            rcol = rst[:, tau:tau + 1]
                            p.emit("scalar", lambda e, pbk=pbk, rcol=rcol: e.activation(
                                sc2[:, :], pbk[:, :], AF.Square, accum_out=rcol),
                                reads=(pkey,), writes=("sc2", "rst"))
                            p.emit("scalar", lambda e, rcol=rcol: e.activation(
                                rcol, rcol, AF.Sqrt, scale=1.0 / Dv, bias=epst[:, :1]),
                                reads=("rst", "epst"), writes=("rst",))
                            p.emit("vector", lambda e, rcol=rcol: e.reciprocal(
                                rcol, rcol), reads=("rst",), writes=("rst",))
                            dst = on_sb[:, tau * 2 * E + h * 512: tau * 2 * E + (h + 1) * 512]
                            p.emit("vector", lambda e, dst=dst, pbk=pbk, rcol=rcol, h=h:
                                   e.scalar_tensor_tensor(
                                       dst, pbk[:, :], rcol,
                                       gnb[:, h * 512:(h + 1) * 512],
                                       ALU.mult, ALU.mult),
                                   reads=(pkey, "rst", "gnb"), writes=("on_sb",))
                        for tau in range(2):
                            fns = []
                            for dcl in range(2):
                                src = khc[:, (2 * h + dcl) * CH + tau * 128:
                                          (2 * h + dcl) * CH + tau * 128 + 128]
                                dst = pb[kbk][:, dcl * 128:(dcl + 1) * 128]
                                fns.append((lambda e, src=src, dst=dst: e.transpose(
                                    dst, src, ident[:, :]), (kk, "ident")))
                            p.group("tensor", fns, writes=(PK[kbk],))
                            dst = kdec[:, tau * CH:(tau + 1) * CH]
                            gsl = gam_k[:, h * 2 + tau: h * 2 + tau + 1]
                            p.emit("scalar", lambda e, dst=dst, gsl=gsl, kbk=kbk:
                                   e.activation(
                                dst, pb[kbk][:, :CH], AF.Copy, scale=gsl),
                                reads=(PK[kbk], "gam_k"), writes=("kdec",))
                        gc = float(GAMMAS[h] ** CH)
                        for dcl in range(2):
                            fns = []
                            for tau in range(2):
                                lhs = kdec[:, tau * CH + dcl * 128: tau * CH + dcl * 128 + 128]
                                rhs = v_sb[:, tau * 2 * E + h * 512: tau * 2 * E + (h + 1) * 512]
                                fns.append((lambda e, lhs=lhs, rhs=rhs, st=(tau == 0),
                                            sp=(tau == 1): e.matmul(
                                    pb[6][:, :], lhs, rhs, start=st, stop=sp),
                                    ("kdec", "v_sb")))
                            p.group("tensor", fns, writes=(PK[6],))
                            ssl = s_sb[:, (h * 2 + dcl) * 512:(h * 2 + dcl + 1) * 512]
                            p.emit("vector", lambda e, ssl=ssl, gc=gc:
                                   e.scalar_tensor_tensor(ssl, ssl, gc, pb[6][:, :],
                                                          ALU.mult, ALU.add),
                                   reads=("s_sb", PK[6]), writes=("s_sb",))
                    for tau in range(2):
                        p.dma("gpsimd", lambda e, s=s, t0=t0, tau=tau: e.dma_start(
                            out=on_d[s, t0 + tau * 128: t0 + (tau + 1) * 128, :],
                            in_=on_sb[:, tau * 2 * E:(tau + 1) * 2 * E]),
                            reads=("on_sb",))
            p.barrier()
        a2s.close()

    if "B" in phases:
        with nc.sbuf_tensor("wgp_sb", [128, 8 * 2 * E], F32) as wgp_sb, \
             nc.sbuf_tensor("wo_sb", [128, 16 * E], F32) as wo_sb, \
             nc.sbuf_tensor("xnfb", [128, 8 * TT], F32) as xnfb, \
             nc.sbuf_tensor("onb0", [128, 2 * E], F32) as onb0, \
             nc.sbuf_tensor("onb1", [128, 2 * E], F32) as onb1, \
             nc.sbuf_tensor("xrb0", [128, E], F32) as xrb0, \
             nc.sbuf_tensor("xrb1", [128, E], F32) as xrb1, \
             nc.sbuf_tensor("g_sb", [128, 2 * E], F32) as g_sb, \
             nc.sbuf_tensor("u_sb", [128, 2 * E], F32) as u_sb, \
             nc.sbuf_tensor("ufm_sb", [128, 2 * E], F32) as ufm_sb, \
             nc.sbuf_tensor("xp_sb0", [128, E], F32) as xp_sb0, \
             nc.sbuf_tensor("xp_sb1", [128, E], F32) as xp_sb1:

            for kc in range(8):
                p.dma("sync", lambda e, kc=kc: e.dma_start(
                    out=wgp_sb[:, kc * 2 * E:(kc + 1) * 2 * E],
                    in_=wg_d[kc * 128:(kc + 1) * 128, :]),
                    writes=("wgp_sb",))
            for fc in range(16):
                p.dma("sync", lambda e, fc=fc: e.dma_start(
                    out=wo_sb[:, fc * E:(fc + 1) * E],
                    in_=wo_d[fc * 128:(fc + 1) * 128, :]),
                    writes=("wo_sb",))

            for s in range(SPC):
                for it in range(NTT):
                    t0 = it * TT
                    p.dma("sync", lambda e, s=s, t0=t0: e.dma_start(
                        out=bass.AP(xnfb, 0, [[8 * TT, 128], [TT, 8], [1, TT]]),
                        in_=dram_ap(xnfm_d, SH_XN, [s, 0, 0, t0],
                                    [(2, 128), (1, 8), (3, TT)])),
                        writes=("xnfb",))
                    for tau in range(4):
                        tr = t0 + tau * 128
                        slot = tau % 2
                        onb = onb0 if slot == 0 else onb1
                        xrb = xrb0 if slot == 0 else xrb1
                        xpb = xp_sb0 if slot == 0 else xp_sb1
                        onk, xrk, xpk = ("onb%d" % slot, "xrb%d" % slot,
                                         "xp_sb%d" % slot)
                        p.dma("sync", lambda e, s=s, tr=tr, onb=onb: e.dma_start(
                            out=onb[:, :], in_=on_d[s, tr:tr + 128, :]),
                            writes=(onk,))
                        p.dma("sync", lambda e, s=s, tr=tr, xrb=xrb: e.dma_start(
                            out=xrb[:, :], in_=x_d[s, tr:tr + 128, :]),
                            writes=(xrk,))
                        for gvt in range(4):
                            pbk = pb[gvt % 2]
                            pkey = PK[gvt % 2]
                            fns = []
                            for kc in range(8):
                                lhs = xnfb[:, kc * TT + tau * 128:
                                           kc * TT + (tau + 1) * 128]
                                rhs = wgp_sb[:, kc * 2 * E + gvt * 512:
                                             kc * 2 * E + (gvt + 1) * 512]
                                fns.append((lambda e, lhs=lhs, rhs=rhs, pbk=pbk,
                                            st=(kc == 0), sp=(kc == 7): e.matmul(
                                    pbk[:, :], lhs, rhs, start=st, stop=sp),
                                    ("xnfb", "wgp_sb")))
                            p.group("tensor", fns, writes=(pkey,))
                            dst = g_sb[:, gvt * 512:(gvt + 1) * 512]
                            p.emit("scalar", lambda e, dst=dst, pbk=pbk:
                                   e.copy(dst, pbk[:, :]),
                                   reads=(pkey,), writes=("g_sb",))
                        p.emit("vector", lambda e, onb=onb: e.tensor_tensor(
                            u_sb[:, :], onb[:, :], g_sb[:, :], ALU.mult),
                            reads=(onk, "g_sb"), writes=("u_sb",))
                        for a in range(4):
                            ubk = 2 if a % 2 == 0 else 5
                            fns = []
                            for j in range(4):
                                fcq = a * 4 + j
                                srcu = u_sb[:, fcq * 128:(fcq + 1) * 128]
                                dstp = pb[ubk][:, j * 128:(j + 1) * 128]
                                fns.append((lambda e, srcu=srcu, dstp=dstp:
                                            e.transpose(dstp, srcu, ident[:, :]),
                                            ("u_sb", "ident")))
                            p.group("tensor", fns, writes=(PK[ubk],))
                            dst = ufm_sb[:, a * 512:(a + 1) * 512]
                            p.emit("scalar", lambda e, dst=dst, ubk=ubk: e.copy(
                                dst, pb[ubk][:, :]), reads=(PK[ubk],), writes=("ufm_sb",))
                        bpair = (3, 4) if tau % 2 == 0 else (6, 2)
                        for et in range(2):
                            pbk = pb[bpair[et]]
                            pkey = PK[bpair[et]]
                            fns = []
                            for fc in range(16):
                                lhs = ufm_sb[:, fc * 128:(fc + 1) * 128]
                                rhs = wo_sb[:, fc * E + et * 512:
                                            fc * E + (et + 1) * 512]
                                fns.append((lambda e, lhs=lhs, rhs=rhs, pbk=pbk,
                                            st=(fc == 0), sp=(fc == 15): e.matmul(
                                    pbk[:, :], lhs, rhs, start=st, stop=sp),
                                    ("ufm_sb", "wo_sb")))
                            p.group("tensor", fns, writes=(pkey,))
                            dst = xpb[:, et * 512:(et + 1) * 512]
                            p.emit("vector", lambda e, dst=dst, pbk=pbk, xrb=xrb,
                                   et=et: e.tensor_add(
                                dst, pbk[:, :], xrb[:, et * 512:(et + 1) * 512]),
                                reads=(pkey, xrk), writes=(xpk,))
                        p.dma("gpsimd", lambda e, s=s, tr=tr, xpb=xpb: e.dma_start(
                            out=xp_d[s, tr:tr + 128, :], in_=xpb[:, :]),
                            reads=(xpk,))
            p.barrier()

    if "C1" in phases or "C2" in phases:
        ones1 = stack.enter_context(nc.sbuf_tensor("ones1", [128, 1], F32))
        p.emit("vector", lambda e: e.tensor_scalar_add(ones1[:, :], zeros1[:, :], 1.0),
               reads=("zeros1",), writes=("ones1",))
        smem_d = nc.dram_tensor("smem", [8], F32, kind="Internal")

    def quant_weight(wsrc_d, nkc, wid, wtern_sb, wtkey, mcb, mckey, qs):
        wfpA = qs["wfp"]
        wfpB = qs["wfpB"]
        rt1 = qs["rt1"]
        rt2 = qs["rt2"]
        acc = qs["acc"]
        pk6 = PK[6]
        p.emit("vector", lambda e: e.memset(acc[:, :1], 0.0), writes=("qacc",))
        for kc in range(nkc):
            wfp = wfpA if kc % 2 == 0 else wfpB
            wkq = "qwfp" if kc % 2 == 0 else "qwfpB"
            a_i = 1 + (kc % 2)
            p.dma("sync", lambda e, kc=kc, wfp=wfp: e.dma_start(
                out=wfp[:, :wid], in_=wsrc_d[kc * 128:(kc + 1) * 128, :]),
                writes=(wkq,))
            p.emit("vector", lambda e, wfp=wfp, a_i=a_i: e.reduce_sum(
                acc[:, a_i:a_i + 1], wfp[:, :wid], axis=AX.X,
                apply_absolute_value=True),
                reads=(wkq,), writes=("qacc",))
            p.emit("vector", lambda e, a_i=a_i: e.tensor_add(
                acc[:, :1], acc[:, :1], acc[:, a_i:a_i + 1]),
                reads=("qacc",), writes=("qacc",))
        p.group("tensor", [(lambda e: e.matmul(pb[6][:1, :1], acc[:, :1],
                                               ones1[:, :1], start=True, stop=True),
                            ("qacc", "ones1"))], writes=(pk6,))
        p.emit("vector", lambda e: e.tensor_scalar(
            acc[:1, 3:4], pb[6][:1, :1], 1.0 / (nkc * 128 * wid), 1e-5,
            ALU.mult, ALU.max), reads=(pk6,), writes=("qacc",))
        p.dma("gpsimd", lambda e: e.dma_start(out=smem_d[0:1], in_=acc[:1, 3:4]),
              reads=("qacc",), writes=("smem",), buf="smem")
        p.dma("sync", lambda e, mcb=mcb: e.dma_start(
            out=mcb[:, :], in_=bass.AP(smem_d, 0, [[0, 128], [1, 1]])),
            reads=("smem",), writes=(mckey,), buf=mckey)
        p.emit("vector", lambda e, mcb=mcb: e.reciprocal(acc[:, 0:1], mcb[:, :]),
               reads=(mckey,), writes=("qacc",))
        for kc in range(nkc):
            wfp = wfpA if kc % 2 == 0 else wfpB
            wkq = "qwfp" if kc % 2 == 0 else "qwfpB"
            p.dma("sync", lambda e, kc=kc, wfp=wfp: e.dma_start(
                out=wfp[:, :wid], in_=wsrc_d[kc * 128:(kc + 1) * 128, :]),
                writes=(wkq,))
            vap = wfp[:, :wid]
            p.emit("vector", lambda e, vap=vap: e.tensor_scalar_mul(
                vap, vap, acc[:, 0:1]), reads=(wkq, "qacc"), writes=(wkq,))
            p.emit("vector", lambda e, vap=vap: e.tensor_single_scalar(
                rt1[:, :wid], vap, 0.5, ALU.is_gt), reads=(wkq,), writes=("qrt1",))
            p.emit("vector", lambda e, vap=vap: e.tensor_single_scalar(
                rt2[:, :wid], vap, -0.5, ALU.is_lt), reads=(wkq,), writes=("qrt2",))
            dst = wtern_sb[:, kc * wid:(kc + 1) * wid]
            p.emit("vector", lambda e, dst=dst: e.tensor_sub(
                dst, rt1[:, :wid], rt2[:, :wid]),
                reads=("qrt1", "qrt2"), writes=(wtkey,))

    if "C1" in phases:
        c1s = contextlib.ExitStack()
        wgt_sb = c1s.enter_context(nc.sbuf_tensor("wgt_sb", [128, 8 * INNER], BF16, side="right"))
        wut_sb = c1s.enter_context(nc.sbuf_tensor("wut_sb", [128, 8 * INNER], BF16, side="right"))
        mcg = c1s.enter_context(nc.sbuf_tensor("mcg", [128, 1], F32, side="right"))
        mcu = c1s.enter_context(nc.sbuf_tensor("mcu", [128, 1], F32, side="right"))
        with nc.sbuf_tensor("qwfp", [128, INNER], F32) as qwfp, \
             nc.sbuf_tensor("qwfpB", [128, INNER], F32) as qwfpB, \
             nc.sbuf_tensor("qrt1", [128, INNER], F32) as qrt1, \
             nc.sbuf_tensor("qrt2", [128, INNER], F32) as qrt2, \
             nc.sbuf_tensor("qacc", [128, 4], F32) as qacc:
            qs = {"wfp": qwfp, "wfpB": qwfpB, "rt1": qrt1, "rt2": qrt2, "acc": qacc}
            quant_weight(wgate_d, 8, INNER, wgt_sb, "wgt_sb", mcg, "mcg", qs)
            quant_weight(wup_d, 8, INNER, wut_sb, "wut_sb", mcu, "mcu", qs)
            p.barrier()

        c1l = contextlib.ExitStack()
        xpc0 = c1l.enter_context(nc.sbuf_tensor("xpc0", [128, E], F32))
        xpc1 = c1l.enter_context(nc.sbuf_tensor("xpc1", [128, E], F32))
        xn2 = c1l.enter_context(nc.sbuf_tensor("xn2", [128, E], F32))
        cst = c1l.enter_context(nc.sbuf_tensor("cst", [128, 8], F32))
        crt1 = c1l.enter_context(nc.sbuf_tensor("crt1", [128, E], F32))
        xqf = c1l.enter_context(nc.sbuf_tensor("xqf", [128, E], BF16))
        qbf = c1l.enter_context(nc.sbuf_tensor("qbf", [128, E], BF16))
        gate_sb = c1l.enter_context(nc.sbuf_tensor("gate_sb", [128, 2 * 512], F32))
        h_sb = c1l.enter_context(nc.sbuf_tensor("h_sb", [128, INNER], F32))
        hr1c = c1l.enter_context(nc.sbuf_tensor("hr1c", [128, E], F32))
        hr2c = c1l.enter_context(nc.sbuf_tensor("hr2c", [128, E], F32))
        gr1c = c1l.enter_context(nc.sbuf_tensor("gr1c", [128, E], F32))
        gr2c = c1l.enter_context(nc.sbuf_tensor("gr2c", [128, E], F32))
        gbf = c1l.enter_context(nc.sbuf_tensor("gbf", [128, E], BF16))
        gsc1 = c1l.enter_context(nc.sbuf_tensor("gsc1", [128, E], F32))
        csc1 = c1l.enter_context(nc.sbuf_tensor("csc1", [128, E], F32))
        hqt0 = c1l.enter_context(nc.sbuf_tensor("hqt0", [128, 1024], BF16))
        hqt1 = c1l.enter_context(nc.sbuf_tensor("hqt1", [128, 1024], BF16))
        if True:
            for s in range(SPC):
                for it in range(NTT):
                    for tau in range(4):
                        tr = it * TT + tau * 128
                        tcol = it * 4 + tau
                        slot = tau % 2
                        xpc = xpc0 if slot == 0 else xpc1
                        xpk = "xpc%d" % slot
                        p.dma("sync", lambda e, s=s, tr=tr, xpc=xpc: e.dma_start(
                            out=xpc[:, :], in_=xp_d[s, tr:tr + 128, :]),
                            writes=(xpk,), sem=None, buf=xpk)
                        mcol = cst[:, 0:1]
                        qcol = cst[:, 1:2]
                        p.emit("vector", lambda e, xpc=xpc, mcol=mcol: e.reduce_sum(
                            mcol, xpc[:, :], axis=AX.X), reads=(xpk,), writes=("cst",))
                        p.emit("scalar", lambda e, mcol=mcol: e.mul(
                            mcol, mcol, 1.0 / E), reads=("cst",), writes=("cst",))
                        p.emit("vector", lambda e, xpc=xpc, mcol=mcol:
                               e.tensor_scalar_sub(xn2[:, :], xpc[:, :], mcol),
                               reads=(xpk, "cst"), writes=("xn2",))
                        p.emit("scalar", lambda e, qcol=qcol: e.activation(
                            csc1[:, :], xn2[:, :], AF.Square, accum_out=qcol),
                            reads=("xn2",), writes=("csc1", "cst"))
                        p.emit("scalar", lambda e, qcol=qcol: e.activation(
                            qcol, qcol, AF.Sqrt, scale=1.0 / E, bias=epst[:, :1]),
                            reads=("cst", "epst"), writes=("cst",))
                        p.emit("vector", lambda e, qcol=qcol: e.reciprocal(
                            qcol, qcol), reads=("cst",), writes=("cst",))
                        p.emit("vector", lambda e, qcol=qcol: e.tensor_scalar_mul(
                            xn2[:, :], xn2[:, :], qcol),
                            reads=("xn2", "cst"), writes=("xn2",))
                        am = cst[:, 2:3]
                        sx = cst[:, 3:4]
                        isx = cst[:, 4:5]
                        p.emit("vector", lambda e, am=am: e.reduce_max(
                            am, xn2[:, :], axis=AX.X, apply_absolute_value=True),
                            reads=("xn2",), writes=("cst",))
                        p.emit("vector", lambda e, am=am: e.tensor_scalar_max(
                            am, am, 1e-5), reads=("cst",), writes=("cst",))
                        p.emit("vector", lambda e, am=am, sx=sx: e.reciprocal(
                            sx, am), reads=("cst",), writes=("cst",))
                        p.emit("vector", lambda e, sx=sx: e.tensor_scalar_mul(
                            sx, sx, 127.0), reads=("cst",), writes=("cst",))
                        p.emit("vector", lambda e, am=am, isx=isx: e.tensor_scalar_mul(
                            isx, am, 1.0 / 127.0), reads=("cst",), writes=("cst",))
                        p.emit("scalar", lambda e, sx=sx: e.activation(
                            xn2[:, :], xn2[:, :], AF.Copy, scale=sx),
                            reads=("xn2", "cst"), writes=("xn2",))
                        emit_round(p, crt1[:, :], csc1[:, :], xn2[:, :],
                                   "crt1", "csc1", "xn2")
                        p.emit("vector", lambda e: e.tensor_scalar(
                            qbf[:, :], crt1[:, :], -128.0, 127.0, ALU.max, ALU.min),
                            reads=("crt1",), writes=("qbf",))
                        fns = []
                        for fc in range(8):
                            srcq = qbf[:, fc * 128:(fc + 1) * 128]
                            dstq = pbb[:, fc * 128:(fc + 1) * 128]
                            fns.append((lambda e, srcq=srcq, dstq=dstq: e.transpose(
                                dstq, srcq, identb[:, :]), ("qbf", "identb")))
                        p.group("tensor", fns, writes=("pbb",))
                        p.emit("scalar", lambda e: e.copy(
                            xqf[:, :], pbb[:, :E]), reads=("pbb",), writes=("xqf",))
                        scg = cst[:, 5:6]
                        scu = cst[:, 6:7]
                        p.emit("vector", lambda e, isx=isx, scg=scg: e.tensor_tensor(
                            scg, isx, mcg[:, :], ALU.mult),
                            reads=("cst", "mcg"), writes=("cst",))
                        p.emit("vector", lambda e, isx=isx, scu=scu: e.tensor_tensor(
                            scu, isx, mcu[:, :], ALU.mult),
                            reads=("cst", "mcu"), writes=("cst",))
                        for ivt in range(8):
                            gb, ub = (0, 1) if ivt % 2 == 0 else (3, 4)
                            for (wsb2, wk2, pbk, pkey) in (
                                    (wgt_sb, "wgt_sb", pb[gb], PK[gb]),
                                    (wut_sb, "wut_sb", pb[ub], PK[ub])):
                                fns = []
                                for kc in range(8):
                                    lhs = xqf[:, kc * 128:(kc + 1) * 128]
                                    rhs = wsb2[:, kc * INNER + ivt * 512:
                                               kc * INNER + (ivt + 1) * 512]
                                    fns.append((lambda e, lhs=lhs, rhs=rhs, pbk=pbk,
                                                st=(kc == 0), sp=(kc == 7): e.matmul(
                                        pbk[:, :], lhs, rhs, start=st, stop=sp),
                                        ("xqf", wk2)))
                                p.group("tensor", fns, writes=(pkey,))
                            dstg = gate_sb[:, (ivt % 2) * 512:(ivt % 2) * 512 + 512]
                            p.emit("scalar", lambda e, dstg=dstg, gb=gb, scg=scg:
                                   e.activation(
                                dstg, pb[gb][:, :], AF.Silu, scale=scg),
                                reads=(PK[gb], "cst"),
                                writes=("gate_sb%d" % (ivt % 2),))
                            dsth = h_sb[:, ivt * 512:(ivt + 1) * 512]
                            p.emit("scalar", lambda e, dsth=dsth, ub=ub, scu=scu:
                                   e.activation(
                                dsth, pb[ub][:, :], AF.Copy, scale=scu),
                                reads=(PK[ub], "cst"), writes=("h_sb",))
                            if ivt % 2 == 1:
                                dhp = h_sb[:, (ivt - 1) * 512:(ivt + 1) * 512]
                                p.emit("vector", lambda e, dhp=dhp:
                                       e.tensor_mul(dhp, gate_sb[:, :], dhp),
                                       reads=("gate_sb0", "gate_sb1", "h_sb"),
                                       writes=("h_sb",))
                        am2 = cst[:, 2:3]
                        sx2 = cst[:, 3:4]
                        isx2 = cst[:, 7:8]
                        p.emit("vector", lambda e, am2=am2: e.reduce_max(
                            am2, h_sb[:, :], axis=AX.X, apply_absolute_value=True),
                            reads=("h_sb",), writes=("cst",))
                        p.emit("vector", lambda e, am2=am2: e.tensor_scalar_max(
                            am2, am2, 1e-5), reads=("cst",), writes=("cst",))
                        p.emit("vector", lambda e, am2=am2, sx2=sx2: e.reciprocal(
                            sx2, am2), reads=("cst",), writes=("cst",))
                        p.emit("vector", lambda e, sx2=sx2: e.tensor_scalar_mul(
                            sx2, sx2, 127.0), reads=("cst",), writes=("cst",))
                        p.emit("vector", lambda e, am2=am2, isx2=isx2:
                               e.tensor_scalar_mul(isx2, am2, 1.0 / 127.0),
                               reads=("cst",), writes=("cst",))
                        p.dma("gpsimd", lambda e, s=s, tcol=tcol, isx2=isx2:
                              e.dma_start(
                                  out=dram_ap(hsc_d, [SPC, 128, T // 128],
                                              [s, 0, tcol], [(1, 128), (2, 1)]),
                                  in_=isx2, allow_slow_non_contiguous=True),
                              reads=("cst",), buf="hsc_st")
                        for a in range(4):
                            hsl = h_sb[:, a * E:(a + 1) * E]
                            lane = "vector"
                            r1, r2, rsc = ((hr1c, hr2c, csc1) if a % 2 == 0
                                           else (gr1c, gr2c, gsc1))
                            k1, k2, ks = (("hr1c", "hr2c", "csc1") if a % 2 == 0
                                          else ("gr1c", "gr2c", "gsc1"))
                            qb, qk2 = (qbf, "qbf") if a % 2 == 0 else (gbf, "gbf")
                            p.emit(lane, lambda e, sx2=sx2, hsl=hsl, r2=r2:
                                   e.tensor_scalar_mul(r2[:, :], hsl, sx2),
                                   reads=("h_sb", "cst"), writes=(k2,))
                            emit_round(p, r1[:, :], rsc[:, :], r2[:, :],
                                       k1, ks, k2, eng=lane)
                            p.emit(lane, lambda e, r1=r1, qb=qb: e.tensor_scalar(
                                qb[:, :], r1[:, :], -128.0, 127.0,
                                ALU.max, ALU.min), reads=(k1,), writes=(qk2,))
                            fns = []
                            for j in range(8):
                                srch = qb[:, j * 128:(j + 1) * 128]
                                dsth2 = pbb[:, j * 128:(j + 1) * 128]
                                fns.append((lambda e, srch=srch, dsth2=dsth2:
                                            e.transpose(dsth2, srch, identb[:, :]),
                                            (qk2, "identb")))
                            p.group("tensor", fns, writes=("pbb",))
                            hqt = hqt0 if a % 2 == 0 else hqt1
                            hqk = "hqt%d" % (a % 2)
                            p.emit("scalar", lambda e, hqt=hqt: e.copy(
                                hqt[:, :], pbb[:, :]), reads=("pbb",), writes=(hqk,))
                            p.dma("gpsimd", lambda e, s=s, tr=tr, a=a, hqt=hqt:
                                  e.dma_start(
                                      out=dram_ap(hq_d, SH_HQ, [s, 8 * a, 0, tr],
                                                  [(2, 128), (1, 8), (3, 128)]),
                                      in_=bass.AP(hqt, 0, [[1024, 128], [128, 8],
                                                           [1, 128]])),
                                  reads=(hqk,), buf=hqk)
                        p.barrier()
        c1l.close()
        c1s.close()

    if "C2" in phases:
        c2s = contextlib.ExitStack()
        wdt_sb = c2s.enter_context(nc.sbuf_tensor("wdt_sb", [128, 32 * E], BF16, side="right"))
        mcd = c2s.enter_context(nc.sbuf_tensor("mcd", [128, 1], F32, side="right"))
        with nc.sbuf_tensor("qwfp2", [128, E], F32) as qwfp2, \
             nc.sbuf_tensor("qwfp2B", [128, E], F32) as qwfp2B, \
             nc.sbuf_tensor("qrt1b", [128, E], F32) as qrt1b, \
             nc.sbuf_tensor("qrt2b", [128, E], F32) as qrt2b, \
             nc.sbuf_tensor("qaccb", [128, 4], F32) as qaccb:
            qs2 = {"wfp": qwfp2, "wfpB": qwfp2B, "rt1": qrt1b, "rt2": qrt2b,
                   "acc": qaccb}
            quant_weight(wdown_d, 32, E, wdt_sb, "wdt_sb", mcd, "mcd", qs2)
            p.barrier()

        with nc.sbuf_tensor("hqc0", [128, INNER], BF16) as hqc0, \
             nc.sbuf_tensor("hqc1", [128, INNER], BF16) as hqc1, \
             nc.sbuf_tensor("xpd0", [128, E], F32) as xpd0, \
             nc.sbuf_tensor("xpd1", [128, E], F32) as xpd1, \
             nc.sbuf_tensor("oub0", [128, E], F32) as oub0, \
             nc.sbuf_tensor("oub1", [128, E], F32) as oub1, \
             nc.sbuf_tensor("dst2", [128, 2], F32) as dst2:

            for s in range(SPC):
                for it in range(NTT):
                    for tau in range(4):
                        tr = it * TT + tau * 128
                        tcol = it * 4 + tau
                        slot = tau % 2
                        hqc = hqc0 if slot == 0 else hqc1
                        xpd = xpd0 if slot == 0 else xpd1
                        oub = oub0 if slot == 0 else oub1
                        hk, xk2, ok2 = ("hqc%d" % slot, "xpd%d" % slot,
                                        "oub%d" % slot)
                        p.dma("sync", lambda e, s=s, tr=tr, hqc=hqc: e.dma_start(
                            out=bass.AP(hqc, 0, [[INNER, 128], [128, 32], [1, 128]]),
                            in_=dram_ap(hq_d, SH_HQ, [s, 0, 0, tr],
                                        [(2, 128), (1, 32), (3, 128)])),
                            writes=(hk,))
                        p.dma("sync", lambda e, s=s, tr=tr, xpd=xpd: e.dma_start(
                            out=xpd[:, :], in_=xp_d[s, tr:tr + 128, :]),
                            writes=(xk2,))
                        p.dma("sync", lambda e, s=s, tcol=tcol: e.dma_start(
                            out=dst2[:, 0:1],
                            in_=dram_ap(hsc_d, [SPC, 128, T // 128],
                                        [s, 0, tcol], [(1, 128), (2, 1)]),
                            allow_slow_non_contiguous=True),
                            writes=("dst2",))
                        p.emit("vector", lambda e: e.tensor_tensor(
                            dst2[:, 1:2], dst2[:, 0:1], mcd[:, :], ALU.mult),
                            reads=("dst2", "mcd"), writes=("dst2",))
                        cpair = (0, 1) if tau % 2 == 0 else (3, 4)
                        for et in range(2):
                            pbk = pb[cpair[et]]
                            pkey = PK[cpair[et]]
                            fns = []
                            for ic in range(32):
                                lhs = hqc[:, ic * 128:(ic + 1) * 128]
                                rhs = wdt_sb[:, ic * E + et * 512:
                                             ic * E + (et + 1) * 512]
                                fns.append((lambda e, lhs=lhs, rhs=rhs, pbk=pbk,
                                            st=(ic == 0), sp=(ic == 31): e.matmul(
                                    pbk[:, :], lhs, rhs, start=st, stop=sp),
                                    (hk, "wdt_sb")))
                            p.group("tensor", fns, writes=(pkey,))
                            dsto = oub[:, et * 512:(et + 1) * 512]
                            xsl = xpd[:, et * 512:(et + 1) * 512]
                            p.emit("vector", lambda e, dsto=dsto, pbk=pbk, xsl=xsl:
                                   e.scalar_tensor_tensor(
                                       dsto, pbk[:, :], dst2[:, 1:2], xsl,
                                       ALU.mult, ALU.add),
                                   reads=(pkey, "dst2", xk2), writes=(ok2,))
                        p.dma("gpsimd", lambda e, s=s, tr=tr, oub=oub: e.dma_start(
                            out=out_d[s, tr:tr + 128, :], in_=oub[:, :]),
                            reads=(ok2,))
            p.barrier()
        c2s.close()

    for sname, v in list(p.cnt.items()):
        if sname.startswith("D_") and v > 0:
            p._wait("sync", sname, v)
            p._wait("gpsimd", sname, v)

    with nc.Block() as block:
        p.finalize(block)
    return nc, stack


def kernel(L0_attn_x, L0_attn_Wq, L0_attn_Wk, L0_attn_Wv, L0_attn_Wg, L0_attn_Wo,
           L0_attn_gn_weight, L0_ffn_w_gate, L0_ffn_w_up, L0_ffn_w_down):
    x = np.ascontiguousarray(np.asarray(L0_attn_x, dtype=np.float32))
    base = {
        "wq": np.ascontiguousarray(np.asarray(L0_attn_Wq, np.float32)),
        "wk": np.ascontiguousarray(np.asarray(L0_attn_Wk, np.float32)),
        "wv": np.ascontiguousarray(np.asarray(L0_attn_Wv, np.float32)),
        "wg": np.ascontiguousarray(np.asarray(L0_attn_Wg, np.float32)),
        "wo": np.ascontiguousarray(np.asarray(L0_attn_Wo, np.float32)),
        "gn": np.ascontiguousarray(np.asarray(L0_attn_gn_weight, np.float32)),
        "wgate": np.ascontiguousarray(np.asarray(L0_ffn_w_gate, np.float32).T),
        "wup": np.ascontiguousarray(np.asarray(L0_ffn_w_up, np.float32).T),
        "wdown": np.ascontiguousarray(np.asarray(L0_ffn_w_down, np.float32).T),
    }
    nc, stack = build_program()
    in_maps = [dict(base, x=np.ascontiguousarray(x[i * SPC:(i + 1) * SPC]))
               for i in range(NCORES)]
    res = run_bass_kernel_spmd(nc, in_maps, core_ids=list(range(NCORES)))
    stack.close()
    out = np.concatenate([r["out"] for r in res.results], axis=0)
    return out.reshape(B, T, E)
